# version 12
# baseline (speedup 1.0000x reference)
import numpy as np
import ml_dtypes
import concourse.bass as bass
import concourse.mybir as mybir
from concourse import bass_utils

F32 = mybir.dt.float32
BF16 = mybir.dt.bfloat16
AF = mybir.ActivationFunctionType
ALU = mybir.AluOpType
AX = mybir.AxisListType

BF = ml_dtypes.bfloat16
T = 2048
D = 1024
NCORE = 8

SINGLE_LAUNCH = True


class Buf:
    __slots__ = ("name", "writers", "readers")
    def __init__(self, name):
        self.name = name
        self.writers = {}
        self.readers = {}

ENGINES = ("sync", "scalar", "vector", "gpsimd", "tensor")
CSEM = {"tensor": "semt", "vector": "semv", "scalar": "sems", "gpsimd": "semg"}
DSEM = {"sync": "qs", "scalar": "qa", "gpsimd": "qg"}
NQSEM = 12

class Prog:
    def __init__(self, nc):
        self.nc = nc
        self.streams = {e: [] for e in ENGINES}
        self.cnt = {e: 0 for e in CSEM}
        self.dcnt = {q: 0 for q in DSEM}
        self.qcnt = {f"{DSEM[q]}{i}": 0 for q in DSEM for i in range(NQSEM)}
        self.waited = {e: {} for e in ENGINES}
        self.pending = {e: [] for e in CSEM}
        self.names = {}

    def uname(self, base):
        i = self.names.get(base, 0)
        self.names[base] = i + 1
        return f"{base}_{i}"

    def _wait(self, stream, sem, val):
        if val <= 0:
            return
        if stream == "tensor" and sem == CSEM["tensor"]:
            return
        if self.waited[stream].get(sem, 0) >= val:
            return
        self.waited[stream][sem] = val
        self.streams[stream].append(("w", sem, val))

    def _dep(self, stream, writers):
        for sem, val in writers.items():
            self._wait(stream, sem, val)

    def op(self, stream, emit, reads=(), writes=(), inc=True):
        sem = CSEM[stream]
        for b in reads:
            self._dep(stream, b.writers)
        for b in writes:
            self._dep(stream, b.writers)
            for rs, rv in list(b.readers.items()):
                self._wait(stream, rs, rv)
        nv = self.cnt[stream] + 1
        for b in reads:
            b.readers[sem] = max(b.readers.get(sem, 0), nv)
        self.pending[stream].extend(writes)
        if inc == "mark":
            self.cnt[stream] = nv
            self.streams[stream].append(("i", emit, sem))
        elif inc:
            self.cnt[stream] = nv
            self.streams[stream].append(("i", emit, sem))
            for b in self.pending[stream]:
                b.writers = {sem: nv}
                b.readers = {}
            self.pending[stream] = []
        else:
            self.streams[stream].append(("e", emit))

    def dma(self, queue, out_ap, in_ap, reads=(), writes=(), **kw):
        qbase = DSEM[queue]
        k = self.dcnt[queue] % NQSEM
        qsem = f"{qbase}{k}"
        qprefix = qbase
        for b in reads:
            self._dep(queue, b.writers)
        for b in writes:
            self._dep(queue, {s: v for s, v in b.writers.items() if not s.startswith(qprefix)})
            for rs, rv in list(b.readers.items()):
                self._wait(queue, rs, rv)
        self._wait(queue, qsem, self.qcnt[qsem] * 16)
        nv = (self.qcnt[qsem] + 1) * 16
        for b in reads:
            b.readers[qsem] = max(b.readers.get(qsem, 0), nv)
        for b in writes:
            b.writers[qsem] = nv
            b.readers = {}
        self.qcnt[qsem] += 1
        self.dcnt[queue] += 1
        self.streams[queue].append(("d", out_ap, in_ap, qsem, kw))

    def barrier(self):
        tgt = {CSEM[e]: self.cnt[e] for e in self.cnt}
        tgt.update({s: c * 16 for s, c in self.qcnt.items()})
        for e in ENGINES:
            for sem, val in tgt.items():
                if e == "tensor" and sem == CSEM["tensor"]:
                    continue
                self._wait(e, sem, val)

    def build_block(self, sems):
        nc = self.nc
        blk_cm = nc.Block()
        block = blk_cm.__enter__()
        def make_runner(items):
            def run(eng):
                for it in items:
                    tag = it[0]
                    if tag == "w":
                        eng.wait_ge(sems[it[1]], it[2])
                    elif tag == "i":
                        ins = it[1](eng)
                        ins.then_inc(sems[it[2]], 1)
                    elif tag == "e":
                        it[1](eng)
                    elif tag == "d":
                        _, out_ap, in_ap, qsem, kw = it
                        eng.dma_start(out=out_ap, in_=in_ap, **kw).then_inc(sems[qsem], 16)
            return run
        for name in ENGINES:
            getattr(block, name)(make_runner(self.streams[name]))
        blk_cm.__exit__(None, None, None)


class SB:
    __slots__ = ("t", "buf")
    def __init__(self, t, name):
        self.t = t
        self.buf = Buf(name)
    def __getitem__(self, idx):
        return self.t[idx]
    @property
    def ap(self):
        return self.t[:]


class Arena:
    def __init__(self, nc, P):
        self.nc = nc
        self.P = P
        self.frames = []
        self.caches = []

    def push(self):
        self.frames.append([])
        self.caches.append({})

    def pop(self):
        self.P.barrier()
        self.caches.pop()
        for cm in reversed(self.frames.pop()):
            cm.__exit__(None, None, None)

    def sbuf(self, name, shape, dtype):
        key = ("s", name, tuple(shape), dtype)
        if key in self.caches[-1]:
            return self.caches[-1][key]
        uname = self.P.uname(name)
        cm = self.nc.sbuf_tensor(uname, shape, dtype)
        t = cm.__enter__()
        self.frames[-1].append(cm)
        sb = SB(t, uname)
        self.caches[-1][key] = sb
        return sb

    def psum(self, name, dtype=F32):
        shape = [128, 512] if dtype == F32 else [128, 1024]
        key = ("p", name, dtype)
        if key in self.caches[-1]:
            return self.caches[-1][key]
        uname = self.P.uname(name)
        cm = self.nc.psum_tensor(uname, shape, dtype)
        t = cm.__enter__()
        self.frames[-1].append(cm)
        sb = SB(t, uname)
        self.caches[-1][key] = sb
        return sb


class HT:
    __slots__ = ("ap", "buf", "name")
    def __init__(self, ap, name, written=False):
        self.ap = ap
        self.buf = Buf(name)
        self.name = name


FCONST_VALS = {}

def fc_idx(val):
    val = float(val)
    if val not in FCONST_VALS:
        FCONST_VALS[val] = len(FCONST_VALS)
        assert len(FCONST_VALS) <= 128
    return FCONST_VALS[val]

def make_fconst():
    arr = np.zeros((128, 128), dtype=np.float32)
    for v, i in FCONST_VALS.items():
        arr[:, i] = v
    return arr

def cfc(C, val):
    return C.fc_sb.t[:, fc_idx(val):fc_idx(val) + 1]


class Ctx:
    def __init__(self, nc):
        self.nc = nc
        self.P = Prog(nc)
        self.A = Arena(nc, self.P)
        self.inputs = {}
        self.outputs = {}
        self.internal = {}
        self.fc_sb = None

    def hin(self, name, shape, dtype=F32):
        if name not in self.inputs:
            ap = self.nc.dram_tensor(name, list(shape), dtype, kind="ExternalInput").ap()
            self.inputs[name] = HT(ap, name)
        return self.inputs[name]

    def hout(self, name, shape, dtype=F32):
        if name not in self.outputs:
            ap = self.nc.dram_tensor(name, list(shape), dtype, kind="ExternalOutput").ap()
            self.outputs[name] = HT(ap, name)
        return self.outputs[name]

    def hint(self, name, shape, dtype):
        if name not in self.internal:
            ap = self.nc.dram_tensor(name, list(shape), dtype, kind="Internal").ap()
            self.internal[name] = HT(ap, name)
        return self.internal[name]



def v_tt(P, out, a, b, op, outsl=None, asl=None, bsl=None, eng="vector", inc=True, extra_reads=()):
    oap = outsl if outsl is not None else out.ap
    aap = asl if asl is not None else a.ap
    bap = bsl if bsl is not None else b.ap
    P.op(eng, lambda e, oap=oap, aap=aap, bap=bap, op=op: e.tensor_tensor(oap, aap, bap, op),
         reads=[a.buf, b.buf, *extra_reads], writes=[out.buf], inc=inc)

def v_ts(P, out, a, s1, op0, s2=None, op1=ALU.bypass, outsl=None, asl=None, sbufs=(), eng="vector", accum=None, accum_buf=None):
    oap = outsl if outsl is not None else out.ap
    aap = asl if asl is not None else a.ap
    writes = [out.buf] + ([accum_buf] if accum_buf is not None else [])
    P.op(eng, lambda e, oap=oap, aap=aap: e.tensor_scalar(oap, aap, s1, s2, op0, op1, accum_out=accum),
         reads=[a.buf, *sbufs], writes=writes)

def v_stt(P, out, in0, scalar, in1, op0, op1, outsl=None, i0sl=None, i1sl=None, sbufs=(), eng="vector"):
    oap = outsl if outsl is not None else out.ap
    a0 = i0sl if i0sl is not None else in0.ap
    a1 = i1sl if i1sl is not None else in1.ap
    P.op(eng, lambda e, oap=oap, a0=a0, a1=a1: e.scalar_tensor_tensor(oap, a0, scalar, a1, op0, op1),
         reads=[in0.buf, in1.buf, *sbufs], writes=[out.buf])

def v_copy(P, out, a, outsl=None, asl=None, eng="vector"):
    oap = outsl if outsl is not None else out.ap
    aap = asl if asl is not None else a.ap
    P.op(eng, lambda e, oap=oap, aap=aap: e.tensor_copy(oap, aap),
         reads=[a.buf], writes=[out.buf])

def s_act(P, out, a, func, bias=0.0, scale=1.0, outsl=None, asl=None, sbufs=(), accum=None, accum_buf=None):
    oap = outsl if outsl is not None else out.ap
    aap = asl if asl is not None else a.ap
    writes = [out.buf] + ([accum_buf] if accum_buf is not None else [])
    P.op("scalar", lambda e, oap=oap, aap=aap: e.activation(oap, aap, func, bias=bias, scale=scale, accum_out=accum),
         reads=[a.buf, *sbufs], writes=writes)

def v_memset(P, out, val, outsl=None, eng="vector"):
    oap = outsl if outsl is not None else out.ap
    P.op(eng, lambda e, oap=oap, val=val: e.memset(oap, val), reads=[], writes=[out.buf])

def t_mm(P, out_ap, lhsT_ap, rhs_ap, start, stop, reads, writes, inc, tile_position=None):
    P.op("tensor",
         lambda e, o=out_ap, l=lhsT_ap, r=rhs_ap, st=start, sp=stop, tp=tile_position:
             e.matmul(o, l, r, start=st, stop=sp, tile_position=tp),
         reads=list(reads), writes=list(writes), inc=inc)

def t_tr(P, out_ap, in_ap, ident, reads, writes, inc=True, pn=128):
    idap = ident.t[0:pn, 0:pn]
    P.op("tensor",
         lambda e, o=out_ap, i=in_ap, idn=idap: e.transpose(o, i, idn),
         reads=list(reads) + [ident.buf], writes=list(writes), inc=inc)

def v_red(P, out, a, op, outsl=None, asl=None, negate=None):
    oap = outsl if outsl is not None else out.ap
    aap = asl if asl is not None else a.ap
    P.op("vector", lambda e, oap=oap, aap=aap: e.tensor_reduce(oap, aap, AX.X, op, negate=negate),
         reads=[a.buf], writes=[out.buf])

def v_recip(P, out, a, outsl=None, asl=None):
    oap = outsl if outsl is not None else out.ap
    aap = asl if asl is not None else a.ap
    P.op("vector", lambda e, oap=oap, aap=aap: e.reciprocal(oap, aap),
         reads=[a.buf], writes=[out.buf])

def v_scan(P, out, d0, d1, initial, op0, op1, outsl=None, d0sl=None, d1sl=None):
    oap = outsl if outsl is not None else out.ap
    a0 = d0sl if d0sl is not None else d0.ap
    a1 = d1sl if d1sl is not None else d1.ap
    P.op("vector", lambda e, oap=oap, a0=a0, a1=a1: e.tensor_tensor_scan(oap, a0, a1, initial, op0, op1),
         reads=[d0.buf, d1.buf], writes=[out.buf])


def bcast_row(P, C, ps, row_sb, row_sl, width, ones1x128_f32, ps_cols=None):
    out_ap = ps[0:128, 0:width] if ps_cols is None else ps[0:128, ps_cols[0]:ps_cols[1]]
    t_mm(P, out_ap, ones1x128_f32.t[0:1, 0:128], row_sl,
         start=True, stop=True, reads=[row_sb.buf, ones1x128_f32.buf], writes=[ps.buf], inc=True)



def mm_big(C, name, Ws, K, X_ht, TT_, evac, x_resident=None, gt=2, gn=None, extra_w_reads=(), xdouble=True):
    P, A, nc = C.P, C.A, C.nc
    NK = K // 128
    nW = len(Ws)
    if gn is None:
        gn = max(1, 4 // (gt * nW))
    wbs = []
    for wi_, (w_ht, N, _) in enumerate(Ws):
        wb = A.sbuf(f"wb_{name}_{wi_}", [128, NK * N], BF16)
        for kt in range(NK):
            C.P.dma("gpsimd", wb.t[:, kt * N:(kt + 1) * N], w_ht.ap[kt * 128:(kt + 1) * 128, 0:N],
                    reads=[w_ht.buf, *extra_w_reads], writes=[wb.buf])
        wbs.append(wb)
    sweep_w = gt * 512
    nsweep = TT_ // sweep_w
    xbs = None
    if x_resident is None:
        nxb = 2 if xdouble else 1
        xbs = [A.sbuf(f"mmxb{i}", [128, NK * sweep_w], BF16) for i in range(nxb)]
    pss = [A.psum(f"mmps{i}") for i in range(min(7, 2 * gn * gt * nW))]
    psi = 0
    for sw in range(nsweep):
        t0s = sw * sweep_w
        if x_resident is not None:
            xb = x_resident
        else:
            xb = xbs[sw % len(xbs)]
            for kt in range(NK):
                C.P.dma("sync", xb.t[:, kt * sweep_w:(kt + 1) * sweep_w],
                        X_ht.ap[kt * 128:(kt + 1) * 128, t0s:t0s + sweep_w],
                        reads=[X_ht.buf], writes=[xb.buf])
        NN = Ws[0][1] // 128
        for ng0 in range(0, NN, gn):
            nts = list(range(ng0, min(ng0 + gn, NN)))
            grp = []
            for wi in range(nW):
                for nt in nts:
                    for tci in range(gt):
                        grp.append((wi, nt, tci, pss[psi % len(pss)]))
                        psi += 1
            for kt in range(NK):
                for (wi, nt, tci, ps) in grp:
                    N = Ws[wi][1]
                    if x_resident is not None:
                        rhs = xb.t[:, kt * T + t0s + tci * 512: kt * T + t0s + tci * 512 + 512]
                    else:
                        rhs = xb.t[:, kt * sweep_w + tci * 512: kt * sweep_w + tci * 512 + 512]
                    t_mm(P, ps[0:128, 0:512],
                         wbs[wi].t[:, kt * N + nt * 128: kt * N + nt * 128 + 128],
                         rhs,
                         start=(kt == 0), stop=(kt == NK - 1),
                         reads=[wbs[wi].buf, xb.buf], writes=[ps.buf],
                         inc=(kt == NK - 1))
            for (wi, nt, tci, ps) in grp:
                evac(wi, nt, t0s + tci * 512, 512, ps)

def _nn_total(Ws):
    return sum(w[1] // 128 for w in Ws)
def _nn_w(Ws):
    return Ws[0][1] // 128


def mm_xres(C, name, W_ht, K, N, X_sb, evac, thalf=1024, kg=4):
    P, A = C.P, C.A
    NK, NN = K // 128, N // 128
    ntc = thalf // 512
    gn = max(1, 4 // ntc)
    slabs = [A.sbuf(f"mmslab{i}", [128, kg * gn * 128], BF16) for i in range(3)]
    pss = [A.psum(f"mmps{i}") for i in range(min(7, 2 * gn * ntc))]
    psi = 0
    sli = 0
    for tswp in range(T // thalf):
        t0s = tswp * thalf
        for ng0 in range(0, NN, gn):
            gnn = min(gn, NN - ng0)
            grp = []
            for nt in range(gnn):
                for tci in range(ntc):
                    grp.append((nt, tci, pss[psi % len(pss)]))
                    psi += 1
            for ktg in range(NK // kg):
                sl = slabs[sli % 3]
                sli += 1
                src = W_ht.ap[ktg * kg * 128:(ktg + 1) * kg * 128, ng0 * 128:(ng0 + gnn) * 128]
                src = src.rearrange("(a p) n -> p a n", p=128)
                dst = sl.t[:, 0:kg * gnn * 128].rearrange("p (a n) -> p a n", a=kg)
                P.dma("gpsimd", dst, src, reads=[W_ht.buf], writes=[sl.buf])
                for ki in range(kg):
                    kt = ktg * kg + ki
                    for gi, (nt, tci, ps) in enumerate(grp):
                        last_of_slab = (ki == kg - 1) and (gi == len(grp) - 1)
                        inc = True if kt == NK - 1 and gi == len(grp) - 1 else ("mark" if last_of_slab else False)
                        t_mm(P, ps[0:128, 0:512],
                             sl.t[:, ki * gnn * 128 + nt * 128: ki * gnn * 128 + nt * 128 + 128],
                             X_sb.t[:, kt * T + t0s + tci * 512: kt * T + t0s + tci * 512 + 512],
                             start=(kt == 0), stop=(kt == NK - 1),
                             reads=[sl.buf, X_sb.buf], writes=[ps.buf],
                             inc=inc)
            for (nt, tci, ps) in grp:
                evac(0, ng0 + nt, t0s + tci * 512, 512, ps)


def load_fm(C, name, ht, R, TT_, dtype, queue="sync"):
    A, P = C.A, C.P
    sb = A.sbuf(name, [128, (R // 128) * TT_], dtype)
    for rt in range(R // 128):
        P.dma(queue, sb.t[:, rt * TT_:(rt + 1) * TT_], ht.ap[rt * 128:(rt + 1) * 128, 0:TT_],
              reads=[ht.buf], writes=[sb.buf])
    return sb

def store_fm(C, ht, sb, R, TT_, queue="scalar"):
    P = C.P
    for rt in range(R // 128):
        P.dma(queue, ht.ap[rt * 128:(rt + 1) * 128, 0:TT_], sb.t[:, rt * TT_:(rt + 1) * TT_],
              reads=[sb.buf], writes=[ht.buf])



class Consts:
    pass

def load_consts(C):
    A, P = C.A, C.P
    cs = Consts()
    ident_bf = C.hin("c_ident_bf", (128, 128), BF16)
    ident_f = C.hin("c_ident_f", (128, 128), F32)
    ones_col = C.hin("c_ones_col", (128, 1), BF16)
    ones_row_f = C.hin("c_ones_row_f", (1, 128), F32)
    ones_row_bf = C.hin("c_ones_row_bf", (1, 128), BF16)
    stair = C.hin("c_stair", (128, 1536), BF16)
    fcin = C.hin("c_fconst", (128, 128), F32)
    cs.ident_bf = A.sbuf("ident_bf", [128, 128], BF16)
    cs.ident_f = A.sbuf("ident_f", [128, 128], F32)
    cs.ones_col = A.sbuf("ones_col", [128, 1], BF16)
    cs.ones_row_f = A.sbuf("ones_row_f", [1, 128], F32)
    cs.ones_row_bf = A.sbuf("ones_row_bf", [1, 128], BF16)
    cs.stair = A.sbuf("stair", [128, 1536], BF16)
    cs.fc = A.sbuf("fconst", [128, 128], F32)
    C.fc_sb = cs.fc
    for dst, src in [(cs.ident_bf, ident_bf), (cs.ident_f, ident_f), (cs.ones_col, ones_col),
                     (cs.ones_row_f, ones_row_f), (cs.ones_row_bf, ones_row_bf), (cs.stair, stair),
                     (cs.fc, fcin)]:
        P.dma("sync", dst.ap, src.ap, reads=[src.buf], writes=[dst.buf])
    return cs


def host_consts():
    d = {}
    d["c_ident_bf"] = np.eye(128, dtype=BF)
    d["c_ident_f"] = np.eye(128, dtype=np.float32)
    d["c_ones_col"] = np.ones((128, 1), dtype=BF)
    d["c_ones_row_f"] = np.ones((1, 128), dtype=np.float32)
    d["c_ones_row_bf"] = np.ones((1, 128), dtype=BF)
    p = np.arange(128)[:, None]
    c = np.arange(1536)[None, :]
    d["c_stair"] = (p <= c - 512).astype(BF)
    sel8 = np.zeros((8, 8 * 128), dtype=np.float32)
    for e in range(8):
        sel8[e, e * 128:(e + 1) * 128] = 1.0
    d["c_sel8"] = sel8
    stairL = (p >= c - 512).astype(BF)
    d["c_stairL"] = stairL
    s16 = np.zeros((16, 16 * 128), dtype=np.float32)
    for r in range(16):
        s16[r, r * 128:(r + 1) * 128] = 1.0
    d["c_sel16"] = s16
    e127 = np.zeros((128, 1), dtype=np.float32)
    e127[127, 0] = 1.0
    d["c_e127"] = e127
    selh1 = np.zeros((32, 32 * 32), dtype=np.float32)
    for h in range(32):
        selh1[h, h * 32:(h + 1) * 32] = 1.0
    d["c_selH1"] = selh1
    selr = np.zeros((32, 64), dtype=np.float32)
    for m in range(64):
        selr[m % 32, m] = 1.0
    d["c_selR"] = selr
    selw = np.zeros((32, 16 * 128), dtype=np.float32)
    for blk in range(16):
        for m in range(128):
            selw[blk * 2 + m // 64, blk * 128 + m] = 1.0
    d["c_selW"] = selw
    d["c_selW_bf"] = selw.astype(BF)
    e32 = np.zeros((32, 32 * 64), dtype=np.float32)
    for h in range(32):
        e32[h, h * 64:(h + 1) * 64] = 1.0
    d["c_e32x64"] = e32
    return d



def phase_prep(C, cs):
    P, A = C.P, C.A
    A.push()
    x_in = C.hin("x_in", (T, D), F32)
    x_fm = C.hint("x_fm", (D, T), F32)
    xin_sb = load_fm(C, "prep_in", x_in, T, D, F32)
    out_t = [A.sbuf(f"prep_out{i}", [128, T], F32) for i in range(2)]
    ps = [A.psum(f"prep_ps{i}") for i in range(4)]
    for dt in range(8):
        ob = out_t[dt % 2]
        for tt in range(16):
            p = ps[tt % 4]
            t_tr(P, p[0:128, 0:128], xin_sb.t[:, tt * D + dt * 128: tt * D + dt * 128 + 128],
                 cs.ident_f, reads=[xin_sb.buf], writes=[p.buf])
            v_copy(P, ob, p, outsl=ob.t[:, tt * 128:(tt + 1) * 128], asl=p[0:128, 0:128])
        P.dma("scalar", x_fm.ap[dt * 128:(dt + 1) * 128, 0:T], ob.ap, reads=[ob.buf], writes=[x_fm.buf])
    A.pop()
    return x_fm

def store_tile(C, ht, sb, row0, TT_, queue="scalar", sl=None):
    ap = sb.ap if sl is None else sl
    C.P.dma(queue, ht.ap[row0:row0 + 128, 0:TT_], ap, reads=[sb.buf], writes=[ht.buf])



def phase_ln(C, cs):
    P, A = C.P, C.A
    A.push()
    x_fm = C.hint("x_fm", (D, T), F32)
    xn_h = C.hint("xn", (D, T), BF16)
    xq = [A.sbuf(f"ln_xq{i}", [128, 8 * 512], F32) for i in range(2)]
    xb = [A.sbuf(f"ln_xb{i}", [128, 8 * 512], BF16) for i in range(2)]
    sq = [A.sbuf(f"ln_sq{i}", [128, 8 * 512], BF16) for i in range(2)]
    xnt = [A.sbuf(f"ln_xnt{i}", [128, 8 * 512], BF16) for i in range(2)]
    tmp = [A.sbuf(f"ln_tmp{i}", [128, 8 * 512], F32) for i in range(2)]
    sums = A.sbuf("ln_sums", [1, T], F32)
    sqs = A.sbuf("ln_sqs", [1, T], F32)
    mean = A.sbuf("ln_mean", [1, T], F32)
    var = A.sbuf("ln_var", [1, T], F32)
    rstd = A.sbuf("ln_rstd", [1, T], F32)
    m2 = A.sbuf("ln_m2", [1, T], F32)
    ps_s = A.psum("ln_ps_s")
    ps_q = A.psum("ln_ps_q")
    ps_m = [A.psum(f"ln_ps_m{i}") for i in range(2)]
    ps_r = [A.psum(f"ln_ps_r{i}") for i in range(2)]
    for tc in range(4):
        pp = tc % 2
        x_, b_, q_, n_, t_ = xq[pp], xb[pp], sq[pp], xnt[pp], tmp[pp]
        csl = slice(tc * 512, (tc + 1) * 512)
        for ft in range(8):
            P.dma(("sync", "scalar")[ft % 2], x_.t[:, ft * 512:(ft + 1) * 512],
                  x_fm.ap[ft * 128:(ft + 1) * 128, csl], reads=[x_fm.buf], writes=[x_.buf])
        for ft in range(8):
            fsl = slice(ft * 512, (ft + 1) * 512)
            v_copy(P, b_, x_, outsl=b_.t[:, fsl], asl=x_.t[:, fsl])
            s_act(P, q_, x_, AF.Square, outsl=q_.t[:, fsl], asl=x_.t[:, fsl])
        for ft in range(8):
            fsl = slice(ft * 512, (ft + 1) * 512)
            t_mm(P, ps_s[0:1, 0:512], cs.ones_col.t[0:128, 0:1], b_.t[:, fsl],
                 start=(ft == 0), stop=(ft == 7), reads=[b_.buf, cs.ones_col.buf],
                 writes=[ps_s.buf], inc=(ft == 7))
        for ft in range(8):
            fsl = slice(ft * 512, (ft + 1) * 512)
            t_mm(P, ps_q[0:1, 0:512], cs.ones_col.t[0:128, 0:1], q_.t[:, fsl],
                 start=(ft == 0), stop=(ft == 7), reads=[q_.buf, cs.ones_col.buf],
                 writes=[ps_q.buf], inc=(ft == 7))
        v_copy(P, sums, ps_s, outsl=sums.t[0:1, csl], asl=ps_s[0:1, 0:512])
        v_copy(P, sqs, ps_q, outsl=sqs.t[0:1, csl], asl=ps_q[0:1, 0:512])
        v_ts(P, mean, sums, 1.0 / D, ALU.mult, outsl=mean.t[0:1, csl], asl=sums.t[0:1, csl])
        v_tt(P, m2, mean, mean, ALU.mult, outsl=m2.t[0:1, csl], asl=mean.t[0:1, csl], bsl=mean.t[0:1, csl])
        v_ts(P, var, sqs, 1.0 / D, ALU.mult, outsl=var.t[0:1, csl], asl=sqs.t[0:1, csl])
        v_tt(P, var, var, m2, ALU.subtract, outsl=var.t[0:1, csl], asl=var.t[0:1, csl], bsl=m2.t[0:1, csl])
        s_act(P, rstd, var, AF.Ln, bias=cfc(C, 1e-5)[0:1, :], sbufs=[C.fc_sb.buf],
              outsl=rstd.t[0:1, csl], asl=var.t[0:1, csl])
        s_act(P, rstd, rstd, AF.Exp, scale=cfc(C, -0.5)[0:1, :], sbufs=[C.fc_sb.buf],
              outsl=rstd.t[0:1, csl], asl=rstd.t[0:1, csl])
        bcast_row(P, C, ps_m[pp], mean, mean.t[0:1, csl], 512, cs.ones_row_f)
        bcast_row(P, C, ps_r[pp], rstd, rstd.t[0:1, csl], 512, cs.ones_row_f)
        for ft in range(8):
            fsl = slice(ft * 512, (ft + 1) * 512)
            v_tt(P, t_, x_, ps_m[pp], ALU.subtract, outsl=t_.t[:, fsl],
                 asl=x_.t[:, fsl], bsl=ps_m[pp][0:128, 0:512])
            v_tt(P, n_, t_, ps_r[pp], ALU.mult, outsl=n_.t[:, fsl],
                 asl=t_.t[:, fsl], bsl=ps_r[pp][0:128, 0:512])
        for ft in range(8):
            P.dma("gpsimd", xn_h.ap[ft * 128:(ft + 1) * 128, csl],
                  n_.t[:, ft * 512:(ft + 1) * 512], reads=[n_.buf], writes=[xn_h.buf])
    A.pop()



def phase_ffn(C, cs, L):
    P, A = C.P, C.A
    pre = f"L{L}_ffn"
    x_fm = C.hint("x_fm", (D, T), F32)
    xn_h = C.hint("xn", (D, T), BF16)
    ha = C.hint("ffn_ha", (4096, T), BF16)
    hb = C.hint("ffn_hb", (4096, T), BF16)

    w1 = C.hin(f"{pre}_w1T", (D, 4096), F32)
    w2 = C.hin(f"{pre}_w2T", (D, 4096), F32)
    h1 = C.hin(f"{pre}_h1T", (4096, 4096), F32)
    h2 = C.hin(f"{pre}_h2T", (4096, 4096), F32)
    w3 = C.hin(f"{pre}_w3T", (4096, D), F32)
    sc = C.hin(f"{pre}_scales", (128, 32 * 4 + 8), F32)

    A.push()
    scs = A.sbuf("ffn_sc", [128, 32 * 4 + 8], F32)
    P.dma("sync", scs.ap, sc.ap, reads=[sc.buf], writes=[scs.buf])
    stg = [A.sbuf(f"ffn_stgA{i}", [128, 512], F32) for i in range(4)]
    outt = [A.sbuf(f"ffn_outA{i}", [128, 512], BF16) for i in range(4)]
    si = [0]
    def evacA(wi, nt, t0, tw, ps, _state={}):
        key = (nt, t0)
        if wi == 0:
            k = si[0] % 4
            si[0] += 1
            _state[key] = k
            s_act(P, stg[k], ps, AF.Silu, scale=scs.t[:, nt:nt + 1], asl=ps[0:128, 0:tw],
                  outsl=stg[k].t[:, 0:tw], sbufs=[scs.buf])
        else:
            k = _state.pop(key)
            v_ts(P, outt[k], ps, scs.t[:, 32 + nt:32 + nt + 1], ALU.mult,
                 outsl=outt[k].t[:, 0:tw], asl=ps[0:128, 0:tw], sbufs=[scs.buf])
            v_tt(P, outt[k], stg[k], outt[k], ALU.mult, outsl=outt[k].t[:, 0:tw],
                 asl=stg[k].t[:, 0:tw], bsl=outt[k].t[:, 0:tw])
            P.dma("scalar", ha.ap[nt * 128:(nt + 1) * 128, t0:t0 + tw], outt[k].t[:, 0:tw],
                  reads=[outt[k].buf], writes=[ha.buf])
    mm_big(C, "ffnA", [(w1, 4096, None), (w2, 4096, None)], D, C.hint("xn", (D, T), BF16), T, evacA, gt=2, gn=1)
    A.pop()

    def run_gelu_stage(name, W_ht, src_ht, dst_ht, scol0):
        A.push()
        scl = A.sbuf("ffn_scg", [128, 32 * 4 + 8], F32)
        P.dma("sync", scl.ap, sc.ap, reads=[sc.buf], writes=[scl.buf])
        xres = load_fm(C, f"xres_{name}", src_ht, 4096, T, BF16)
        obufs = [A.sbuf(f"ffn_og{i}", [128, 512], BF16) for i in range(4)]
        oi = [0]
        def ev(wi, nt, t0, tw, ps):
            ob = obufs[oi[0] % 4]
            oi[0] += 1
            s_act(P, ob, ps, AF.Gelu, scale=scl.t[:, scol0 + nt:scol0 + nt + 1],
                  asl=ps[0:128, 0:tw], outsl=ob.t[:, 0:tw], sbufs=[scl.buf])
            P.dma("scalar", dst_ht.ap[nt * 128:(nt + 1) * 128, t0:t0 + tw], ob.t[:, 0:tw],
                  reads=[ob.buf], writes=[dst_ht.buf])
        mm_xres(C, name, W_ht, 4096, 4096, xres, ev)
        A.pop()
    run_gelu_stage("ffnB", h1, ha, hb, 64)
    run_gelu_stage("ffnC", h2, hb, ha, 96)

    A.push()
    scs3 = A.sbuf("ffn_sc4", [128, 32 * 4 + 8], F32)
    P.dma("sync", scs3.ap, sc.ap, reads=[sc.buf], writes=[scs3.buf])
    xres = load_fm(C, "xres_ffnD", ha, 4096, T, BF16)
    xt = [A.sbuf(f"ffn_xt{i}", [128, 512], F32) for i in range(4)]
    xo = [A.sbuf(f"ffn_xo{i}", [128, 512], F32) for i in range(4)]
    di = [0]
    def evacD(wi, nt, t0, tw, ps):
        k = di[0] % 4
        di[0] += 1
        P.dma("sync", xt[k].t[:, 0:tw], x_fm.ap[nt * 128:(nt + 1) * 128, t0:t0 + tw],
              reads=[x_fm.buf], writes=[xt[k].buf])
        v_ts(P, xo[k], ps, scs3.t[:, 128 + nt:128 + nt + 1], ALU.mult,
             outsl=xo[k].t[:, 0:tw], asl=ps[0:128, 0:tw], sbufs=[scs3.buf])
        v_tt(P, xo[k], xo[k], xt[k], ALU.add, outsl=xo[k].t[:, 0:tw],
             asl=xo[k].t[:, 0:tw], bsl=xt[k].t[:, 0:tw])
        P.dma("scalar", x_fm.ap[nt * 128:(nt + 1) * 128, t0:t0 + tw], xo[k].t[:, 0:tw],
              reads=[xo[k].buf], writes=[x_fm.buf])
    mm_xres(C, "ffnD", w3, 4096, D, xres, evacD)
    A.pop()



def phase_attn0(C, cs):
    P, A = C.P, C.A
    x_fm = C.hint("x_fm", (D, T), F32)
    xn_h = C.hint("xn", (D, T), BF16)
    wq = C.hin("L0_attn_W_q", (D, 2048), F32)
    wk = C.hin("L0_attn_W_k", (D, 2048), F32)
    wv = C.hin("L0_attn_W_v", (D, 2048), F32)
    wo = C.hin("L0_attn_W_o", (2048, D), F32)
    lq1 = C.hin("L0_attn_lambda_q1", (8, 128), F32)
    lk1 = C.hin("L0_attn_lambda_k1", (8, 128), F32)
    lq2 = C.hin("L0_attn_lambda_q2", (8, 128), F32)
    lk2 = C.hin("L0_attn_lambda_k2", (8, 128), F32)
    rmss = C.hin("L0_attn_rms_fm", (128, 2), F32)
    lam_hbm = C.hint("l0_lam", (1, 8), F32)
    SCALE = 128.0 ** -0.5

    A.push()
    lt = [A.sbuf(f"a0_l{i}", [8, 128], F32) for i in range(4)]
    for sb, ht in zip(lt, [lq1, lk1, lq2, lk2]):
        P.dma("sync", sb.ap, ht.ap, reads=[ht.buf], writes=[sb.buf])
    lprod = A.sbuf("a0_lp", [8, 128], F32)
    lred = A.sbuf("a0_lr", [8, 2], F32)
    v_tt(P, lprod, lt[0], lt[1], ALU.mult)
    v_red(P, lred, lprod, ALU.add, outsl=lred.t[0:8, 0:1], asl=lprod.t[0:8, 0:128])
    v_tt(P, lprod, lt[2], lt[3], ALU.mult)
    v_red(P, lred, lprod, ALU.add, outsl=lred.t[0:8, 1:2], asl=lprod.t[0:8, 0:128])
    s_act(P, lred, lred, AF.Exp)
    lamc = A.sbuf("a0_lam", [8, 1], F32)
    v_tt(P, lamc, lred, lred, ALU.subtract, outsl=lamc.t[0:8, 0:1],
         asl=lred.t[0:8, 0:1], bsl=lred.t[0:8, 1:2])
    v_ts(P, lamc, lamc, 0.8, ALU.add)
    P.dma("scalar", lam_hbm.ap.rearrange("a b -> b a"), lamc.t[0:8, 0:1], reads=[lamc.buf], writes=[lam_hbm.buf])
    lam_row = A.sbuf("a0_lamr", [1, 8], F32)
    P.dma("sync", lam_row.ap, lam_hbm.ap, reads=[lam_hbm.buf], writes=[lam_row.buf])
    gsc = A.sbuf("a0_g", [128, 2], F32)
    P.dma("sync", gsc.ap, rmss.ap, reads=[rmss.buf], writes=[gsc.buf])
    v_ts(P, gsc, gsc, 0.2, ALU.mult)

    xn_sb = load_fm(C, "a0_xn", xn_h, D, T, BF16)
    O_fm = A.sbuf("a0_O", [128, 16 * T], BF16)

    A.push()
    for h in range(8):
        qkv = {}
        ps_p = [A.psum(f"a0_psp{i}") for i in range(2)]
        for nm, w_ht in [("q", wq), ("k", wk)]:
            wb = A.sbuf(f"a0_wb_{nm}", [128, 8 * 256], BF16)
            for kt in range(8):
                P.dma("gpsimd", wb.t[:, kt * 256:(kt + 1) * 256],
                      w_ht.ap[kt * 128:(kt + 1) * 128, h * 256:(h + 1) * 256],
                      reads=[w_ht.buf], writes=[wb.buf])
            dst = A.sbuf(f"a0_{nm}", [128, 2 * T], BF16)
            for half in range(2):
                for tc in range(4):
                    ps = ps_p[tc % 2]
                    for kt in range(8):
                        t_mm(P, ps[0:128, 0:512],
                             wb.t[:, kt * 256 + half * 128: kt * 256 + half * 128 + 128],
                             xn_sb.t[:, kt * T + tc * 512: kt * T + tc * 512 + 512],
                             start=(kt == 0), stop=(kt == 7),
                             reads=[wb.buf, xn_sb.buf], writes=[ps.buf], inc=(kt == 7))
                    v_copy(P, dst, ps, outsl=dst.t[:, half * T + tc * 512: half * T + tc * 512 + 512],
                           asl=ps[0:128, 0:512])
            qkv[nm] = dst
        wb = A.sbuf("a0_wb_v", [128, 8 * 256], BF16)
        for kt in range(8):
            P.dma("gpsimd", wb.t[:, kt * 256:(kt + 1) * 256],
                  wv.ap[kt * 128:(kt + 1) * 128, h * 256:(h + 1) * 256],
                  reads=[wv.buf], writes=[wb.buf])
        vtm = A.sbuf("a0_vtm", [128, 16 * 256], BF16)
        for tt in range(16):
            ps = ps_p[tt % 2]
            for kt in range(8):
                t_mm(P, ps[0:128, 0:256],
                     xn_sb.t[:, kt * T + tt * 128: kt * T + tt * 128 + 128],
                     wb.t[:, kt * 256:(kt + 1) * 256],
                     start=(kt == 0), stop=(kt == 7),
                     reads=[wb.buf, xn_sb.buf], writes=[ps.buf], inc=(kt == 7))
            v_copy(P, vtm, ps, outsl=vtm.t[:, tt * 256:(tt + 1) * 256], asl=ps[0:128, 0:256])
        e1 = A.sbuf("a0_e1", [128, 16 * 512], BF16)
        e2 = A.sbuf("a0_e2", [128, 16 * 512], BF16)
        ps_r = [A.psum(f"a0_psr{i}") for i in range(2)]
        ps_o = [A.psum(f"a0_pso{i}") for i in range(4)]
        tmp1 = A.sbuf("a0_t1", [128, 512], F32)
        tmp2 = A.sbuf("a0_t2", [128, 512], F32)
        u_sb = A.sbuf("a0_u", [128, 2 * 512], F32)
        usq = A.sbuf("a0_usq", [128, 2 * 512], BF16)
        rows = A.sbuf("a0_rows", [1, 6 * 512], F32)
        for qc in range(4):
            nkt = qc * 4 + 4
            for ai, (e, qoff) in enumerate([(e1, 0), (e2, T)]):
                for kt in range(nkt):
                    ps = ps_p[kt % 2]
                    t_mm(P, ps[0:128, 0:512],
                         qkv["k"].t[:, qoff + kt * 128: qoff + kt * 128 + 128],
                         qkv["q"].t[:, qoff + qc * 512: qoff + qc * 512 + 512],
                         start=True, stop=True,
                         reads=[qkv["k"].buf, qkv["q"].buf], writes=[ps.buf], inc=True)
                    s_act(P, e, ps, AF.Exp, scale=SCALE,
                          outsl=e.t[:, kt * 512:(kt + 1) * 512], asl=ps[0:128, 0:512])
                for kt in range(qc * 4, nkt):
                    off = qc * 512 - kt * 128
                    v_tt(P, e, e, cs.stair, ALU.mult,
                         outsl=e.t[:, kt * 512:(kt + 1) * 512],
                         asl=e.t[:, kt * 512:(kt + 1) * 512],
                         bsl=cs.stair.t[:, off + 512: off + 1024])
                pr = ps_r[ai]
                for kt in range(nkt):
                    t_mm(P, pr[0:1, 0:512], cs.ones_col.t[0:128, 0:1],
                         e.t[:, kt * 512:(kt + 1) * 512],
                         start=(kt == 0), stop=(kt == nkt - 1),
                         reads=[e.buf, cs.ones_col.buf], writes=[pr.buf], inc=(kt == nkt - 1))
                for half in range(2):
                    po = ps_o[ai * 2 + half]
                    for kt in range(nkt):
                        t_mm(P, po[0:128, 0:512],
                             vtm.t[:, kt * 256 + half * 128: kt * 256 + half * 128 + 128],
                             e.t[:, kt * 512:(kt + 1) * 512],
                             start=(kt == 0), stop=(kt == nkt - 1),
                             reads=[vtm.buf, e.buf], writes=[po.buf], inc=(kt == nkt - 1))
            r1s = rows.t[0:1, 0:512]; rr2 = rows.t[0:1, 512:1024]
            crow = rows.t[0:1, 1024:1536]; erow = rows.t[0:1, 1536:2048]
            ssq = rows.t[0:1, 2048:2560]; rstd = rows.t[0:1, 2560:3072]
            v_copy(P, rows, ps_r[0], outsl=r1s, asl=ps_r[0][0:1, 0:512])
            v_recip(P, rows, ps_r[1], outsl=rr2, asl=ps_r[1][0:1, 0:512])
            v_tt(P, rows, rows, rows, ALU.mult, outsl=crow, asl=r1s, bsl=rr2)
            v_ts(P, rows, rows, lam_row.t[0:1, h:h + 1], ALU.mult, outsl=crow, asl=crow,
                 sbufs=[lam_row.buf])
            v_tt(P, rows, rows, rows, ALU.mult, outsl=erow, asl=r1s, bsl=r1s)
            v_ts(P, rows, rows, 1e-5, ALU.mult, outsl=erow, asl=erow)
            bcast_row(P, C, ps_p[0], rows, crow, 512, cs.ones_row_f)
            v_copy(P, tmp2, ps_p[0], asl=ps_p[0][0:128, 0:512])
            for half in range(2):
                v_tt(P, tmp1, ps_o[2 + half], tmp2, ALU.mult, asl=ps_o[2 + half][0:128, 0:512],
                     outsl=tmp1.t[:, 0:512], bsl=tmp2.t[:, 0:512])
                v_tt(P, u_sb, ps_o[half], tmp1, ALU.subtract,
                     outsl=u_sb.t[:, half * 512:(half + 1) * 512],
                     asl=ps_o[half][0:128, 0:512], bsl=tmp1.t[:, 0:512])
                s_act(P, usq, u_sb, AF.Square, outsl=usq.t[:, half * 512:(half + 1) * 512],
                      asl=u_sb.t[:, half * 512:(half + 1) * 512])
            for half in range(2):
                t_mm(P, ps_r[0][0:1, 0:512], cs.ones_col.t[0:128, 0:1],
                     usq.t[:, half * 512:(half + 1) * 512],
                     start=(half == 0), stop=(half == 1),
                     reads=[usq.buf, cs.ones_col.buf], writes=[ps_r[0].buf], inc=(half == 1))
            v_ts(P, rows, ps_r[0], 1.0 / 256, ALU.mult, outsl=ssq, asl=ps_r[0][0:1, 0:512])
            v_tt(P, rows, rows, rows, ALU.add, outsl=ssq, asl=ssq, bsl=erow)
            s_act(P, rows, rows, AF.Ln, outsl=rstd, asl=ssq)
            s_act(P, rows, rows, AF.Exp, scale=cfc(C, -0.5)[0:1, :], outsl=rstd, asl=rstd,
                  sbufs=[C.fc_sb.buf])
            bcast_row(P, C, ps_p[1], rows, rstd, 512, cs.ones_row_f)
            v_copy(P, tmp2, ps_p[1], asl=ps_p[1][0:128, 0:512])
            for half in range(2):
                v_tt(P, tmp1, u_sb, tmp2, ALU.mult,
                     outsl=tmp1.t[:, 0:512], asl=u_sb.t[:, half * 512:(half + 1) * 512],
                     bsl=tmp2.t[:, 0:512])
                v_ts(P, O_fm, tmp1, gsc.t[:, half:half + 1], ALU.mult,
                     outsl=O_fm.t[:, (h * 2 + half) * T + qc * 512: (h * 2 + half) * T + qc * 512 + 512],
                     asl=tmp1.t[:, 0:512], sbufs=[gsc.buf])
    A.pop()

    xt = [A.sbuf(f"a0_xt{i}", [128, 512], F32) for i in range(4)]
    xo = [A.sbuf(f"a0_xo{i}", [128, 512], F32) for i in range(4)]
    di = [0]
    def evacO(wi, nt, t0, tw, ps):
        k = di[0] % 4
        di[0] += 1
        P.dma("sync", xt[k].t[:, 0:tw], x_fm.ap[nt * 128:(nt + 1) * 128, t0:t0 + tw],
              reads=[x_fm.buf], writes=[xt[k].buf])
        v_tt(P, xo[k], ps, xt[k], ALU.add, outsl=xo[k].t[:, 0:tw],
             asl=ps[0:128, 0:tw], bsl=xt[k].t[:, 0:tw])
        P.dma("scalar", x_fm.ap[nt * 128:(nt + 1) * 128, t0:t0 + tw], xo[k].t[:, 0:tw],
              reads=[xo[k].buf], writes=[x_fm.buf])
    mm_big(C, "a0O", [(wo, D, None)], 2048, None, T, evacO, x_resident=O_fm, gt=2, gn=2)
    A.pop()



def phase_attn2(C, cs):
    P, A = C.P, C.A
    x_fm = C.hint("x_fm", (D, T), F32)
    xn_h = C.hint("xn", (D, T), BF16)
    wq = C.hin("L2_attn_Wq", (D, D), F32)
    wk = C.hin("L2_attn_Wk", (D, D), F32)
    wv = C.hin("L2_attn_Wv", (D, D), F32)
    wo = C.hin("L2_attn_Wo", (D, D), F32)
    bo = C.hin("L2_attn_bo_fm", (128, 8), F32)
    SCALE = 64.0 ** -0.5

    A.push()
    xn_sb = load_fm(C, "a2_xn", xn_h, D, T, BF16)
    O_fm = A.sbuf("a2_O", [128, 8 * T], BF16)
    bo_sb = A.sbuf("a2_bo", [128, 8], F32)
    P.dma("sync", bo_sb.ap, bo.ap, reads=[bo.buf], writes=[bo_sb.buf])

    A.push()
    vtm_init = [False]
    for g in range(8):
        qkv = {}
        ps_p = [A.psum(f"a2_psp{i}") for i in range(2)]
        for wi, (nm, w_ht) in enumerate([("q", wq), ("k", wk)]):
            wb = A.sbuf(f"a2_wb_{nm}", [128, 8 * 128], BF16)
            for kt in range(8):
                P.dma("gpsimd", wb.t[:, kt * 128:(kt + 1) * 128],
                      w_ht.ap[kt * 128:(kt + 1) * 128, g * 128:(g + 1) * 128],
                      reads=[w_ht.buf], writes=[wb.buf])
            dst = A.sbuf(f"a2_{nm}", [128, T], BF16)
            for tc in range(4):
                ps = ps_p[tc % 2]
                for kt in range(8):
                    t_mm(P, ps[0:128, 0:512], wb.t[:, kt * 128:(kt + 1) * 128],
                         xn_sb.t[:, kt * T + tc * 512: kt * T + tc * 512 + 512],
                         start=(kt == 0), stop=(kt == 7),
                         reads=[wb.buf, xn_sb.buf], writes=[ps.buf], inc=(kt == 7))
                v_copy(P, dst, ps, outsl=dst.t[:, tc * 512:(tc + 1) * 512], asl=ps[0:128, 0:512])
            qkv[nm] = dst
        wb = A.sbuf("a2_wb_v", [128, 8 * 128], BF16)
        for kt in range(8):
            P.dma("gpsimd", wb.t[:, kt * 128:(kt + 1) * 128],
                  wv.ap[kt * 128:(kt + 1) * 128, g * 128:(g + 1) * 128],
                  reads=[wv.buf], writes=[wb.buf])
        vtm = A.sbuf("a2_vtm", [128, 16 * 144], BF16)
        if not vtm_init[0]:
            v_memset(P, vtm, 1.0)
            vtm_init[0] = True
        for tt in range(16):
            ps = ps_p[tt % 2]
            for kt in range(8):
                t_mm(P, ps[0:128, 0:128],
                     xn_sb.t[:, kt * T + tt * 128: kt * T + tt * 128 + 128],
                     wb.t[:, kt * 128:(kt + 1) * 128],
                     start=(kt == 0), stop=(kt == 7),
                     reads=[wb.buf, xn_sb.buf], writes=[ps.buf], inc=(kt == 7))
            v_copy(P, vtm, ps, outsl=vtm.t[:, tt * 144:tt * 144 + 64], asl=ps[0:128, 0:64])
            v_copy(P, vtm, ps, outsl=vtm.t[:, tt * 144 + 65:tt * 144 + 129], asl=ps[0:128, 64:128])
        eT = [A.sbuf(f"a2_eT{i}", [128, 16 * 512], BF16) for i in range(2)]
        ps_sc = [A.psum(f"a2_pssc{i}") for i in range(2)]
        ps_o = [A.psum(f"a2_pso{i}") for i in range(2)]
        ps_bc = A.psum("a2_psbc")
        rrow = A.sbuf("a2_rrow", [1, 512], F32)
        btmp = A.sbuf("a2_btmp", [128, 512], F32)
        for h in range(2):
            hb = h * 64
            for qc in range(4):
                e = eT[qc % 2]
                for kt in range(16):
                    ps = ps_sc[kt % 2]
                    t_mm(P, ps[0:128, 0:512],
                         qkv["k"].t[hb:hb + 64, kt * 128:(kt + 1) * 128],
                         qkv["q"].t[hb:hb + 64, qc * 512:(qc + 1) * 512],
                         start=True, stop=True,
                         reads=[qkv["k"].buf, qkv["q"].buf], writes=[ps.buf], inc=True)
                    s_act(P, e, ps, AF.Exp, scale=SCALE,
                          outsl=e.t[:, kt * 512:(kt + 1) * 512], asl=ps[0:128, 0:512])
                po = ps_o[qc % 2]
                for kt in range(16):
                    t_mm(P, po[0:65, 0:512],
                         vtm.t[:, kt * 144 + h * 65: kt * 144 + h * 65 + 65],
                         e.t[:, kt * 512:(kt + 1) * 512],
                         start=(kt == 0), stop=(kt == 15),
                         reads=[vtm.buf, e.buf], writes=[po.buf], inc=(kt == 15))
                v_recip(P, rrow, po, outsl=rrow.t[0:1, 0:512], asl=po[64:65, 0:512])
                bcast_row(P, C, ps_bc, rrow, rrow.t[0:1, 0:512], 512, cs.ones_row_f)
                v_copy(P, btmp, ps_bc, asl=ps_bc[0:128, 0:512])
                v_tt(P, O_fm, po, btmp, ALU.mult,
                     outsl=O_fm.t[hb:hb + 64, g * T + qc * 512: g * T + qc * 512 + 512],
                     asl=po[0:64, 0:512], bsl=btmp[0:64, 0:512])
    A.pop()

    xt = [A.sbuf(f"a2_xt{i}", [128, 512], F32) for i in range(4)]
    xo = [A.sbuf(f"a2_xo{i}", [128, 512], F32) for i in range(4)]
    di = [0]
    def evacO(wi, nt, t0, tw, ps):
        k = di[0] % 4
        di[0] += 1
        P.dma("sync", xt[k].t[:, 0:tw], x_fm.ap[nt * 128:(nt + 1) * 128, t0:t0 + tw],
              reads=[x_fm.buf], writes=[xt[k].buf])
        v_ts(P, xo[k], ps, bo_sb.t[:, nt:nt + 1], ALU.add,
             outsl=xo[k].t[:, 0:tw], asl=ps[0:128, 0:tw], sbufs=[bo_sb.buf])
        v_tt(P, xo[k], xo[k], xt[k], ALU.add, outsl=xo[k].t[:, 0:tw],
             asl=xo[k].t[:, 0:tw], bsl=xt[k].t[:, 0:tw])
        P.dma("scalar", x_fm.ap[nt * 128:(nt + 1) * 128, t0:t0 + tw], xo[k].t[:, 0:tw],
              reads=[xo[k].buf], writes=[x_fm.buf])
    mm_big(C, "a2O", [(wo, D, None)], D, None, T, evacO, x_resident=O_fm, gt=2, gn=2)
    A.pop()



TPI = float(np.float32(2 * np.pi))
TPI2 = float(np.float64(2 * np.pi) - np.float64(np.float32(2 * np.pi)))
LN1024 = float(np.log(1024.0))

def trig_reduce(P, C, dst, src, dstsl, srcsl, yy, ki, kf, phase_add=0.0):
    sh = 8192.0 - phase_add / TPI if False else 8192.0
    if phase_add != 0.0:
        v_ts(P, dst, src, phase_add, ALU.add, outsl=dstsl, asl=srcsl)
        srcap = dstsl
        srcbuf = dst
    else:
        srcap = srcsl
        srcbuf = src
    v_ts(P, yy, srcbuf, 1.0 / TPI, ALU.mult, 8192.0, ALU.add, asl=srcap)
    v_copy(P, ki, yy)
    v_copy(P, kf, ki)
    v_ts(P, kf, kf, -8192.0, ALU.add)
    v_stt(P, dst, kf, -TPI, srcbuf, ALU.mult, ALU.add, outsl=dstsl, i1sl=srcap)
    v_stt(P, dst, kf, -TPI2, dst, ALU.mult, ALU.add, outsl=dstsl, i1sl=dstsl)
    v_ts(P, yy, dst, float(np.pi), ALU.is_gt, asl=dstsl)
    v_stt(P, dst, yy, -TPI, dst, ALU.mult, ALU.add, outsl=dstsl, i1sl=dstsl)

def softplus32(P, C, out, outsl, src, srcsl, wk, bias_col=None, bias_bufs=(), pn=32, width=T):
    v0 = wk.t[0:pn, 0:width]; a0 = wk.t[0:pn, width:2 * width]; m0 = wk.t[0:pn, 2 * width:3 * width]
    if bias_col is not None:
        v_ts(P, wk, src, bias_col, ALU.add, outsl=v0, asl=srcsl, sbufs=list(bias_bufs))
    else:
        v_copy(P, wk, src, outsl=v0, asl=srcsl)
    s_act(P, wk, wk, AF.Abs, outsl=a0, asl=v0)
    s_act(P, wk, wk, AF.Exp, scale=-1.0, outsl=a0, asl=a0)
    s_act(P, wk, wk, AF.Ln, bias=cfc(C, 1.0)[0:pn, :], outsl=a0, asl=a0, sbufs=[C.fc_sb.buf])
    v_ts(P, wk, wk, 0.0, ALU.max, outsl=m0, asl=v0)
    v_tt(P, out, wk, wk, ALU.add, outsl=outsl, asl=m0, bsl=a0)


def _sl64(sb, g, hp):
    return sb.t[hp * 64:(hp + 1) * 64, g * 128:(g + 1) * 128]


def phase_ssm(C, cs):
    P, A = C.P, C.A
    x_fm = C.hint("x_fm", (D, T), F32)
    xn_h = C.hint("xn", (D, T), BF16)
    win = C.hin("L1_ssm_winT", (D, 4736), F32)
    wout = C.hin("L1_ssm_woutT", (2048, D), F32)
    dtb = C.hin("L1_ssm_dtb", (32, 1), F32)
    Dp = C.hin("L1_ssm_D", (32, 1), F32)
    cE127 = C.hin("c_e127", (128, 1), F32)
    cSelH1 = C.hin("c_selH1", (32, 32 * 32), F32)
    cSelW = C.hin("c_selW", (32, 16 * 128), F32)
    cSelWb = C.hin("c_selW_bf", (32, 16 * 128), BF16)
    cSelR = C.hin("c_selR", (32, 64), F32)
    z_h = C.hint("ssm_z", (2048, T), BF16)
    xs_h = C.hint("ssm_xs", (2048, T), BF16)
    w_h = C.hint("ssm_w", (2048, T), BF16)
    xd_h = C.hint("ssm_xd", (2048, T), BF16)
    k1_h = C.hint("ssm_k1", (16, 64, 32 * 128), BF16)
    q1_h = C.hint("ssm_q1", (16, 64, 32 * 128), BF16)
    cE32 = C.hin("c_e32x64", (32, 32 * 64), F32)
    y2_h = C.hint("ssm_y2", (T, 2048), BF16)

    A.push()
    brest = A.sbuf("s_Br", [96, T], F32)
    crest = A.sbuf("s_Cr", [96, T], F32)
    DQ = A.sbuf("s_DQ", [128, 16 * 32], F32)
    DK = A.sbuf("s_DK", [128, 16 * 32], F32)
    DE = A.sbuf("s_DE", [128, 16 * 32], F32)
    lamR = A.sbuf("s_lamR", [128, 16 * 32], F32)
    e127 = A.sbuf("s_e127", [128, 1], F32)
    dtbc = A.sbuf("s_dtb", [32, 1], F32)
    Dc = A.sbuf("s_D", [32, 1], F32)
    for dst, srcht in [(e127, cE127), (dtbc, dtb), (Dc, Dp)]:
        P.dma("sync", dst.ap, srcht.ap, reads=[srcht.buf], writes=[dst.buf])

    A.push()
    bs = A.sbuf("s_B", [32, T], F32)
    cs_ = A.sbuf("s_C", [32, T], F32)
    dtv = A.sbuf("s_dtv", [32, T], F32)
    tah = A.sbuf("s_tah", [32, T], F32)
    cw1 = A.sbuf("s_cw1", [32, T], BF16)
    cw2 = A.sbuf("s_cw2", [32, T], BF16)

    A.push()
    sm = A.sbuf("s_sm", [128, T], F32)
    an = A.sbuf("s_an", [32, T], F32)
    ca = A.sbuf("s_ca", [32, T], F32)
    lav = A.sbuf("s_la", [32, T], F32)
    trv = A.sbuf("s_tr", [32, T], F32)
    wk = A.sbuf("s_wk", [32, 3 * T], F32)
    ca_tm = A.sbuf("s_catm", [128, 16 * 32], F32)
    tmpv = A.sbuf("s_tv", [128, 16 * 32], F32)
    ob = [A.sbuf(f"s_ob{i}", [128, 512], BF16) for i in range(4)]
    obi = [0]
    def evacP(wi, nt, t0, tw, ps):
        if nt < 32:
            k = obi[0] % 4
            obi[0] += 1
            dst = z_h if nt < 16 else xs_h
            r0 = (nt % 16) * 128
            v_copy(P, ob[k], ps, outsl=ob[k].t[:, 0:tw], asl=ps[0:128, 0:tw])
            P.dma("scalar", dst.ap[r0:r0 + 128, t0:t0 + tw], ob[k].t[:, 0:tw],
                  reads=[ob[k].buf], writes=[dst.buf])
        elif nt == 32:
            v_copy(P, bs, ps, outsl=bs.t[0:32, t0:t0 + tw], asl=ps[0:32, 0:tw])
            v_copy(P, cs_, ps, outsl=cs_.t[0:32, t0:t0 + tw], asl=ps[32:64, 0:tw])
        elif nt == 33:
            v_copy(P, brest, ps, outsl=brest.t[0:96, t0:t0 + tw], asl=ps[0:96, 0:tw])
        elif nt == 34:
            v_copy(P, crest, ps, outsl=crest.t[0:96, t0:t0 + tw], asl=ps[0:96, 0:tw])
        elif nt == 35:
            v_copy(P, sm, ps, outsl=sm.t[:, t0:t0 + tw], asl=ps[0:128, 0:tw])
        else:
            v_copy(P, an, ps, outsl=an.t[0:32, t0:t0 + tw], asl=ps[0:32, 0:tw])
    A.push()
    xn_sb = load_fm(C, "s_xn", xn_h, D, T, BF16)
    mm_xres(C, "sproj", win, D, 4736, xn_sb, evacP, thalf=1024, kg=4)
    A.pop()
    softplus32(P, C, dtv, dtv.t[0:32, 0:T], sm, sm.t[0:32, 0:T], wk, bias_col=dtbc.t[0:32, 0:1], bias_bufs=[dtbc.buf])
    softplus32(P, C, lav, lav.t[0:32, 0:T], sm, sm.t[32:64, 0:T], wk)
    v_ts(P, lav, lav, 1e-4, ALU.max)
    v_stt(P, lav, lav, -1.0, dtv, ALU.mult, ALU.mult)
    v_scan(P, ca, lav, lav, 0.0, ALU.add, ALU.bypass)
    s_act(P, trv, sm, AF.Sigmoid, asl=sm.t[64:96, 0:T])
    ela = wk.t[0:32, 0:T]
    s_act(P, wk, lav, AF.Exp, outsl=ela)
    t1_ = wk.t[0:32, T:2 * T]
    v_ts(P, wk, trv, -1.0, ALU.mult, 1.0, ALU.add, outsl=t1_)
    v_tt(P, wk, wk, dtv, ALU.mult, outsl=t1_, asl=t1_)
    v_tt(P, cw1, wk, wk, ALU.mult, asl=t1_, bsl=ela)
    v_tt(P, cw2, trv, dtv, ALU.mult)
    s_act(P, tah, an, AF.Tanh, asl=an.t[0:32, 0:T])
    A.push()
    ps_a = A.psum("s_psa")
    ps_b = A.psum("s_psb")
    for c in range(16):
        t_tr(P, ps_a[0:128, 0:32], ca.t[0:32, c * 128:(c + 1) * 128], cs.ident_f,
             reads=[ca.buf], writes=[ps_a.buf], pn=32)
        v_copy(P, ca_tm, ps_a, outsl=ca_tm.t[:, c * 32:(c + 1) * 32], asl=ps_a[0:128, 0:32])
    brow = A.sbuf("s_brow", [1, 32 + 16 * 32], F32)
    v_memset(P, brow, 0.0, outsl=brow.t[0:1, 0:32])
    t_mm(P, ps_b[0:1, 0:512], e127.t[0:128, 0:1], ca_tm.ap,
         start=True, stop=True, reads=[ca_tm.buf, e127.buf], writes=[ps_b.buf], inc=True)
    v_copy(P, brow, ps_b, outsl=brow.t[0:1, 32:32 + 512], asl=ps_b[0:1, 0:512])
    bcast_row(P, C, ps_a, brow, brow.t[0:1, 0:512], 512, cs.ones_row_f)
    v_tt(P, tmpv, ca_tm, ps_a, ALU.subtract, bsl=ps_a[0:128, 0:512])
    s_act(P, DQ, tmpv, AF.Exp, bias=cfc(C, LN1024)[0:128, :], sbufs=[C.fc_sb.buf])
    v_ts(P, tmpv, tmpv, -1.0, ALU.mult, 84.0, ALU.min)
    s_act(P, DK, tmpv, AF.Exp, bias=cfc(C, -LN1024)[0:128, :], sbufs=[C.fc_sb.buf])
    prevb = A.sbuf("s_prevb", [128, 16 * 32], F32)
    v_copy(P, prevb, ps_a, asl=ps_a[0:128, 0:512])
    bcast_row(P, C, ps_b, brow, brow.t[0:1, 32:32 + 512], 512, cs.ones_row_f)
    v_tt(P, tmpv, ps_b, ca_tm, ALU.subtract, asl=ps_b[0:128, 0:512])
    s_act(P, DE, tmpv, AF.Exp, bias=cfc(C, -LN1024)[0:128, :], sbufs=[C.fc_sb.buf])
    v_tt(P, tmpv, ps_b, prevb, ALU.subtract, asl=ps_b[0:128, 0:512])
    s_act(P, lamR, tmpv, AF.Exp)
    A.pop()
    A.pop()

    A.push()
    selw = A.sbuf("s_selw", [32, 16 * 128], F32)
    selwb = A.sbuf("s_selwb", [32, 16 * 128], BF16)
    P.dma("sync", selw.ap, cSelW.ap, reads=[cSelW.buf], writes=[selw.buf])
    P.dma("sync", selwb.ap, cSelWb.ap, reads=[cSelWb.buf], writes=[selwb.buf])
    xsb = A.sbuf("s_xsb", [128, T], BF16)
    xsh = A.sbuf("s_xsh", [128, T], BF16)
    cwb = A.sbuf("s_cwb", [128, 2 * 512], BF16)
    ut = A.sbuf("s_ut", [128, 512], F32)
    ut2 = A.sbuf("s_ut2", [128, 512], F32)
    wo_ = A.sbuf("s_wo", [128, T], BF16)
    xdt = A.sbuf("s_xdt", [128, T], BF16)
    ps_c = [A.psum(f"s_psc{i}") for i in range(2)]
    ps_d = A.psum("s_psd")
    for blk in range(16):
        P.dma("sync", xsb.ap, xs_h.ap[blk * 128:(blk + 1) * 128, 0:T], reads=[xs_h.buf], writes=[xsb.buf])
        v_memset(P, xsh, 0.0, outsl=xsh.t[:, 0:1])
        v_copy(P, xsh, xsb, outsl=xsh.t[:, 1:T], asl=xsb.t[:, 0:T - 1])
        t_mm(P, ps_d[0:128, 0:1], selw.t[0:32, blk * 128:(blk + 1) * 128], Dc.t[0:32, 0:1],
             start=True, stop=True, reads=[selw.buf, Dc.buf], writes=[ps_d.buf], inc=True)
        for tc in range(4):
            sl = slice(tc * 512, (tc + 1) * 512)
            t_mm(P, ps_c[0][0:128, 0:512], selwb.t[0:32, blk * 128:(blk + 1) * 128],
                 cw1.t[0:32, sl], start=True, stop=True,
                 reads=[selwb.buf, cw1.buf], writes=[ps_c[0].buf], inc=True)
            t_mm(P, ps_c[1][0:128, 0:512], selwb.t[0:32, blk * 128:(blk + 1) * 128],
                 cw2.t[0:32, sl], start=True, stop=True,
                 reads=[selwb.buf, cw2.buf], writes=[ps_c[1].buf], inc=True)
            v_copy(P, cwb, ps_c[0], outsl=cwb.t[:, 0:512], asl=ps_c[0][0:128, 0:512])
            v_copy(P, cwb, ps_c[1], outsl=cwb.t[:, 512:1024], asl=ps_c[1][0:128, 0:512])
            v_tt(P, ut, xsh, cwb, ALU.mult, outsl=ut.ap, asl=xsh.t[:, sl], bsl=cwb.t[:, 0:512])
            v_tt(P, ut2, xsb, cwb, ALU.mult, outsl=ut2.ap, asl=xsb.t[:, sl], bsl=cwb.t[:, 512:1024])
            v_tt(P, wo_, ut, ut2, ALU.add, outsl=wo_.t[:, sl], asl=ut.ap, bsl=ut2.ap)
        P.dma("scalar", w_h.ap[blk * 128:(blk + 1) * 128, 0:T], wo_.ap, reads=[wo_.buf], writes=[w_h.buf])
        v_ts(P, xdt, xsb, ps_d[0:128, 0:1], ALU.mult, sbufs=[ps_d.buf])
        P.dma("scalar", xd_h.ap[blk * 128:(blk + 1) * 128, 0:T], xdt.ap, reads=[xdt.buf], writes=[xd_h.buf])
    A.pop()

    A.push()
    selh = A.sbuf("s_selh", [32, 32 * 32], F32)
    P.dma("sync", selh.ap, cSelH1.ap, reads=[cSelH1.buf], writes=[selh.buf])
    rate = A.sbuf("s_rate", [32, 2 * T], F32)
    th = A.sbuf("s_th", [32, 2 * T], F32)
    cosv = A.sbuf("s_cos", [32, 2 * T], F32)
    sinv = A.sbuf("s_sin", [32, 2 * T], F32)
    yy64 = A.sbuf("s_yy", [32, 1024], F32)
    ki64 = A.sbuf("s_ki", [32, 1024], mybir.dt.int32)
    kf64 = A.sbuf("s_kf", [32, 1024], F32)
    kq = A.sbuf("s_kq", [32, 2 * T], BF16)
    qq = A.sbuf("s_qq", [32, 2 * T], BF16)
    ps_e = [A.psum(f"s_pse{i}") for i in range(2)]
    for g in range(16):
        for hp in range(2):
            h = 2 * g + hp
            for tc in range(4):
                sl = slice(tc * 512, (tc + 1) * 512)
                osl = slice(hp * T + tc * 512, hp * T + tc * 512 + 512)
                t_mm(P, ps_e[0][0:32, 0:512], selh.t[0:32, h * 32:(h + 1) * 32], dtv.t[0:32, sl],
                     start=True, stop=True, reads=[selh.buf, dtv.buf], writes=[ps_e[0].buf], inc=True)
                v_copy(P, rate, ps_e[0], outsl=rate.t[0:32, osl], asl=ps_e[0][0:32, 0:512])
                v_stt(P, rate, tah, float(np.pi), rate, ALU.mult, ALU.mult,
                      outsl=rate.t[0:32, osl], i0sl=tah.t[0:32, sl], i1sl=rate.t[0:32, osl])
        v_scan(P, th, rate, rate, 0.0, ALU.add, ALU.bypass, outsl=th.t[0:32, 0:2 * T],
               d0sl=rate.t[0:32, 0:2 * T], d1sl=rate.t[0:32, 0:2 * T])
        for qtr in range(4):
            hsl = slice(qtr * 1024, (qtr + 1) * 1024)
            trig_reduce(P, C, cosv, th, cosv.t[0:32, hsl], th.t[0:32, hsl], yy64, ki64, kf64,
                        phase_add=float(np.pi / 2))
            trig_reduce(P, C, sinv, th, sinv.t[0:32, hsl], th.t[0:32, hsl], yy64, ki64, kf64)
        s_act(P, cosv, cosv, AF.Sin, outsl=cosv.t[0:32, 0:2 * T], asl=cosv.t[0:32, 0:2 * T])
        s_act(P, sinv, sinv, AF.Sin, outsl=sinv.t[0:32, 0:2 * T], asl=sinv.t[0:32, 0:2 * T])
        for hp in range(2):
            h = 2 * g + hp
            hT = slice(hp * T, hp * T + T)
            v_tt(P, kq, bs, cosv, ALU.mult, outsl=kq.t[0:32, 0:T],
                 asl=bs.t[0:32, 0:T], bsl=cosv.t[0:32, hT])
            v_tt(P, kq, bs, sinv, ALU.mult, outsl=kq.t[0:32, T:2 * T],
                 asl=bs.t[0:32, 0:T], bsl=sinv.t[0:32, hT])
            v_tt(P, qq, cs_, cosv, ALU.mult, outsl=qq.t[0:32, 0:T],
                 asl=cs_.t[0:32, 0:T], bsl=cosv.t[0:32, hT])
            v_tt(P, qq, cs_, sinv, ALU.mult, outsl=qq.t[0:32, T:2 * T],
                 asl=cs_.t[0:32, 0:T], bsl=sinv.t[0:32, hT])
            for piece in range(2):
                P.dma("scalar", k1_h.ap[:, piece * 32:(piece + 1) * 32, h * 128:(h + 1) * 128]
                      .rearrange("c r t -> r c t"),
                      kq.t[0:32, piece * T:(piece + 1) * T].rearrange("r (c t) -> r c t", t=128),
                      reads=[kq.buf], writes=[k1_h.buf])
                P.dma("scalar", q1_h.ap[:, piece * 32:(piece + 1) * 32, h * 128:(h + 1) * 128]
                      .rearrange("c r t -> r c t"),
                      qq.t[0:32, piece * T:(piece + 1) * T].rearrange("r (c t) -> r c t", t=128),
                      reads=[qq.buf], writes=[q1_h.buf])
    A.pop()
    A.pop()

    A.push()
    e32 = A.sbuf("s_e32", [32, 2048], BF16)
    P.dma("gpsimd", e32.ap, cE32.ap, reads=[cE32.buf], writes=[e32.buf])
    stair4 = A.sbuf("s_st4", [128, 512], F32)
    for k in range(4):
        v_copy(P, stair4, cs.stair, outsl=stair4.t[:, k * 128:(k + 1) * 128],
               asl=cs.stair.t[:, 512:640])
    bbf = A.sbuf("s_bbf", [96, T], BF16)
    cbf = A.sbuf("s_cbf", [96, T], BF16)
    v_copy(P, bbf, brest)
    v_copy(P, cbf, crest)
    SaA = A.sbuf("s_SaA", [64, 2048], F32)
    SbA = A.sbuf("s_SbA", [96, 2048], F32)
    SaB = A.sbuf("s_SaB", [64, 2048], BF16)
    SbB = A.sbuf("s_SbB", [96, 2048], BF16)
    for t_ in (SaA, SbA, SaB, SbB):
        v_memset(P, t_, 0.0)
    k1c = [A.sbuf(f"s_k1c{i}", [64, 32 * 128], BF16) for i in range(2)]
    q1c = [A.sbuf(f"s_q1c{i}", [64, 32 * 128], BF16) for i in range(2)]
    k1tm = [A.sbuf(f"s_k1tm{i}", [128, 32 * 64], BF16) for i in range(2)]
    k2tm = [A.sbuf(f"s_k2tm{i}", [128, 96], BF16) for i in range(2)]
    wtm = [A.sbuf(f"s_wtm{i}", [128, 2048], BF16) for i in range(2)]
    ztm = [A.sbuf(f"s_ztm{i}", [128, 2048], BF16) for i in range(2)]
    xdtm = [A.sbuf(f"s_xdtm{i}", [128, 2048], BF16) for i in range(2)]
    zsl = A.sbuf("s_zsl", [128, 2048], BF16)
    wdk = A.sbuf("s_wdk", [128, 2048], BF16)
    wdkv = A.sbuf("s_wdkv", [128, 2048], BF16)
    eDQ = A.sbuf("s_eDQ", [128, 2048], BF16)
    eLam = A.sbuf("s_eLam", [128, 2048], BF16)
    shm = A.sbuf("s_shm", [128, 128], BF16)
    tmsb = [A.sbuf(f"s_tmsb{i}", [32, 128], BF16) for i in range(2)]
    gmb = [A.sbuf(f"s_gmb{i}", [128, 1024], BF16) for i in range(2)]
    yt1 = [A.sbuf(f"s_yt1{i}", [128, 512], F32) for i in range(2)]
    y2c = [A.sbuf(f"s_y2c{i}", [128, 2048], BF16) for i in range(2)]
    ps_ex = [A.psum(f"e_pex{i}") for i in range(2)]
    ps_sc = [A.psum(f"e_psc{i}") for i in range(2)]
    ps_py = A.psum("e_ppy")
    ps_sa = A.psum("e_psa")
    ps_sb = A.psum("e_psb")

    def grp_front(c, g8, kc, qc):
        gb = gmb[g8 % 2]
        for half in range(2):
            pb = ps_sc[half]
            for r in range(4):
                h = g8 * 8 + half * 4 + r
                t_mm(P, pb[0:128, r * 128:(r + 1) * 128], kc.t[0:64, h * 128:(h + 1) * 128],
                     qc.t[0:64, h * 128:(h + 1) * 128], start=(r == 0), stop=(r == 3),
                     reads=[kc.buf, qc.buf], writes=[pb.buf], inc=(r == 3))
            v_tt(P, gb, pb, stair4, ALU.mult, outsl=gb.t[:, half * 512:(half + 1) * 512],
                 asl=pb[0:128, 0:512], bsl=stair4.ap)

    def grp_back(c, g8, kc, qc, ktm, k2, xt, yo):
        gsl = slice(g8 * 512, (g8 + 1) * 512)
        gb = gmb[g8 % 2]
        yt = yt1[g8 % 2]
        t_mm(P, ps_py[0:128, 0:512], cbf.t[0:96, c * 128:(c + 1) * 128], SbB.t[0:96, gsl],
             start=True, stop=False, reads=[cbf.buf, SbB.buf], writes=[ps_py.buf], inc=False)
        t_mm(P, ps_py[0:128, 0:512], shm.ap, wdkv.t[:, gsl],
             start=False, stop=False, reads=[shm.buf, wdkv.buf], writes=[ps_py.buf], inc=False)
        for hl in range(8):
            h = g8 * 8 + hl
            rsl = slice(hl * 64, (hl + 1) * 64)
            t_mm(P, ps_py[0:128, rsl], gb.t[:, hl * 128:(hl + 1) * 128], wdkv.t[:, h * 64:(h + 1) * 64],
                 start=False, stop=False, reads=[gb.buf, wdkv.buf], writes=[ps_py.buf], inc=False)
            t_mm(P, ps_py[0:128, rsl], qc.t[0:64, h * 128:(h + 1) * 128], SaB.t[0:64, h * 64:(h + 1) * 64],
                 start=False, stop=(hl == 7), reads=[qc.buf, SaB.buf], writes=[ps_py.buf], inc=(hl == 7))
        for hl in range(8):
            h = g8 * 8 + hl
            t_mm(P, ps_sa[0:64, hl * 64:(hl + 1) * 64], ktm.t[:, h * 64:(h + 1) * 64],
                 wdk.t[:, h * 64:(h + 1) * 64], start=(hl == 0), stop=(hl == 7),
                 reads=[ktm.buf, wdk.buf], writes=[ps_sa.buf], inc=(hl == 7))
        t_mm(P, ps_sb[0:96, 0:512], k2.t[:, 0:96], wdk.t[:, gsl],
             start=True, stop=True, reads=[k2.buf, wdk.buf], writes=[ps_sb.buf], inc=True)
        v_tt(P, yt, ps_py, eDQ, ALU.mult, asl=ps_py[0:128, 0:512], bsl=eDQ.t[:, gsl])
        v_tt(P, yt, yt, xt, ALU.add, bsl=xt.t[:, gsl])
        v_tt(P, yo, yt, zsl, ALU.mult, outsl=yo.t[:, gsl], asl=yt.ap, bsl=zsl.t[:, gsl])
        v_tt(P, SaA, SaA, eLam, ALU.mult, outsl=SaA.t[0:64, gsl], asl=SaA.t[0:64, gsl],
             bsl=eLam.t[0:64, gsl])
        v_tt(P, SaA, SaA, ps_sa, ALU.add, outsl=SaA.t[0:64, gsl], asl=SaA.t[0:64, gsl],
             bsl=ps_sa[0:64, 0:512])
        v_copy(P, SaB, SaA, outsl=SaB.t[0:64, gsl], asl=SaA.t[0:64, gsl])
        v_tt(P, SbA, SbA, eLam, ALU.mult, outsl=SbA.t[0:96, gsl], asl=SbA.t[0:96, gsl],
             bsl=eLam.t[0:96, gsl])
        v_tt(P, SbA, SbA, ps_sb, ALU.add, outsl=SbA.t[0:96, gsl], asl=SbA.t[0:96, gsl],
             bsl=ps_sb[0:96, 0:512])
        v_copy(P, SbB, SbA, outsl=SbB.t[0:96, gsl], asl=SbA.t[0:96, gsl])

    for c in range(16):
        pp = c % 2
        csl = slice(c * 128, (c + 1) * 128)
        kc, qc, ktm, k2 = k1c[pp], q1c[pp], k1tm[pp], k2tm[pp]
        wt, zt, xt, yo = wtm[pp], ztm[pp], xdtm[pp], y2c[pp]
        P.dma("sync", kc.t[0:64, :], k1_h.ap[c, :, :], reads=[k1_h.buf], writes=[kc.buf])
        P.dma("sync", qc.t[0:64, :], q1_h.ap[c, :, :], reads=[q1_h.buf], writes=[qc.buf])
        ps_tb = A.psum("e_ptb", BF16)
        for h in range(32):
            t_tr(P, ps_tb[0:128, (h % 8) * 64:(h % 8 + 1) * 64],
                 kc.t[0:64, h * 128:(h + 1) * 128], cs.ident_bf,
                 reads=[kc.buf], writes=[ps_tb.buf], pn=64)
            v_copy(P, ktm, ps_tb, outsl=ktm.t[:, h * 64:(h + 1) * 64],
                   asl=ps_tb[0:128, (h % 8) * 64:(h % 8 + 1) * 64])
        for g in range(16):
            gg = slice(g * 128, (g + 1) * 128)
            P.dma("scalar", zt.t[:, gg], z_h.ap[gg, csl], reads=[z_h.buf], writes=[zt.buf], transpose=True)
            P.dma("scalar", xt.t[:, gg], xd_h.ap[gg, csl], reads=[xd_h.buf], writes=[xt.buf], transpose=True)
            P.dma("scalar", wt.t[:, gg], w_h.ap[gg, csl], reads=[w_h.buf], writes=[wt.buf], transpose=True)
        for q4 in range(4):
            sl = slice(q4 * 512, (q4 + 1) * 512)
            s_act(P, zsl, zt, AF.Silu, outsl=zsl.t[:, sl], asl=zt.t[:, sl])
        t_mm(P, ps_ex[1][0:128, 0:128], bbf.t[0:96, csl], cbf.t[0:96, csl],
             start=True, stop=True, reads=[bbf.buf, cbf.buf], writes=[ps_ex[1].buf], inc=True)
        v_tt(P, shm, ps_ex[1], stair4, ALU.mult, asl=ps_ex[1][0:128, 0:128],
             bsl=stair4.t[:, 0:128])
        t_tr(P, ps_ex[0][0:128, 0:96], brest.t[0:96, csl], cs.ident_f,
             reads=[brest.buf], writes=[ps_ex[0].buf], pn=96)
        v_copy(P, k2, ps_ex[0], outsl=k2.t[:, 0:96], asl=ps_ex[0][0:128, 0:96])
        cl = slice(c * 32, (c + 1) * 32)
        for i, (src, dst, fold) in enumerate([(DK, wdkv, wt), (DE, wdk, wt),
                                              (DQ, eDQ, None), (lamR, eLam, None)]):
            pt = ps_ex[i % 2]
            tms = tmsb[i % 2]
            t_tr(P, pt[0:32, 0:128], src.t[:, cl], cs.ident_f, reads=[src.buf],
                 writes=[pt.buf], pn=128)
            v_copy(P, tms, pt, outsl=tms.t[0:32, 0:128], asl=pt[0:32, 0:128])
            for q4 in range(4):
                sl = slice(q4 * 512, (q4 + 1) * 512)
                pe = (ps_ex[1], ps_sc[0], ps_sc[1], ps_py)[q4]
                t_mm(P, pe[0:128, 0:512], tms.t[0:32, 0:128], e32.t[0:32, sl],
                     start=True, stop=True, reads=[tms.buf, e32.buf], writes=[pe.buf], inc=True)
                if fold is not None:
                    v_tt(P, dst, fold, pe, ALU.mult, outsl=dst.t[:, sl], asl=fold.t[:, sl],
                         bsl=pe[0:128, 0:512])
                else:
                    v_copy(P, dst, pe, outsl=dst.t[:, sl], asl=pe[0:128, 0:512])
        PIPE = 1
        if PIPE:
            grp_front(c, 0, kc, qc)
            for g8 in range(4):
                if g8 < 3:
                    grp_front(c, g8 + 1, kc, qc)
                grp_back(c, g8, kc, qc, ktm, k2, xt, yo)
        else:
            for g8 in range(4):
                grp_front(c, g8, kc, qc)
                grp_back(c, g8, kc, qc, ktm, k2, xt, yo)
        P.dma("gpsimd", y2_h.ap[csl, :], yo.ap, reads=[yo.buf], writes=[y2_h.buf])
    A.pop()

    A.push()
    wb = A.sbuf("s_wob", [128, 16 * D], BF16)
    for kt in range(16):
        P.dma("gpsimd", wb.t[:, kt * D:(kt + 1) * D], wout.ap[kt * 128:(kt + 1) * 128, 0:D],
              reads=[wout.buf], writes=[wb.buf])
    y2f = A.sbuf("s_y2f", [128, 16 * 1024], BF16)
    ps_o = [A.psum(f"s_pso{i}") for i in range(4)]
    xt = [A.sbuf(f"s_xt{i}", [128, 512], F32) for i in range(4)]
    xo = [A.sbuf(f"s_xo{i}", [128, 512], F32) for i in range(4)]
    for sw in range(2):
        t0s = sw * 1024
        for kt in range(16):
            P.dma("sync", y2f.t[:, kt * 1024:(kt + 1) * 1024],
                  y2_h.ap[t0s:t0s + 1024, kt * 128:(kt + 1) * 128],
                  reads=[y2_h.buf], writes=[y2f.buf], transpose=True)
        for ng in range(4):
            grp = []
            for nt in range(2):
                for tci in range(2):
                    grp.append((ng * 2 + nt, tci, ps_o[len(grp)]))
            for kt in range(16):
                for gi, (nt, tci, ps) in enumerate(grp):
                    t_mm(P, ps[0:128, 0:512], wb.t[:, kt * D + nt * 128: kt * D + nt * 128 + 128],
                         y2f.t[:, kt * 1024 + tci * 512: kt * 1024 + tci * 512 + 512],
                         start=(kt == 0), stop=(kt == 15),
                         reads=[wb.buf, y2f.buf], writes=[ps.buf],
                         inc=(kt == 15 and gi == len(grp) - 1))
            for ki_, (nt, tci, ps) in enumerate(grp):
                k = ki_ % 4
                t0 = t0s + tci * 512
                P.dma("sync", xt[k].t[:, 0:512], x_fm.ap[nt * 128:(nt + 1) * 128, t0:t0 + 512],
                      reads=[x_fm.buf], writes=[xt[k].buf])
                v_tt(P, xo[k], ps, xt[k], ALU.add, outsl=xo[k].t[:, 0:512],
                     asl=ps[0:128, 0:512], bsl=xt[k].t[:, 0:512])
                P.dma("scalar", x_fm.ap[nt * 128:(nt + 1) * 128, t0:t0 + 512], xo[k].t[:, 0:512],
                      reads=[xo[k].buf], writes=[x_fm.buf])
    A.pop()
    A.pop()


def phase_moe(C, cs):
    P, A = C.P, C.A
    x_fm = C.hint("x_fm", (D, T), F32)
    xn_h = C.hint("xn", (D, T), BF16)
    gw = C.hin("L1_moe_gateT", (D, 8), F32)
    w13 = [C.hin(f"L1_moe_w13T_{e}", (D, 4096), F32) for e in range(8)]
    w2e = [C.hin(f"L1_moe_w2T_{e}", (2048, D), F32) for e in range(8)]
    sel8 = C.hin("c_sel8", (8, 8 * 128), F32)
    act_h = C.hint("moe_act", (2048, T), BF16)
    BIG = 1.0e9

    A.push()
    xn_sb = load_fm(C, "moe_xn", xn_h, D, T, BF16)
    comb8 = A.sbuf("moe_comb8", [8, T], F32)
    sel_sb = A.sbuf("moe_sel8", [8, 8 * 128], F32)
    P.dma("sync", sel_sb.ap, sel8.ap, reads=[sel8.buf], writes=[sel_sb.buf])
    A.push()
    gwb = A.sbuf("moe_gw", [128, 8 * 8], BF16)
    for kt in range(8):
        P.dma("gpsimd", gwb.t[:, kt * 8:(kt + 1) * 8], gw.ap[kt * 128:(kt + 1) * 128, 0:8],
              reads=[gw.buf], writes=[gwb.buf])
    ps_g = A.psum("moe_psg")
    ps_t = A.psum("moe_pst")
    rt = A.sbuf("moe_rt", [128, 8], F32)
    col = A.sbuf("moe_col", [128, 8], F32)
    g4 = A.sbuf("moe_g4", [128, 4], F32)
    e4 = A.sbuf("moe_e4", [128, 4], F32)
    sm8 = A.sbuf("moe_sm8", [128, 8], F32)
    for tt in range(16):
        for kt in range(8):
            t_mm(P, ps_g[0:128, 0:8], xn_sb.t[:, kt * T + tt * 128: kt * T + tt * 128 + 128],
                 gwb.t[:, kt * 8:(kt + 1) * 8], start=(kt == 0), stop=(kt == 7),
                 reads=[xn_sb.buf, gwb.buf], writes=[ps_g.buf], inc=(kt == 7))
        s_act(P, rt, ps_g, AF.Exp, outsl=rt.t[:, 0:8], asl=ps_g[0:128, 0:8])
        v_red(P, col, rt, ALU.add, outsl=col.t[:, 0:1], asl=rt.t[:, 0:8])
        v_recip(P, col, col, outsl=col.t[:, 1:2], asl=col.t[:, 0:1])
        v_ts(P, rt, rt, col.t[:, 1:2], ALU.mult, sbufs=[col.buf])
        v_tt(P, g4, rt, rt, ALU.max, outsl=g4.ap, asl=rt.t[:, 0:8:2], bsl=rt.t[:, 1:8:2])
        v_red(P, col, g4, ALU.max, outsl=col.t[:, 2:3], asl=g4.ap)
        v_ts(P, e4, g4, col.t[:, 2:3], ALU.is_equal, sbufs=[col.buf])
        v_stt(P, e4, e4, -BIG, g4, ALU.mult, ALU.add)
        v_red(P, col, e4, ALU.max, outsl=col.t[:, 3:4], asl=e4.ap)
        v_ts(P, g4, g4, col.t[:, 3:4], ALU.is_ge, sbufs=[col.buf])
        v_copy(P, sm8, g4, outsl=sm8.t[:, 0:8:2], asl=g4.ap)
        v_copy(P, sm8, g4, outsl=sm8.t[:, 1:8:2], asl=g4.ap)
        v_tt(P, rt, rt, sm8, ALU.mult)
        v_red(P, col, rt, ALU.max, outsl=col.t[:, 4:5], asl=rt.t[:, 0:8])
        v_ts(P, sm8, rt, col.t[:, 4:5], ALU.is_equal, sbufs=[col.buf])
        v_stt(P, sm8, sm8, -BIG, rt, ALU.mult, ALU.add)
        v_red(P, col, sm8, ALU.max, outsl=col.t[:, 5:6], asl=sm8.t[:, 0:8])
        v_ts(P, sm8, rt, col.t[:, 5:6], ALU.is_ge, sbufs=[col.buf])
        v_tt(P, col, col, col, ALU.add, outsl=col.t[:, 6:7], asl=col.t[:, 4:5], bsl=col.t[:, 5:6])
        v_recip(P, col, col, outsl=col.t[:, 7:8], asl=col.t[:, 6:7])
        v_tt(P, rt, rt, sm8, ALU.mult)
        v_ts(P, rt, rt, col.t[:, 7:8], ALU.mult, sbufs=[col.buf])
        t_tr(P, ps_t[0:8, 0:128], rt.t[:, 0:8], cs.ident_f, reads=[rt.buf], writes=[ps_t.buf])
        v_copy(P, comb8, ps_t, outsl=comb8.t[0:8, tt * 128:(tt + 1) * 128], asl=ps_t[0:8, 0:128])
    A.pop()

    acc = A.sbuf("moe_acc", [128, 8 * T], F32)
    for ft in range(8):
        v_memset(P, acc, 0.0, outsl=acc.t[:, ft * T:(ft + 1) * T])

    stg = [A.sbuf(f"moe_stg{i}", [128, 512], F32) for i in range(4)]
    ob = [A.sbuf(f"moe_ob{i}", [128, 512], BF16) for i in range(4)]
    tmp = [A.sbuf(f"moe_t{i}", [128, 512], F32) for i in range(4)]
    cbt = A.sbuf("moe_cb", [128, T], F32)
    for e in range(8):
        si = [0]
        def evac1(wi, nt, t0, tw, ps, _state={}):
            blk = nt // 2
            if nt % 2 == 0:
                k = si[0] % 4
                si[0] += 1
                _state[(blk, t0)] = k
                s_act(P, stg[k], ps, AF.Silu, outsl=stg[k].t[:, 0:tw], asl=ps[0:128, 0:tw])
            else:
                k = _state.pop((blk, t0))
                v_tt(P, ob[k], ps, stg[k], ALU.mult, outsl=ob[k].t[:, 0:tw],
                     asl=ps[0:128, 0:tw], bsl=stg[k].t[:, 0:tw])
                P.dma("scalar", act_h.ap[blk * 128:(blk + 1) * 128, t0:t0 + tw], ob[k].t[:, 0:tw],
                      reads=[ob[k].buf], writes=[act_h.buf])
        mm_xres(C, "moe1", w13[e], D, 4096, xn_sb, evac1, thalf=1024, kg=4)
        ps_b = A.psum("moe_psb")
        for tc in range(4):
            t_mm(P, ps_b[0:128, 0:512], sel_sb.t[0:8, e * 128:(e + 1) * 128],
                 comb8.t[0:8, tc * 512:(tc + 1) * 512], start=True, stop=True,
                 reads=[sel_sb.buf, comb8.buf], writes=[ps_b.buf], inc=True)
            v_copy(P, cbt, ps_b, outsl=cbt.t[:, tc * 512:(tc + 1) * 512], asl=ps_b[0:128, 0:512])
        ti = [0]
        def evac2(wi, nt, t0, tw, ps):
            k = ti[0] % 4
            ti[0] += 1
            v_tt(P, tmp[k], ps, cbt, ALU.mult, outsl=tmp[k].t[:, 0:tw],
                 asl=ps[0:128, 0:tw], bsl=cbt.t[:, t0:t0 + tw])
            v_tt(P, acc, acc, tmp[k], ALU.add,
                 outsl=acc.t[:, nt * T + t0: nt * T + t0 + tw],
                 asl=acc.t[:, nt * T + t0: nt * T + t0 + tw], bsl=tmp[k].t[:, 0:tw])
        mm_big(C, "moe2", [(w2e[e], D, None)], 2048, act_h, T, evac2, gt=1, gn=2, xdouble=False)

    xt2 = A.sbuf("moe_xt", [128, T], F32)
    for ft in range(8):
        P.dma("sync", xt2.ap, x_fm.ap[ft * 128:(ft + 1) * 128, 0:T], reads=[x_fm.buf], writes=[xt2.buf])
        v_tt(P, xt2, xt2, acc, ALU.add, asl=xt2.ap, bsl=acc.t[:, ft * T:(ft + 1) * T])
        P.dma("scalar", x_fm.ap[ft * 128:(ft + 1) * 128, 0:T], xt2.ap, reads=[xt2.buf], writes=[x_fm.buf])
    A.pop()



def phase_ffn3(C, cs):
    P, A = C.P, C.A
    x_fm = C.hint("x_fm", (D, T), F32)
    xn_h = C.hint("xn", (D, T), BF16)
    wr = C.hin("L3_ffn_W_rT", (D, D), F32)
    wk = C.hin("L3_ffn_W_kT", (D, D), F32)
    wv = C.hin("L3_ffn_W_vT", (D, D), F32)
    mus = C.hin("L3_ffn_mu_fm", (128, 16), F32)

    A.push()
    xr = A.sbuf("f3_xr", [128, 8 * T], BF16)
    xk = A.sbuf("f3_xk", [128, 8 * T], BF16)
    mu = A.sbuf("f3_mu", [128, 16], F32)
    P.dma("sync", mu.ap, mus.ap, reads=[mus.buf], writes=[mu.buf])
    A.push()
    xn_sb = load_fm(C, "f3_xn", xn_h, D, T, BF16)
    xsh = A.sbuf("f3_xsh", [128, 8 * T], BF16)
    dtmp = A.sbuf("f3_d", [128, T], F32)
    for ft in range(8):
        v_memset(P, xsh, 0.0, outsl=xsh.t[:, ft * T: ft * T + 1])
        v_copy(P, xsh, xn_sb, outsl=xsh.t[:, ft * T + 1:(ft + 1) * T],
               asl=xn_sb.t[:, ft * T:(ft + 1) * T - 1])
    for ft in range(8):
        v_tt(P, dtmp, xn_sb, xsh, ALU.subtract, outsl=dtmp.ap,
             asl=xn_sb.t[:, ft * T:(ft + 1) * T], bsl=xsh.t[:, ft * T:(ft + 1) * T])
        v_stt(P, xr, dtmp, mu.t[:, ft:ft + 1], xsh, ALU.mult, ALU.add,
              outsl=xr.t[:, ft * T:(ft + 1) * T], i0sl=dtmp.ap,
              i1sl=xsh.t[:, ft * T:(ft + 1) * T], sbufs=[mu.buf])
        v_stt(P, xk, dtmp, mu.t[:, 8 + ft:9 + ft], xsh, ALU.mult, ALU.add,
              outsl=xk.t[:, ft * T:(ft + 1) * T], i0sl=dtmp.ap,
              i1sl=xsh.t[:, ft * T:(ft + 1) * T], sbufs=[mu.buf])
    A.pop()
    r_sb = A.sbuf("f3_r", [128, 8 * T], BF16)
    ka = A.sbuf("f3_ka", [128, 8 * T], BF16)
    def evacR(wi, nt, t0, tw, ps):
        s_act(P, r_sb, ps, AF.Sigmoid, outsl=r_sb.t[:, nt * T + t0: nt * T + t0 + tw],
              asl=ps[0:128, 0:tw])
    mm_big(C, "f3r", [(wr, D, None)], D, None, T, evacR, x_resident=xr, gt=2, gn=2)
    relu_t = [A.sbuf(f"f3_rel{i}", [128, 512], F32) for i in range(4)]
    ri = [0]
    def evacK(wi, nt, t0, tw, ps):
        k = ri[0] % 4
        ri[0] += 1
        v_ts(P, relu_t[k], ps, 0.0, ALU.max, outsl=relu_t[k].t[:, 0:tw], asl=ps[0:128, 0:tw])
        s_act(P, ka, relu_t[k], AF.Square, outsl=ka.t[:, nt * T + t0: nt * T + t0 + tw],
              asl=relu_t[k].t[:, 0:tw])
    mm_big(C, "f3k", [(wk, D, None)], D, None, T, evacK, x_resident=xk, gt=2, gn=2)
    xt = [A.sbuf(f"f3_xt{i}", [128, 512], F32) for i in range(4)]
    xo = [A.sbuf(f"f3_xo{i}", [128, 512], F32) for i in range(4)]
    di = [0]
    def evacV(wi, nt, t0, tw, ps):
        k = di[0] % 4
        di[0] += 1
        P.dma("sync", xt[k].t[:, 0:tw], x_fm.ap[nt * 128:(nt + 1) * 128, t0:t0 + tw],
              reads=[x_fm.buf], writes=[xt[k].buf])
        v_tt(P, xo[k], ps, r_sb, ALU.mult, outsl=xo[k].t[:, 0:tw],
             asl=ps[0:128, 0:tw], bsl=r_sb.t[:, nt * T + t0: nt * T + t0 + tw])
        v_tt(P, xo[k], xo[k], xt[k], ALU.add, outsl=xo[k].t[:, 0:tw],
             asl=xo[k].t[:, 0:tw], bsl=xt[k].t[:, 0:tw])
        P.dma("scalar", x_fm.ap[nt * 128:(nt + 1) * 128, t0:t0 + tw], xo[k].t[:, 0:tw],
              reads=[xo[k].buf], writes=[x_fm.buf])
    mm_big(C, "f3v", [(wv, D, None)], D, None, T, evacV, x_resident=ka, gt=2, gn=2)
    A.pop()


TPAD = 64
TP_ = TPAD + T

def phase_attn3(C, cs):
    P, A = C.P, C.A
    x_fm = C.hint("x_fm", (D, T), F32)
    xn_h = C.hint("xn", (D, T), BF16)
    wq = C.hin("L3_wqT", (D, D), F32)
    wk = C.hin("L3_wkT", (D, D), F32)
    wv = C.hin("L3_wvT", (D, D), F32)
    wb_ = C.hin("L3_wbT", (D, 4), F32)
    wo = C.hin("L3_woT", (D, D), F32)
    convw = C.hin("L3_conv_fm", (128, 8 * 12), F32)
    firs = C.hin("L3_firs_fm", (128, 8 * 5), F32)
    firl = C.hin("L3_firl_fm", (128, 8 * 63), F32)
    gw1 = C.hin("L3_gw1T", (1088, D), F32)
    gb1 = C.hin("L3_gb1_fm", (128, 8), F32)
    gw2 = C.hin("L3_gw2T", (D, 16), F32)
    gb2 = C.hin("L3_gb2_col", (16, 1), F32)
    ltemp = C.hin("L3_attn_log_temp", (4, 4), F32)
    fpar = C.hin("L3_attn_floor_param", (4, 4), F32)
    onw = C.hin("L3_onw_fm", (128, 2), F32)
    stairL = C.hin("c_stairL", (128, 1536), BF16)
    sel16 = C.hin("c_sel16", (16, 16 * 128), F32)
    ps_hbm = C.hint("l3_ps", (D, T), BF16)
    pl_hbm = C.hint("l3_pl", (D, T), BF16)
    pd_hbm = C.hint("l3_pd", (D, T), BF16)
    gi_h = C.hint("l3_gate_in", (1088, T), BF16)
    sc_h = C.hint("l3_small", (1, 32), F32)

    A.push()
    stl = A.sbuf("a3_stl", [128, 1536], BF16)
    P.dma("sync", stl.ap, stairL.ap, reads=[stairL.buf], writes=[stl.buf])
    vpad = A.sbuf("a3_v", [128, 8 * TP_], BF16)
    qn = A.sbuf("a3_q", [128, 8 * T], BF16)
    kn = A.sbuf("a3_k", [128, 8 * T], BF16)
    beta_tm = A.sbuf("a3_btm", [128, 16 * 4], F32)
    wts = A.sbuf("a3_wts", [16, T], F32)

    A.push()
    xn_sb = load_fm(C, "a3_xn", xn_h, D, T, BF16)
    cwf = A.sbuf("a3_cw", [128, 8 * 12], F32)
    P.dma("sync", cwf.ap, convw.ap, reads=[convw.buf], writes=[cwf.buf])
    raw = A.sbuf("a3_raw", [128, 8 * TP_], BF16)
    diag = A.sbuf("a3_diag", [128, 4 * 128], BF16)
    wbq = A.sbuf("a3_wb", [128, 8 * 128], BF16)
    ps_pj = [A.psum(f"a3_pspj{i}") for i in range(2)]
    ps_cv = [A.psum(f"a3_pscv{i}") for i in range(2)]
    for si, (w_ht, dstname) in enumerate([(wq, "q"), (wk, "k"), (wv, "v")]):
        for ft in range(8):
            v_memset(P, raw, 0.0, outsl=raw.t[:, ft * TP_: ft * TP_ + TPAD])
        for nt in range(8):
            for kt in range(8):
                P.dma("gpsimd", wbq.t[:, kt * 128:(kt + 1) * 128],
                      w_ht.ap[kt * 128:(kt + 1) * 128, nt * 128:(nt + 1) * 128],
                      reads=[w_ht.buf], writes=[wbq.buf])
            for tc in range(4):
                ps = ps_pj[tc % 2]
                for kt in range(8):
                    t_mm(P, ps[0:128, 0:512], wbq.t[:, kt * 128:(kt + 1) * 128],
                         xn_sb.t[:, kt * T + tc * 512: kt * T + tc * 512 + 512],
                         start=(kt == 0), stop=(kt == 7),
                         reads=[wbq.buf, xn_sb.buf], writes=[ps.buf], inc=(kt == 7))
                v_copy(P, raw, ps,
                       outsl=raw.t[:, nt * TP_ + TPAD + tc * 512: nt * TP_ + TPAD + tc * 512 + 512],
                       asl=ps[0:128, 0:512])
        dst = {"q": qn, "k": kn, "v": vpad}[dstname]
        for pt in range(8):
            for tap in range(4):
                v_ts(P, diag, cs.ident_bf, cwf.t[:, pt * 12 + si * 4 + tap: pt * 12 + si * 4 + tap + 1],
                     ALU.mult, outsl=diag.t[:, tap * 128:(tap + 1) * 128], sbufs=[cwf.buf])
            for tc in range(4):
                ps = ps_cv[tc % 2]
                for tap in range(4):
                    off = pt * TP_ + TPAD + tc * 512 - 3 + tap
                    t_mm(P, ps[0:128, 0:512], diag.t[:, tap * 128:(tap + 1) * 128],
                         raw.t[:, off: off + 512],
                         start=(tap == 0), stop=(tap == 3),
                         reads=[diag.buf, raw.buf], writes=[ps.buf], inc=(tap == 3))
                if dstname == "v":
                    osl = dst.t[:, pt * TP_ + TPAD + tc * 512: pt * TP_ + TPAD + tc * 512 + 512]
                else:
                    osl = dst.t[:, pt * T + tc * 512: pt * T + tc * 512 + 512]
                s_act(P, dst, ps, AF.Silu, outsl=osl, asl=ps[0:128, 0:512])
    for ft in range(8):
        v_memset(P, vpad, 0.0, outsl=vpad.t[:, ft * TP_: ft * TP_ + TPAD])
    wbb = A.sbuf("a3_wbb", [128, 8 * 4], BF16)
    for kt in range(8):
        P.dma("gpsimd", wbb.t[:, kt * 4:(kt + 1) * 4], wb_.ap[kt * 128:(kt + 1) * 128, 0:4],
              reads=[wb_.buf], writes=[wbb.buf])
    brow = A.sbuf("a3_brow", [4, T], F32)
    for tc in range(4):
        ps = ps_pj[tc % 2]
        for kt in range(8):
            t_mm(P, ps[0:4, 0:512], wbb.t[:, kt * 4:(kt + 1) * 4],
                 xn_sb.t[:, kt * T + tc * 512: kt * T + tc * 512 + 512],
                 start=(kt == 0), stop=(kt == 7),
                 reads=[wbb.buf, xn_sb.buf], writes=[ps.buf], inc=(kt == 7))
        s_act(P, brow, ps, AF.Sigmoid, outsl=brow.t[0:4, tc * 512:(tc + 1) * 512], asl=ps[0:4, 0:512])
    for tt in range(16):
        ps = ps_cv[tt % 2]
        t_tr(P, ps[0:128, 0:4], brow.t[0:4, tt * 128:(tt + 1) * 128], cs.ident_f,
             reads=[brow.buf], writes=[ps.buf], pn=4)
        v_copy(P, beta_tm, ps, outsl=beta_tm.t[:, tt * 4:(tt + 1) * 4], asl=ps[0:128, 0:4])
    A.pop()

    A.push()
    sqt = A.sbuf("a3_sq", [128, 2 * 512], BF16)
    rows = A.sbuf("a3_rows", [1, 512], F32)
    btmp = A.sbuf("a3_bt", [128, 512], F32)
    ps_r = A.psum("a3_psr")
    ps_bc = A.psum("a3_psbc")
    for tens in (qn, kn):
        for h in range(4):
            for tc in range(4):
                for pt in range(2):
                    blk = h * 2 + pt
                    s_act(P, sqt, tens, AF.Square, outsl=sqt.t[:, pt * 512:(pt + 1) * 512],
                          asl=tens.t[:, blk * T + tc * 512: blk * T + tc * 512 + 512])
                for pt in range(2):
                    t_mm(P, ps_r[0:1, 0:512], cs.ones_col.t[0:128, 0:1],
                         sqt.t[:, pt * 512:(pt + 1) * 512],
                         start=(pt == 0), stop=(pt == 1),
                         reads=[sqt.buf, cs.ones_col.buf], writes=[ps_r.buf], inc=(pt == 1))
                r0 = rows.t[0:1, 0:512]
                s_act(P, rows, ps_r, AF.Ln, bias=cfc(C, 1e-6)[0:1, :], outsl=r0,
                      asl=ps_r[0:1, 0:512], sbufs=[C.fc_sb.buf])
                s_act(P, rows, rows, AF.Exp, scale=-0.5, outsl=r0, asl=r0)
                bcast_row(P, C, ps_bc, rows, r0, 512, cs.ones_row_f)
                v_copy(P, btmp, ps_bc, asl=ps_bc[0:128, 0:512])
                for pt in range(2):
                    blk = h * 2 + pt
                    sl2 = slice(blk * T + tc * 512, blk * T + tc * 512 + 512)
                    v_tt(P, tens, tens, btmp, ALU.mult, outsl=tens.t[:, sl2],
                         asl=tens.t[:, sl2], bsl=btmp.t[:, 0:512])
    A.pop()

    A.push()
    fsw = A.sbuf("a3_fsw", [128, 8 * 5], F32)
    flw = A.sbuf("a3_flw", [128, 8 * 63], F32)
    P.dma("sync", fsw.ap, firs.ap, reads=[firs.buf], writes=[fsw.buf])
    P.dma("sync", flw.ap, firl.ap, reads=[firl.buf], writes=[flw.buf])
    dg = A.sbuf("a3_dg", [128, 63 * 128], BF16)
    fob = [A.sbuf(f"a3_fob{i}", [128, 512], BF16) for i in range(4)]
    ps_f = [A.psum(f"a3_psf{i}") for i in range(2)]
    fi = [0]
    for (wsb, ntap, dsth) in [(fsw, 5, ps_hbm), (flw, 63, pl_hbm)]:
        for pt in range(8):
            for tap in range(ntap):
                v_ts(P, dg, cs.ident_bf, wsb.t[:, pt * ntap + tap: pt * ntap + tap + 1],
                     ALU.mult, outsl=dg.t[:, tap * 128:(tap + 1) * 128], sbufs=[wsb.buf])
            for tc in range(4):
                ps = ps_f[tc % 2]
                for tap in range(ntap):
                    off = pt * TP_ + TPAD + tc * 512 - (ntap - 1) + tap
                    t_mm(P, ps[0:128, 0:512], dg.t[:, tap * 128:(tap + 1) * 128],
                         vpad.t[:, off: off + 512],
                         start=(tap == 0), stop=(tap == ntap - 1),
                         reads=[dg.buf, vpad.buf], writes=[ps.buf], inc=(tap == ntap - 1))
                k = fi[0] % 4
                fi[0] += 1
                v_copy(P, fob[k], ps, outsl=fob[k].t[:, 0:512], asl=ps[0:128, 0:512])
                P.dma("scalar", dsth.ap[pt * 128:(pt + 1) * 128, tc * 512:(tc + 1) * 512],
                      fob[k].t[:, 0:512], reads=[fob[k].buf], writes=[dsth.buf])
    A.pop()

    A.push()
    SstH = [[A.sbuf(f"a3_S{st}_{hh}", [128, 256], F32) for hh in range(4)] for st in range(2)]
    for st in range(2):
        for hh in range(4):
            v_memset(P, SstH[st][hh], 0.0)
    ktm_r = [A.sbuf(f"a3_ktm{i}", [128, 256], BF16) for i in range(2)]
    vbtm_r = [A.sbuf(f"a3_vbtm{i}", [128, 256], BF16) for i in range(2)]
    kbtm_r = [A.sbuf(f"a3_kbtm{i}", [128, 256], BF16) for i in range(2)]
    Amat_r = [A.sbuf(f"a3_A{i}", [128, 128], BF16) for i in range(2)]
    AmatT_r = [A.sbuf(f"a3_AT{i}", [128, 128], BF16) for i in range(2)]
    Bk_r = [A.sbuf(f"a3_Bk{i}", [128, 6 * 128], BF16) for i in range(2)]
    BkT_r = [A.sbuf(f"a3_BkT{i}", [128, 6 * 128], BF16) for i in range(2)]
    B1T_r = [A.sbuf(f"a3_B1T{i}", [128, 128], BF16) for i in range(2)]
    PpT_r = [A.sbuf(f"a3_PT{i}", [128, 128], BF16) for i in range(2)]
    utm_r = [A.sbuf(f"a3_u{i}", [128, 256], BF16) for i in range(2)]
    uadj_r = [A.sbuf(f"a3_ua{i}", [128, 256], BF16) for i in range(2)]
    otm_r = [A.sbuf(f"a3_otm{i}", [128, 256], BF16) for i in range(2)]
    attnT_r = [A.sbuf(f"a3_at{i}", [128, 128], BF16) for i in range(2)]
    pdev_r = [A.sbuf(f"a3_pdev{i}", [128, 128], BF16) for i in range(2)]
    wF_r = [A.sbuf(f"a3_wF{i}", [128, 256], F32) for i in range(2)]
    qf32_r = [A.sbuf(f"a3_qf32{i}", [128, 256], F32) for i in range(2)]
    ps_1 = A.psum("a3_ps1")
    ps_2 = A.psum("a3_ps2")
    ps_3 = A.psum("a3_ps3", BF16)
    ps_o = A.psum("a3_pso")
    ps_su = A.psum("a3_pssu")
    for c in range(16):
        for h in range(4):
            pp = (c * 4 + h) % 2
            ktm = ktm_r[pp]; vbtm = vbtm_r[pp]; kbtm = kbtm_r[pp]
            Amat = Amat_r[pp]; AmatT = AmatT_r[pp]; Bk = Bk_r[pp]; BkT = BkT_r[pp]
            B1T = B1T_r[pp]; PpT = PpT_r[pp]; utm = utm_r[pp]; uadj = uadj_r[pp]
            otm = otm_r[pp]; attnT = attnT_r[pp]; pdev = pdev_r[pp]
            wF = wF_r[pp]; qf32 = qf32_r[pp]
            bcol = beta_tm.t[:, c * 4 + h: c * 4 + h + 1]
            for pt in range(2):
                blk = h * 2 + pt
                t_tr(P, ps_3[0:128, 0:128], kn.t[:, blk * T + c * 128: blk * T + c * 128 + 128],
                     cs.ident_bf, reads=[kn.buf], writes=[ps_3.buf])
                v_copy(P, ktm, ps_3, outsl=ktm.t[:, pt * 128:(pt + 1) * 128], asl=ps_3[0:128, 0:128])
                t_tr(P, ps_3[0:128, 0:128],
                     vpad.t[:, blk * TP_ + TPAD + c * 128: blk * TP_ + TPAD + c * 128 + 128],
                     cs.ident_bf, reads=[vpad.buf], writes=[ps_3.buf])
                v_copy(P, vbtm, ps_3, outsl=vbtm.t[:, pt * 128:(pt + 1) * 128], asl=ps_3[0:128, 0:128])
            v_ts(P, kbtm, ktm, bcol, ALU.mult, sbufs=[beta_tm.buf])
            v_ts(P, vbtm, vbtm, bcol, ALU.mult, sbufs=[beta_tm.buf])
            for pt in range(2):
                blk = h * 2 + pt
                ksl = kn.t[:, blk * T + c * 128: blk * T + c * 128 + 128]
                t_mm(P, ps_1[0:128, 0:128], ksl, ksl, start=(pt == 0), stop=(pt == 1),
                     reads=[kn.buf], writes=[ps_1.buf], inc=(pt == 1))
            v_ts(P, Amat, ps_1, bcol, ALU.mult, -1.0, ALU.mult, asl=ps_1[0:128, 0:128],
                 sbufs=[beta_tm.buf])
            v_tt(P, Amat, Amat, stl, ALU.mult, bsl=stl.t[:, 513:641])
            t_tr(P, ps_3[0:128, 0:128], Amat.ap, cs.ident_bf, reads=[Amat.buf], writes=[ps_3.buf])
            v_copy(P, AmatT, ps_3, asl=ps_3[0:128, 0:128])
            for sq in range(6):
                lT = AmatT.ap if sq == 0 else BkT.t[:, (sq - 1) * 128: sq * 128]
                lv = Amat.ap if sq == 0 else Bk.t[:, (sq - 1) * 128: sq * 128]
                t_mm(P, ps_1[0:128, 0:128], lT, lv, start=True, stop=True,
                     reads=[Amat.buf, AmatT.buf, Bk.buf, BkT.buf], writes=[ps_1.buf], inc=True)
                v_copy(P, Bk, ps_1, outsl=Bk.t[:, sq * 128:(sq + 1) * 128], asl=ps_1[0:128, 0:128])
                t_mm(P, ps_2[0:128, 0:128], lv, lT, start=True, stop=True,
                     reads=[Amat.buf, AmatT.buf, Bk.buf, BkT.buf], writes=[ps_2.buf], inc=True)
                v_copy(P, BkT, ps_2, outsl=BkT.t[:, sq * 128:(sq + 1) * 128], asl=ps_2[0:128, 0:128])
            v_tt(P, B1T, AmatT, cs.ident_bf, ALU.add)
            for sq in range(6):
                v_tt(P, Bk, Bk, cs.ident_bf, ALU.add, outsl=Bk.t[:, sq * 128:(sq + 1) * 128],
                     asl=Bk.t[:, sq * 128:(sq + 1) * 128])
            v_copy(P, PpT, B1T)
            for sq in range(6):
                t_mm(P, ps_2[0:128, 0:128], Bk.t[:, sq * 128:(sq + 1) * 128], PpT.ap,
                     start=True, stop=True, reads=[PpT.buf, Bk.buf], writes=[ps_2.buf], inc=True)
                v_copy(P, PpT, ps_2, asl=ps_2[0:128, 0:128])
            t_mm(P, ps_1[0:128, 0:256], PpT.ap, vbtm.ap, start=True, stop=True,
                 reads=[PpT.buf, vbtm.buf], writes=[ps_1.buf], inc=True)
            v_copy(P, utm, ps_1, asl=ps_1[0:128, 0:256])
            for mt in range(2):
                t_mm(P, ps_2[0:128, mt * 128:(mt + 1) * 128],
                     kbtm.t[:, mt * 128:(mt + 1) * 128], PpT.ap,
                     start=(mt == 0), stop=(mt == 1),
                     reads=[kbtm.buf, PpT.buf], writes=[ps_2.buf], inc=(mt == 1))
            v_copy(P, wF, ps_2, asl=ps_2[0:128, 0:256])
            for st in range(2):
                t_mm(P, ps_1[0:128, 0:256], wF.t[:, st * 128:(st + 1) * 128],
                     SstH[st][h].t[:, 0:256],
                     start=(st == 0), stop=(st == 1),
                     reads=[wF.buf, SstH[st][h].buf], writes=[ps_1.buf], inc=(st == 1))
            v_tt(P, uadj, utm, ps_1, ALU.subtract, asl=utm.ap, bsl=ps_1[0:128, 0:256])
            for pt in range(2):
                blk = h * 2 + pt
                t_mm(P, ps_2[0:128, 0:128], kn.t[:, blk * T + c * 128: blk * T + c * 128 + 128],
                     qn.t[:, blk * T + c * 128: blk * T + c * 128 + 128],
                     start=(pt == 0), stop=(pt == 1),
                     reads=[kn.buf, qn.buf], writes=[ps_2.buf], inc=(pt == 1))
            v_copy(P, attnT, ps_2, asl=ps_2[0:128, 0:128])
            v_tt(P, attnT, attnT, cs.stair, ALU.mult, bsl=cs.stair.t[:, 512:640])
            for st in range(2):
                blk = h * 2 + st
                v_copy(P, qf32, qn, outsl=qf32.t[:, st * 128:(st + 1) * 128],
                       asl=qn.t[:, blk * T + c * 128: blk * T + c * 128 + 128])
            t_mm(P, ps_o[0:128, 0:256], attnT.ap, uadj.ap, start=True, stop=False,
                 reads=[attnT.buf, uadj.buf], writes=[ps_o.buf], inc=False)
            for st in range(2):
                t_mm(P, ps_o[0:128, 0:256], qf32.t[:, st * 128:(st + 1) * 128],
                     SstH[st][h].t[:, 0:256],
                     start=False, stop=(st == 1),
                     reads=[qf32.buf, SstH[st][h].buf], writes=[ps_o.buf], inc=(st == 1))
            v_copy(P, otm, ps_o, asl=ps_o[0:128, 0:256])
            for st in range(2):
                t_mm(P, ps_su[0:128, 0:256], ktm.t[:, st * 128:(st + 1) * 128], uadj.ap,
                     start=True, stop=True, reads=[ktm.buf, uadj.buf], writes=[ps_su.buf], inc=True)
                v_tt(P, SstH[st][h], SstH[st][h], ps_su, ALU.add,
                     outsl=SstH[st][h].t[:, 0:256],
                     asl=SstH[st][h].t[:, 0:256], bsl=ps_su[0:128, 0:256])
            for pt in range(2):
                blk = h * 2 + pt
                t_tr(P, ps_3[0:128, 0:128], otm.t[:, pt * 128:(pt + 1) * 128], cs.ident_bf,
                     reads=[otm.buf], writes=[ps_3.buf])
                v_copy(P, pdev, ps_3, outsl=pdev.t[:, 0:128], asl=ps_3[0:128, 0:128])
                P.dma("scalar", pd_hbm.ap[blk * 128:(blk + 1) * 128, c * 128:(c + 1) * 128],
                      pdev.t[:, 0:128], reads=[pdev.buf], writes=[pd_hbm.buf])
    A.pop()

    P.dma("sync", gi_h.ap[0:1024, 0:T].rearrange("(a p) t -> a p t", p=128),
          xn_h.ap.rearrange("(a p) t -> a p t", p=128), reads=[xn_h.buf], writes=[gi_h.buf])
    A.push()
    pblk = A.sbuf("a3_pblk", [128, 2 * T], BF16)
    sq2 = A.sbuf("a3_sq2", [128, 2 * T], BF16)
    ab2 = A.sbuf("a3_ab2", [128, 2 * T], BF16)
    statr = A.sbuf("a3_statr", [1, T], F32)
    statb = A.sbuf("a3_statb", [1, T], BF16)
    mrow = A.sbuf("a3_mrow", [1, T], F32)
    m2row = A.sbuf("a3_m2row", [1, T], F32)
    ps_s = [A.psum(f"a3_pss{i}") for i in range(3)]
    for pi in range(4):
        for h in range(4):
            if pi in (0, 1, 2):
                hsrc = ps_hbm if pi == 0 else (pl_hbm if pi == 1 else pd_hbm)
                for pt in range(2):
                    blk = h * 2 + pt
                    P.dma("sync", pblk.t[:, pt * T:(pt + 1) * T],
                          hsrc.ap[blk * 128:(blk + 1) * 128, 0:T],
                          reads=[hsrc.buf], writes=[pblk.buf])
            else:
                for pt in range(2):
                    blk = h * 2 + pt
                    v_copy(P, pblk, vpad, outsl=pblk.t[:, pt * T:(pt + 1) * T],
                           asl=vpad.t[:, blk * TP_ + TPAD: blk * TP_ + TPAD + T])
            for pt in range(2):
                s_act(P, sq2, pblk, AF.Square, outsl=sq2.t[:, pt * T:(pt + 1) * T],
                      asl=pblk.t[:, pt * T:(pt + 1) * T])
                s_act(P, ab2, pblk, AF.Abs, outsl=ab2.t[:, pt * T:(pt + 1) * T],
                      asl=pblk.t[:, pt * T:(pt + 1) * T])
            for tc in range(4):
                for pt in range(2):
                    co = pt * T + tc * 512
                    t_mm(P, ps_s[0][0:1, 0:512],
                         cs.ones_col.t[0:128, 0:1], pblk.t[:, co: co + 512],
                         start=(pt == 0), stop=(pt == 1),
                         reads=[pblk.buf, cs.ones_col.buf], writes=[ps_s[0].buf], inc=(pt == 1))
                    t_mm(P, ps_s[1][0:1, 0:512], cs.ones_col.t[0:128, 0:1], sq2.t[:, co: co + 512],
                         start=(pt == 0), stop=(pt == 1),
                         reads=[sq2.buf, cs.ones_col.buf], writes=[ps_s[1].buf], inc=(pt == 1))
                    t_mm(P, ps_s[2][0:1, 0:512], cs.ones_col.t[0:128, 0:1], ab2.t[:, co: co + 512],
                         start=(pt == 0), stop=(pt == 1),
                         reads=[ab2.buf, cs.ones_col.buf], writes=[ps_s[2].buf], inc=(pt == 1))
                tsl = slice(tc * 512, (tc + 1) * 512)
                v_ts(P, mrow, ps_s[0], 1.0 / 256, ALU.mult, outsl=mrow.t[0:1, tsl], asl=ps_s[0][0:1, 0:512])
                v_ts(P, statr, ps_s[1], 1.0 / 256, ALU.mult, outsl=statr.t[0:1, tsl], asl=ps_s[1][0:1, 0:512])
                v_tt(P, m2row, mrow, mrow, ALU.mult, outsl=m2row.t[0:1, tsl],
                     asl=mrow.t[0:1, tsl], bsl=mrow.t[0:1, tsl])
                v_tt(P, statr, statr, m2row, ALU.subtract, outsl=statr.t[0:1, tsl],
                     asl=statr.t[0:1, tsl], bsl=m2row.t[0:1, tsl])
            rbase = 1024 + pi * 16 + h * 4
            v_copy(P, statb, mrow)
            P.dma("scalar", gi_h.ap[rbase + 0: rbase + 1, 0:T], statb.t[0:1, 0:T],
                  reads=[statb.buf], writes=[gi_h.buf])
            v_copy(P, statb, statr)
            P.dma("scalar", gi_h.ap[rbase + 1: rbase + 2, 0:T], statb.t[0:1, 0:T],
                  reads=[statb.buf], writes=[gi_h.buf])
            for tc in range(4):
                for pt in range(2):
                    co = pt * T + tc * 512
                    t_mm(P, ps_s[2][0:1, 0:512], cs.ones_col.t[0:128, 0:1], ab2.t[:, co: co + 512],
                         start=(pt == 0), stop=(pt == 1),
                         reads=[ab2.buf, cs.ones_col.buf], writes=[ps_s[2].buf], inc=(pt == 1))
                    t_mm(P, ps_s[1][0:1, 0:512], cs.ones_col.t[0:128, 0:1], sq2.t[:, co: co + 512],
                         start=(pt == 0), stop=(pt == 1),
                         reads=[sq2.buf, cs.ones_col.buf], writes=[ps_s[1].buf], inc=(pt == 1))
                tsl = slice(tc * 512, (tc + 1) * 512)
                v_ts(P, mrow, ps_s[2], 1.0 / 256, ALU.mult, outsl=mrow.t[0:1, tsl], asl=ps_s[2][0:1, 0:512])
                s_act(P, statr, ps_s[1], AF.Ln, outsl=statr.t[0:1, tsl], asl=ps_s[1][0:1, 0:512])
                s_act(P, statr, statr, AF.Exp, scale=0.5, outsl=statr.t[0:1, tsl], asl=statr.t[0:1, tsl])
            v_copy(P, statb, mrow)
            P.dma("scalar", gi_h.ap[rbase + 2: rbase + 3, 0:T], statb.t[0:1, 0:T],
                  reads=[statb.buf], writes=[gi_h.buf])
            v_copy(P, statb, statr)
            P.dma("scalar", gi_h.ap[rbase + 3: rbase + 4, 0:T], statb.t[0:1, 0:T],
                  reads=[statb.buf], writes=[gi_h.buf])
    A.pop()

    A.push()
    sm4 = A.sbuf("a3_sm4", [4, 4], F32)
    sm4b = A.sbuf("a3_sm4b", [4, 4], F32)
    wk4 = A.sbuf("a3_wk4", [4, 3 * 4], F32)
    P.dma("sync", sm4.ap, ltemp.ap, reads=[ltemp.buf], writes=[sm4.buf])
    softplus32(P, C, sm4b, sm4b.t[0:4, 0:4], sm4, sm4.t[0:4, 0:4], wk4, pn=4, width=4)
    v_ts(P, sm4b, sm4b, 0.25, ALU.add)
    v_recip(P, sm4b, sm4b)
    P.dma("scalar", sc_h.ap[0:1, 0:16].rearrange("a (h p) -> (a h) p", h=4),
          sm4b.t[0:4, 0:4], reads=[sm4b.buf], writes=[sc_h.buf])
    P.dma("sync", sm4.ap, fpar.ap, reads=[fpar.buf], writes=[sm4.buf])
    s_act(P, sm4b, sm4, AF.Sigmoid)
    v_ts(P, sm4b, sm4b, 0.07, ALU.mult, 0.005, ALU.add)
    v_ts(P, sm4b, sm4b, 0.25, ALU.min)
    v_red(P, wk4, sm4b, ALU.add, outsl=wk4.t[0:4, 0:1], asl=sm4b.t[0:4, 0:4])
    v_ts(P, wk4, wk4, 0.99, ALU.min, outsl=wk4.t[0:4, 1:2], asl=wk4.t[0:4, 0:1])
    v_recip(P, wk4, wk4, outsl=wk4.t[0:4, 2:3], asl=wk4.t[0:4, 1:2])
    v_ts(P, sm4b, sm4b, wk4.t[0:4, 2:3], ALU.mult, sbufs=[wk4.buf])
    v_ts(P, sm4b, sm4b, 0.99, ALU.mult, 1e-9, ALU.add)
    P.dma("scalar", sc_h.ap[0:1, 16:32].rearrange("a (h p) -> (a h) p", h=4),
          sm4b.t[0:4, 0:4], reads=[sm4b.buf], writes=[sc_h.buf])
    A.pop()
    A.push()
    gwb = A.sbuf("a3_gwb", [128, 9 * D], BF16)
    for kt in range(8):
        P.dma("gpsimd", gwb.t[:, kt * D:(kt + 1) * D], gw1.ap[kt * 128:(kt + 1) * 128, 0:D],
              reads=[gw1.buf], writes=[gwb.buf])
    P.dma("gpsimd", gwb.t[0:64, 8 * D:9 * D], gw1.ap[1024:1088, 0:D], reads=[gw1.buf], writes=[gwb.buf])
    gb1s = A.sbuf("a3_gb1", [128, 8], F32)
    P.dma("sync", gb1s.ap, gb1.ap, reads=[gb1.buf], writes=[gb1s.buf])
    gxb = A.sbuf("a3_gxb", [128, 9 * 1024], BF16)
    h1g = A.sbuf("a3_h1g", [128, 8 * T], BF16)
    ps_gm = [A.psum(f"a3_psgm{i}") for i in range(4)]
    for sw in range(2):
        t0s = sw * 1024
        for kt in range(8):
            P.dma("sync", gxb.t[:, kt * 1024:(kt + 1) * 1024],
                  gi_h.ap[kt * 128:(kt + 1) * 128, t0s:t0s + 1024],
                  reads=[gi_h.buf], writes=[gxb.buf])
        P.dma("sync", gxb.t[0:64, 8 * 1024:9 * 1024], gi_h.ap[1024:1088, t0s:t0s + 1024],
              reads=[gi_h.buf], writes=[gxb.buf])
        for ng in range(4):
            grp = [(ng * 2 + nt, tci, ps_gm[nt * 2 + tci]) for nt in range(2) for tci in range(2)]
            for kt in range(9):
                for gi_, (nt, tci, ps) in enumerate(grp):
                    lh = gwb.t[:, kt * D + nt * 128: kt * D + nt * 128 + 128] if kt < 8 else \
                         gwb.t[0:64, kt * D + nt * 128: kt * D + nt * 128 + 128]
                    rh = gxb.t[:, kt * 1024 + tci * 512: kt * 1024 + tci * 512 + 512] if kt < 8 else \
                         gxb.t[0:64, kt * 1024 + tci * 512: kt * 1024 + tci * 512 + 512]
                    t_mm(P, ps[0:128, 0:512], lh, rh,
                         start=(kt == 0), stop=(kt == 8),
                         reads=[gwb.buf, gxb.buf], writes=[ps.buf],
                         inc=(kt == 8 and gi_ == len(grp) - 1))
            for (nt, tci, ps) in grp:
                t0 = t0s + tci * 512
                s_act(P, h1g, ps, AF.Gelu, bias=gb1s.t[:, nt: nt + 1],
                      outsl=h1g.t[:, nt * T + t0: nt * T + t0 + 512],
                      asl=ps[0:128, 0:512], sbufs=[gb1s.buf])
    gw2b = A.sbuf("a3_gw2b", [128, 8 * 16], BF16)
    for kt in range(8):
        P.dma("gpsimd", gw2b.t[:, kt * 16:(kt + 1) * 16], gw2.ap[kt * 128:(kt + 1) * 128, 0:16],
              reads=[gw2.buf], writes=[gw2b.buf])
    gb2s = A.sbuf("a3_gb2", [16, 1], F32)
    P.dma("sync", gb2s.ap, gb2.ap, reads=[gb2.buf], writes=[gb2s.buf])
    logits = A.sbuf("a3_lg", [16, T], F32)
    for tc in range(4):
        ps = ps_gm[tc % 2]
        for kt in range(8):
            t_mm(P, ps[0:16, 0:512], gw2b.t[:, kt * 16:(kt + 1) * 16],
                 h1g.t[:, kt * T + tc * 512: kt * T + tc * 512 + 512],
                 start=(kt == 0), stop=(kt == 7),
                 reads=[gw2b.buf, h1g.buf], writes=[ps.buf], inc=(kt == 7))
        v_ts(P, logits, ps, gb2s.t[0:16, 0:1], ALU.add, outsl=logits.t[0:16, tc * 512:(tc + 1) * 512],
             asl=ps[0:16, 0:512], sbufs=[gb2s.buf])
    taucol = A.sbuf("a3_tau", [16, 1], F32)
    P.dma("sync", taucol.ap, sc_h.ap[0:1, 0:16].rearrange("a b -> b a"), reads=[sc_h.buf], writes=[taucol.buf])
    v_ts(P, logits, logits, taucol.t[0:16, 0:1], ALU.mult, sbufs=[taucol.buf])
    nfrow = A.sbuf("a3_nf", [1, 16], F32)
    P.dma("sync", nfrow.ap, sc_h.ap[0:1, 16:32], reads=[sc_h.buf], writes=[nfrow.buf])
    nfb = A.sbuf("a3_nfb", [128, 16], F32)
    ps_t1 = A.psum("a3_pst1")
    bcast_row(P, C, ps_t1, nfrow, nfrow.t[0:1, 0:16], 16, cs.ones_row_f)
    v_copy(P, nfb, ps_t1, asl=ps_t1[0:128, 0:16])
    lgtm = A.sbuf("a3_lgtm", [128, 16], F32)
    cols = A.sbuf("a3_cols", [128, 8], F32)
    for tt in range(16):
        t_tr(P, ps_t1[0:128, 0:16], logits.t[0:16, tt * 128:(tt + 1) * 128], cs.ident_f,
             reads=[logits.buf], writes=[ps_t1.buf], pn=16)
        v_copy(P, lgtm, ps_t1, asl=ps_t1[0:128, 0:16])
        for h in range(4):
            hsl = slice(h * 4, (h + 1) * 4)
            v_red(P, cols, lgtm, ALU.max, outsl=cols.t[:, h: h + 1], asl=lgtm.t[:, hsl], negate=True)
            s_act(P, lgtm, lgtm, AF.Exp, bias=cols.t[:, h: h + 1], outsl=lgtm.t[:, hsl],
                  asl=lgtm.t[:, hsl], sbufs=[cols.buf])
            v_red(P, cols, lgtm, ALU.add, outsl=cols.t[:, 4 + h: 5 + h], asl=lgtm.t[:, hsl])
            v_recip(P, cols, cols, outsl=cols.t[:, 4 + h: 5 + h], asl=cols.t[:, 4 + h: 5 + h])
            v_ts(P, lgtm, lgtm, cols.t[:, 4 + h: 5 + h], ALU.mult, outsl=lgtm.t[:, hsl],
                 asl=lgtm.t[:, hsl], sbufs=[cols.buf])
        v_tt(P, lgtm, lgtm, nfb, ALU.max)
        for h in range(4):
            hsl = slice(h * 4, (h + 1) * 4)
            v_red(P, cols, lgtm, ALU.add, outsl=cols.t[:, h: h + 1], asl=lgtm.t[:, hsl])
            v_ts(P, cols, cols, 1e-8, ALU.add, outsl=cols.t[:, h: h + 1], asl=cols.t[:, h: h + 1])
            v_recip(P, cols, cols, outsl=cols.t[:, h: h + 1], asl=cols.t[:, h: h + 1])
            v_ts(P, lgtm, lgtm, cols.t[:, h: h + 1], ALU.mult, outsl=lgtm.t[:, hsl],
                 asl=lgtm.t[:, hsl], sbufs=[cols.buf])
        t_tr(P, ps_t1[0:16, 0:128], lgtm.ap, cs.ident_f, reads=[lgtm.buf], writes=[ps_t1.buf])
        v_copy(P, wts, ps_t1, outsl=wts.t[0:16, tt * 128:(tt + 1) * 128], asl=ps_t1[0:16, 0:128])
    A.pop()

    A.push()
    sel_sb = A.sbuf("a3_sel", [16, 16 * 128], F32)
    P.dma("sync", sel_sb.ap, sel16.ap, reads=[sel16.buf], writes=[sel_sb.buf])
    onws = A.sbuf("a3_onw", [128, 2], F32)
    P.dma("sync", onws.ap, onw.ap, reads=[onw.buf], writes=[onws.buf])
    obuf = A.sbuf("a3_obuf", [128, T], F32)
    wb_t = A.sbuf("a3_wbt", [128, T], F32)
    ptmp = A.sbuf("a3_ptmp", [128, T], BF16)
    rtmp = A.sbuf("a3_rtmp", [128, 512], F32)
    osq = A.sbuf("a3_osq", [128, 2 * T], BF16)
    obf = A.sbuf("a3_obf", [128, 8 * T], BF16)
    rrow = A.sbuf("a3_rr", [1, T], F32)
    ps_w = A.psum("a3_psw")
    ps_q2 = A.psum("a3_psq2")
    ps_b2 = A.psum("a3_psb2")
    for h in range(4):
        for pt in range(2):
            blk = h * 2 + pt
            first = True
            for pi in range(4):
                for tc in range(4):
                    t_mm(P, ps_w[0:128, 0:512], sel_sb.t[0:16, (h * 4 + pi) * 128:(h * 4 + pi) * 128 + 128],
                         wts.t[0:16, tc * 512:(tc + 1) * 512], start=True, stop=True,
                         reads=[sel_sb.buf, wts.buf], writes=[ps_w.buf], inc=True)
                    v_copy(P, wb_t, ps_w, outsl=wb_t.t[:, tc * 512:(tc + 1) * 512], asl=ps_w[0:128, 0:512])
                if pi in (0, 1, 2):
                    hsrc = ps_hbm if pi == 0 else (pl_hbm if pi == 1 else pd_hbm)
                    P.dma("sync", ptmp.ap, hsrc.ap[blk * 128:(blk + 1) * 128, 0:T],
                          reads=[hsrc.buf], writes=[ptmp.buf])
                    psl = ptmp.t[:, 0:T]
                    pbuf = ptmp
                else:
                    psl = vpad.t[:, blk * TP_ + TPAD: blk * TP_ + TPAD + T]
                    pbuf = vpad
                if first:
                    v_tt(P, obuf, wb_t, pbuf, ALU.mult, asl=wb_t.t[:, 0:T], bsl=psl)
                    first = False
                else:
                    v_tt(P, wb_t, wb_t, pbuf, ALU.mult, asl=wb_t.t[:, 0:T], bsl=psl)
                    v_tt(P, obuf, obuf, wb_t, ALU.add)
            s_act(P, osq, obuf, AF.Square, outsl=osq.t[:, pt * T:(pt + 1) * T])
            v_copy(P, obf, obuf, outsl=obf.t[:, blk * T:(blk + 1) * T])
        for tc in range(4):
            for pt in range(2):
                t_mm(P, ps_q2[0:1, 0:512], cs.ones_col.t[0:128, 0:1],
                     osq.t[:, pt * T + tc * 512: pt * T + tc * 512 + 512],
                     start=(pt == 0), stop=(pt == 1),
                     reads=[osq.buf, cs.ones_col.buf], writes=[ps_q2.buf], inc=(pt == 1))
            v_ts(P, rrow, ps_q2, 1.0 / 256, ALU.mult, outsl=rrow.t[0:1, tc * 512:(tc + 1) * 512],
                 asl=ps_q2[0:1, 0:512])
            s_act(P, rrow, rrow, AF.Ln, bias=cfc(C, 1e-5)[0:1, :],
                  outsl=rrow.t[0:1, tc * 512:(tc + 1) * 512],
                  asl=rrow.t[0:1, tc * 512:(tc + 1) * 512], sbufs=[C.fc_sb.buf])
            s_act(P, rrow, rrow, AF.Exp, scale=-0.5,
                  outsl=rrow.t[0:1, tc * 512:(tc + 1) * 512],
                  asl=rrow.t[0:1, tc * 512:(tc + 1) * 512])
            bcast_row(P, C, ps_b2, rrow, rrow.t[0:1, tc * 512:(tc + 1) * 512], 512, cs.ones_row_f)
            v_copy(P, rtmp, ps_b2, outsl=rtmp.t[:, 0:512], asl=ps_b2[0:128, 0:512])
            for pt in range(2):
                blk = h * 2 + pt
                sl3 = slice(blk * T + tc * 512, blk * T + tc * 512 + 512)
                v_tt(P, obf, obf, rtmp, ALU.mult, outsl=obf.t[:, sl3], asl=obf.t[:, sl3],
                     bsl=rtmp.t[:, 0:512])
                v_ts(P, obf, obf, onws.t[:, pt: pt + 1], ALU.mult, outsl=obf.t[:, sl3],
                     asl=obf.t[:, sl3], sbufs=[onws.buf])
    A.pop()

    A.push()
    wob = A.sbuf("a3_wob", [128, 8 * D], BF16)
    for kt in range(8):
        P.dma("gpsimd", wob.t[:, kt * D:(kt + 1) * D], wo.ap[kt * 128:(kt + 1) * 128, 0:D],
              reads=[wo.buf], writes=[wob.buf])
    xt = [A.sbuf(f"a3_xt{i}", [128, 512], F32) for i in range(4)]
    xo = [A.sbuf(f"a3_xo{i}", [128, 512], F32) for i in range(4)]
    ps_wo = [A.psum(f"a3_pswo{i}") for i in range(4)]
    di = [0]
    for ng in range(4):
        for tcq in range(2):
            grp = [(ng * 2 + nt, tcq * 2 + tci, ps_wo[nt * 2 + tci]) for nt in range(2) for tci in range(2)]
            for kt in range(8):
                for gi_, (nt, tci, ps) in enumerate(grp):
                    t_mm(P, ps[0:128, 0:512], wob.t[:, kt * D + nt * 128: kt * D + nt * 128 + 128],
                         obf.t[:, kt * T + tci * 512: kt * T + tci * 512 + 512],
                         start=(kt == 0), stop=(kt == 7),
                         reads=[wob.buf, obf.buf], writes=[ps.buf],
                         inc=(kt == 7 and gi_ == len(grp) - 1))
            for ki_, (nt, tci, ps) in enumerate(grp):
                k = di[0] % 4
                di[0] += 1
                t0 = tci * 512
                P.dma("sync", xt[k].t[:, 0:512], x_fm.ap[nt * 128:(nt + 1) * 128, t0:t0 + 512],
                      reads=[x_fm.buf], writes=[xt[k].buf])
                v_tt(P, xo[k], ps, xt[k], ALU.add, outsl=xo[k].t[:, 0:512],
                     asl=ps[0:128, 0:512], bsl=xt[k].t[:, 0:512])
                P.dma("scalar", x_fm.ap[nt * 128:(nt + 1) * 128, t0:t0 + 512], xo[k].t[:, 0:512],
                      reads=[xo[k].buf], writes=[x_fm.buf])
    A.pop()
    A.pop()




def phase_final(C, cs):
    P, A = C.P, C.A
    A.push()
    x_fm = C.hint("x_fm", (D, T), F32)
    y = C.hout("y", (T, D), F32)
    x_sb = load_fm(C, "fin_x", x_fm, D, T, F32)
    out_t = [A.sbuf(f"fin_out{i}", [128, D], F32) for i in range(2)]
    ps = [A.psum(f"fin_ps{i}") for i in range(4)]
    for tt in range(16):
        ob = out_t[tt % 2]
        for dt in range(8):
            p = ps[dt % 4]
            t_tr(P, p[0:128, 0:128], x_sb.t[:, dt * T + tt * 128: dt * T + tt * 128 + 128],
                 cs.ident_f, reads=[x_sb.buf], writes=[p.buf])
            v_copy(P, ob, p, outsl=ob.t[:, dt * 128:(dt + 1) * 128], asl=p[0:128, 0:128])
        P.dma("scalar", y.ap[tt * 128:(tt + 1) * 128, 0:D], ob.ap, reads=[ob.buf], writes=[y.buf])
    A.pop()



PHASE_FNS = {}

def build_launch(parts, hostw):
    nc = bass.Bass("TRN2", target_bir_lowering=False, debug=False)
    C = Ctx(nc)
    sems = {}
    import contextlib
    stack = contextlib.ExitStack()
    semnames = ["semt", "semv", "sems", "semg"] + [f"{q}{i}" for q in DSEM.values() for i in range(NQSEM)]
    for s in semnames:
        sems[s] = stack.enter_context(nc.semaphore(s))
    C.A.push()
    cs = load_consts(C)
    for part in parts:
        emit_part(C, cs, part)
    C.P.barrier()
    C.A.pop()
    C.P.build_block(sems)
    needed = list(C.inputs.keys())
    stack.close()
    return nc, needed


def emit_part(C, cs, part):
    if part == "prep":
        phase_prep(C, cs)
    elif part == "final":
        phase_final(C, cs)
    elif part == "xfm_in":
        xin = C.hin("xfm_ext", (D, T), F32)
        x_fm = C.hint("x_fm", (D, T), F32)
        C.P.dma("sync", x_fm.ap, xin.ap, reads=[xin.buf], writes=[x_fm.buf])
    elif part == "xfm_out":
        xo = C.hout("xfm_ext_out", (D, T), F32)
        x_fm = C.hint("x_fm", (D, T), F32)
        C.P.dma("sync", xo.ap, x_fm.ap, reads=[x_fm.buf], writes=[xo.buf])
    elif part == "ln":
        phase_ln(C, cs)
    elif part.startswith("ffn:"):
        phase_ffn(C, cs, int(part.split(":")[1]))
    elif part == "attn2":
        phase_attn2(C, cs)
    elif part == "attn0":
        phase_attn0(C, cs)
    elif part == "ffn3":
        phase_ffn3(C, cs)
    elif part == "moe":
        phase_moe(C, cs)
    elif part == "ssm":
        phase_ssm(C, cs)
    elif part == "attn3":
        phase_attn3(C, cs)
    elif part in PHASE_FNS:
        PHASE_FNS[part](C, cs)
    else:
        raise ValueError(f"unknown part {part}")



def prep_host_weights(inp):
    w = {}
    cc = np.ascontiguousarray
    for L in (0, 2):
        pre = f"L{L}_ffn"
        if f"{pre}_w1_w" not in inp:
            continue
        w[f"{pre}_w1T"] = cc(inp[f"{pre}_w1_w"].T)
        w[f"{pre}_w2T"] = cc(inp[f"{pre}_w2_w"].T)
        w[f"{pre}_h1T"] = cc(inp[f"{pre}_h1_w"].T)
        w[f"{pre}_h2T"] = cc(inp[f"{pre}_h2_w"].T)
        w[f"{pre}_w3T"] = cc(inp[f"{pre}_w3_w"].T)
        sc = np.zeros((128, 32 * 4 + 8), dtype=np.float32)
        sc[:, 0:32] = inp[f"{pre}_w1_s"].reshape(32, 128).T
        sc[:, 32:64] = inp[f"{pre}_w2_s"].reshape(32, 128).T
        sc[:, 64:96] = inp[f"{pre}_h1_s"].reshape(32, 128).T
        sc[:, 96:128] = inp[f"{pre}_h2_s"].reshape(32, 128).T
        sc[:, 128:136] = inp[f"{pre}_w3_s"].reshape(8, 128).T
        w[f"{pre}_scales"] = sc
    if "L2_attn_bo" in inp:
        w["L2_attn_bo_fm"] = cc(inp["L2_attn_bo"].reshape(8, 128).T)
    if "L0_attn_rms_scale" in inp:
        w["L0_attn_rms_fm"] = cc(inp["L0_attn_rms_scale"].reshape(2, 128).T)
    if "L3_ffn_W_r" in inp:
        w["L3_ffn_W_rT"] = cc(inp["L3_ffn_W_r"].T)
        w["L3_ffn_W_kT"] = cc(inp["L3_ffn_W_k"].T)
        w["L3_ffn_W_vT"] = cc(inp["L3_ffn_W_v"].T)
        muf = np.zeros((128, 16), dtype=np.float32)
        muf[:, 0:8] = inp["L3_ffn_mu_r"].reshape(8, 128).T
        muf[:, 8:16] = inp["L3_ffn_mu_k"].reshape(8, 128).T
        w["L3_ffn_mu_fm"] = muf
    if "L1_attn_w_in" in inp:
        wt = inp["L1_attn_w_in"].T
        pad = np.zeros((D, 4736), dtype=np.float32)
        pad[:, 0:4096] = wt[:, 0:4096]
        pad[:, 4096:4128] = wt[:, 4096:4128]
        pad[:, 4128:4160] = wt[:, 4224:4256]
        pad[:, 4224:4320] = wt[:, 4128:4224]
        pad[:, 4352:4448] = wt[:, 4256:4352]
        pad[:, 4480:4576] = wt[:, 4352:4448]
        pad[:, 4608:4640] = wt[:, 4448:4480]
        w["L1_ssm_winT"] = pad
        w["L1_ssm_woutT"] = cc(inp["L1_attn_w_out"].T)
        w["L1_ssm_dtb"] = cc(inp["L1_attn_dt_bias"].reshape(32, 1))
        w["L1_ssm_D"] = cc(inp["L1_attn_D"].reshape(32, 1))
    if "L3_attn_Wq" in inp:
        w["L3_wqT"] = cc(inp["L3_attn_Wq"].T)
        w["L3_wkT"] = cc(inp["L3_attn_Wk"].T)
        w["L3_wvT"] = cc(inp["L3_attn_Wv"].T)
        w["L3_wbT"] = cc(inp["L3_attn_Wb"].T)
        w["L3_woT"] = cc(inp["L3_attn_Wo"].T)
        cw = np.zeros((128, 8 * 12), dtype=np.float32)
        for si, nm in enumerate(["conv_q_w", "conv_k_w", "conv_v_w"]):
            a = inp[f"L3_attn_{nm}"].reshape(8, 128, 4)
            for pt in range(8):
                for tap in range(4):
                    cw[:, pt * 12 + si * 4 + tap] = a[pt, :, tap]
        w["L3_conv_fm"] = cw
        fs = inp["L3_attn_fir_short_w"].reshape(8, 128, 5)
        fl = inp["L3_attn_fir_long_w"].reshape(8, 128, 63)
        fsw = np.zeros((128, 8 * 5), dtype=np.float32)
        flw = np.zeros((128, 8 * 63), dtype=np.float32)
        for pt in range(8):
            fsw[:, pt * 5:(pt + 1) * 5] = fs[pt]
            flw[:, pt * 63:(pt + 1) * 63] = fl[pt]
        w["L3_firs_fm"] = fsw
        w["L3_firl_fm"] = flw
        w["L3_gw1T"] = cc(inp["L3_attn_gate_W1"].T)
        w["L3_gb1_fm"] = cc(inp["L3_attn_gate_b1"].reshape(8, 128).T)
        w["L3_gw2T"] = cc(inp["L3_attn_gate_W2"].T)
        w["L3_gb2_col"] = cc(inp["L3_attn_gate_b2"].reshape(16, 1))
        w["L3_onw_fm"] = cc(inp["L3_attn_o_norm_w"].reshape(2, 128).T)
    if "L1_ffn_gate_w" in inp:
        w["L1_moe_gateT"] = cc(inp["L1_ffn_gate_w"].T)
        for e in range(8):
            w13t = inp["L1_ffn_w13_weight"][e].T
            w[f"L1_moe_w13T_{e}"] = cc(w13t.reshape(D, 2, 16, 128).transpose(0, 2, 1, 3).reshape(D, 4096))
            w[f"L1_moe_w2T_{e}"] = cc(inp["L1_ffn_w2_weight"][e].T)
    w.update(host_consts())
    return w


TEST_PLAN = None

def launch_plan():
    if TEST_PLAN is not None:
        plans = [list(p) for p in TEST_PLAN]
    else:
        l0 = ["prep", "ln", "attn0", "ffn:0"]
        l1 = ["ln", "ssm", "moe"]
        l2 = ["ln", "attn2", "ffn:2"]
        l3 = ["ln", "attn3", "ffn3", "final"]
        if SINGLE_LAUNCH:
            plans = [l0 + l1 + l2 + l3]
        else:
            plans = [l0, l1, l2, l3]
    for i, p in enumerate(plans):
        if i > 0:
            p.insert(0, "xfm_in")
        if i < len(plans) - 1:
            p.append("xfm_out")
    return plans


_launch_cache = {}

def kernel(**inputs):
    hostw = prep_host_weights(inputs)
    x = np.ascontiguousarray(inputs["L0_attn_X"]).astype(np.float32, copy=False)
    plans = launch_plan()
    per_core_x = [np.ascontiguousarray(x[c]) for c in range(NCORE)]
    carry = None
    out = None
    for li, parts in enumerate(plans):
        key = (li, tuple(parts))
        if key not in _launch_cache:
            _launch_cache[key] = build_launch(tuple(parts), hostw)
        nc, needed = _launch_cache[key]
        hostw["c_fconst"] = make_fconst()
        in_maps = []
        for c in range(NCORE):
            m = {}
            for name in needed:
                if name == "x_in":
                    m[name] = per_core_x[c]
                elif name == "xfm_ext":
                    m[name] = carry[c]
                elif name in hostw:
                    m[name] = hostw[name]
                else:
                    m[name] = np.ascontiguousarray(inputs[name])
            in_maps.append(m)
        res = bass_utils.run_bass_kernel_spmd(nc, in_maps, core_ids=list(range(NCORE)))
        if "xfm_ext_out" in res.results[0]:
            carry = [res.results[c]["xfm_ext_out"] for c in range(NCORE)]
        if "y" in res.results[0]:
            out = np.stack([res.results[c]["y"] for c in range(NCORE)], axis=0)
    return out


# version 13 (final)
# speedup vs baseline: 1.0322x; 1.0322x over previous
import numpy as np
import ml_dtypes
import concourse.bass as bass
import concourse.mybir as mybir
from concourse import bass_utils

F32 = mybir.dt.float32
BF16 = mybir.dt.bfloat16
AF = mybir.ActivationFunctionType
ALU = mybir.AluOpType
AX = mybir.AxisListType

BF = ml_dtypes.bfloat16
T = 2048
D = 1024
NCORE = 8

SINGLE_LAUNCH = True


class Buf:
    __slots__ = ("name", "writers", "readers")
    def __init__(self, name):
        self.name = name
        self.writers = {}
        self.readers = {}

ENGINES = ("sync", "scalar", "vector", "gpsimd", "tensor")
CSEM = {"tensor": "semt", "vector": "semv", "scalar": "sems", "gpsimd": "semg"}
DSEM = {"sync": "qs", "scalar": "qa", "gpsimd": "qg"}
NQSEM = 12

class Prog:
    def __init__(self, nc):
        self.nc = nc
        self.streams = {e: [] for e in ENGINES}
        self.cnt = {e: 0 for e in CSEM}
        self.dcnt = {q: 0 for q in DSEM}
        self.qcnt = {f"{DSEM[q]}{i}": 0 for q in DSEM for i in range(NQSEM)}
        self.waited = {e: {} for e in ENGINES}
        self.pending = {e: [] for e in CSEM}
        self.names = {}

    def uname(self, base):
        i = self.names.get(base, 0)
        self.names[base] = i + 1
        return f"{base}_{i}"

    def _wait(self, stream, sem, val):
        if val <= 0:
            return
        if stream == "tensor" and sem == CSEM["tensor"]:
            return
        if self.waited[stream].get(sem, 0) >= val:
            return
        self.waited[stream][sem] = val
        self.streams[stream].append(("w", sem, val))

    def _dep(self, stream, writers):
        for sem, val in writers.items():
            self._wait(stream, sem, val)

    def op(self, stream, emit, reads=(), writes=(), inc=True):
        sem = CSEM[stream]
        for b in reads:
            self._dep(stream, b.writers)
        for b in writes:
            self._dep(stream, b.writers)
            for rs, rv in list(b.readers.items()):
                self._wait(stream, rs, rv)
        nv = self.cnt[stream] + 1
        for b in reads:
            b.readers[sem] = max(b.readers.get(sem, 0), nv)
        self.pending[stream].extend(writes)
        if inc == "mark":
            self.cnt[stream] = nv
            self.streams[stream].append(("i", emit, sem))
        elif inc:
            self.cnt[stream] = nv
            self.streams[stream].append(("i", emit, sem))
            for b in self.pending[stream]:
                b.writers = {sem: nv}
                b.readers = {}
            self.pending[stream] = []
        else:
            self.streams[stream].append(("e", emit))

    def dma(self, queue, out_ap, in_ap, reads=(), writes=(), **kw):
        qbase = DSEM[queue]
        k = self.dcnt[queue] % NQSEM
        qsem = f"{qbase}{k}"
        qprefix = qbase
        for b in reads:
            self._dep(queue, b.writers)
        for b in writes:
            self._dep(queue, {s: v for s, v in b.writers.items() if not s.startswith(qprefix)})
            for rs, rv in list(b.readers.items()):
                self._wait(queue, rs, rv)
        self._wait(queue, qsem, self.qcnt[qsem] * 16)
        nv = (self.qcnt[qsem] + 1) * 16
        for b in reads:
            b.readers[qsem] = max(b.readers.get(qsem, 0), nv)
        for b in writes:
            b.writers[qsem] = nv
            b.readers = {}
        self.qcnt[qsem] += 1
        self.dcnt[queue] += 1
        self.streams[queue].append(("d", out_ap, in_ap, qsem, kw))

    def barrier(self):
        tgt = {CSEM[e]: self.cnt[e] for e in self.cnt}
        tgt.update({s: c * 16 for s, c in self.qcnt.items()})
        for e in ENGINES:
            for sem, val in tgt.items():
                if e == "tensor" and sem == CSEM["tensor"]:
                    continue
                self._wait(e, sem, val)

    def build_block(self, sems):
        nc = self.nc
        blk_cm = nc.Block()
        block = blk_cm.__enter__()
        def make_runner(items):
            def run(eng):
                for it in items:
                    tag = it[0]
                    if tag == "w":
                        eng.wait_ge(sems[it[1]], it[2])
                    elif tag == "i":
                        ins = it[1](eng)
                        ins.then_inc(sems[it[2]], 1)
                    elif tag == "e":
                        it[1](eng)
                    elif tag == "d":
                        _, out_ap, in_ap, qsem, kw = it
                        eng.dma_start(out=out_ap, in_=in_ap, **kw).then_inc(sems[qsem], 16)
            return run
        for name in ENGINES:
            getattr(block, name)(make_runner(self.streams[name]))
        blk_cm.__exit__(None, None, None)


class SB:
    __slots__ = ("t", "buf")
    def __init__(self, t, name):
        self.t = t
        self.buf = Buf(name)
    def __getitem__(self, idx):
        return self.t[idx]
    @property
    def ap(self):
        return self.t[:]


class Arena:
    def __init__(self, nc, P):
        self.nc = nc
        self.P = P
        self.frames = []
        self.caches = []

    def push(self):
        self.frames.append([])
        self.caches.append({})

    def pop(self):
        self.P.barrier()
        self.caches.pop()
        for cm in reversed(self.frames.pop()):
            cm.__exit__(None, None, None)

    def sbuf(self, name, shape, dtype):
        key = ("s", name, tuple(shape), dtype)
        if key in self.caches[-1]:
            return self.caches[-1][key]
        uname = self.P.uname(name)
        cm = self.nc.sbuf_tensor(uname, shape, dtype)
        t = cm.__enter__()
        self.frames[-1].append(cm)
        sb = SB(t, uname)
        self.caches[-1][key] = sb
        return sb

    def psum(self, name, dtype=F32):
        shape = [128, 512] if dtype == F32 else [128, 1024]
        key = ("p", name, dtype)
        if key in self.caches[-1]:
            return self.caches[-1][key]
        uname = self.P.uname(name)
        cm = self.nc.psum_tensor(uname, shape, dtype)
        t = cm.__enter__()
        self.frames[-1].append(cm)
        sb = SB(t, uname)
        self.caches[-1][key] = sb
        return sb


class HT:
    __slots__ = ("ap", "buf", "name")
    def __init__(self, ap, name, written=False):
        self.ap = ap
        self.buf = Buf(name)
        self.name = name


FCONST_VALS = {}

def fc_idx(val):
    val = float(val)
    if val not in FCONST_VALS:
        FCONST_VALS[val] = len(FCONST_VALS)
        assert len(FCONST_VALS) <= 128
    return FCONST_VALS[val]

def make_fconst():
    arr = np.zeros((128, 128), dtype=np.float32)
    for v, i in FCONST_VALS.items():
        arr[:, i] = v
    return arr

def cfc(C, val):
    return C.fc_sb.t[:, fc_idx(val):fc_idx(val) + 1]


class Ctx:
    def __init__(self, nc):
        self.nc = nc
        self.P = Prog(nc)
        self.A = Arena(nc, self.P)
        self.inputs = {}
        self.outputs = {}
        self.internal = {}
        self.fc_sb = None

    def hin(self, name, shape, dtype=F32):
        if name not in self.inputs:
            ap = self.nc.dram_tensor(name, list(shape), dtype, kind="ExternalInput").ap()
            self.inputs[name] = HT(ap, name)
        return self.inputs[name]

    def hout(self, name, shape, dtype=F32):
        if name not in self.outputs:
            ap = self.nc.dram_tensor(name, list(shape), dtype, kind="ExternalOutput").ap()
            self.outputs[name] = HT(ap, name)
        return self.outputs[name]

    def hint(self, name, shape, dtype):
        if name not in self.internal:
            ap = self.nc.dram_tensor(name, list(shape), dtype, kind="Internal").ap()
            self.internal[name] = HT(ap, name)
        return self.internal[name]



def v_tt(P, out, a, b, op, outsl=None, asl=None, bsl=None, eng="vector", inc=True, extra_reads=()):
    oap = outsl if outsl is not None else out.ap
    aap = asl if asl is not None else a.ap
    bap = bsl if bsl is not None else b.ap
    P.op(eng, lambda e, oap=oap, aap=aap, bap=bap, op=op: e.tensor_tensor(oap, aap, bap, op),
         reads=[a.buf, b.buf, *extra_reads], writes=[out.buf], inc=inc)

def v_ts(P, out, a, s1, op0, s2=None, op1=ALU.bypass, outsl=None, asl=None, sbufs=(), eng="vector", accum=None, accum_buf=None):
    oap = outsl if outsl is not None else out.ap
    aap = asl if asl is not None else a.ap
    writes = [out.buf] + ([accum_buf] if accum_buf is not None else [])
    P.op(eng, lambda e, oap=oap, aap=aap: e.tensor_scalar(oap, aap, s1, s2, op0, op1, accum_out=accum),
         reads=[a.buf, *sbufs], writes=writes)

def v_stt(P, out, in0, scalar, in1, op0, op1, outsl=None, i0sl=None, i1sl=None, sbufs=(), eng="vector"):
    oap = outsl if outsl is not None else out.ap
    a0 = i0sl if i0sl is not None else in0.ap
    a1 = i1sl if i1sl is not None else in1.ap
    P.op(eng, lambda e, oap=oap, a0=a0, a1=a1: e.scalar_tensor_tensor(oap, a0, scalar, a1, op0, op1),
         reads=[in0.buf, in1.buf, *sbufs], writes=[out.buf])

def v_copy(P, out, a, outsl=None, asl=None, eng="vector"):
    oap = outsl if outsl is not None else out.ap
    aap = asl if asl is not None else a.ap
    P.op(eng, lambda e, oap=oap, aap=aap: e.tensor_copy(oap, aap),
         reads=[a.buf], writes=[out.buf])

def s_act(P, out, a, func, bias=0.0, scale=1.0, outsl=None, asl=None, sbufs=(), accum=None, accum_buf=None):
    oap = outsl if outsl is not None else out.ap
    aap = asl if asl is not None else a.ap
    writes = [out.buf] + ([accum_buf] if accum_buf is not None else [])
    P.op("scalar", lambda e, oap=oap, aap=aap: e.activation(oap, aap, func, bias=bias, scale=scale, accum_out=accum),
         reads=[a.buf, *sbufs], writes=writes)

def v_memset(P, out, val, outsl=None, eng="vector"):
    oap = outsl if outsl is not None else out.ap
    P.op(eng, lambda e, oap=oap, val=val: e.memset(oap, val), reads=[], writes=[out.buf])

def t_mm(P, out_ap, lhsT_ap, rhs_ap, start, stop, reads, writes, inc, tile_position=None):
    P.op("tensor",
         lambda e, o=out_ap, l=lhsT_ap, r=rhs_ap, st=start, sp=stop, tp=tile_position:
             e.matmul(o, l, r, start=st, stop=sp, tile_position=tp),
         reads=list(reads), writes=list(writes), inc=inc)

def t_tr(P, out_ap, in_ap, ident, reads, writes, inc=True, pn=128):
    idap = ident.t[0:pn, 0:pn]
    P.op("tensor",
         lambda e, o=out_ap, i=in_ap, idn=idap: e.transpose(o, i, idn),
         reads=list(reads) + [ident.buf], writes=list(writes), inc=inc)

def v_red(P, out, a, op, outsl=None, asl=None, negate=None):
    oap = outsl if outsl is not None else out.ap
    aap = asl if asl is not None else a.ap
    P.op("vector", lambda e, oap=oap, aap=aap: e.tensor_reduce(oap, aap, AX.X, op, negate=negate),
         reads=[a.buf], writes=[out.buf])

def v_recip(P, out, a, outsl=None, asl=None):
    oap = outsl if outsl is not None else out.ap
    aap = asl if asl is not None else a.ap
    P.op("vector", lambda e, oap=oap, aap=aap: e.reciprocal(oap, aap),
         reads=[a.buf], writes=[out.buf])

def v_scan(P, out, d0, d1, initial, op0, op1, outsl=None, d0sl=None, d1sl=None):
    oap = outsl if outsl is not None else out.ap
    a0 = d0sl if d0sl is not None else d0.ap
    a1 = d1sl if d1sl is not None else d1.ap
    P.op("vector", lambda e, oap=oap, a0=a0, a1=a1: e.tensor_tensor_scan(oap, a0, a1, initial, op0, op1),
         reads=[d0.buf, d1.buf], writes=[out.buf])


def bcast_row(P, C, ps, row_sb, row_sl, width, ones1x128_f32, ps_cols=None):
    out_ap = ps[0:128, 0:width] if ps_cols is None else ps[0:128, ps_cols[0]:ps_cols[1]]
    t_mm(P, out_ap, ones1x128_f32.t[0:1, 0:128], row_sl,
         start=True, stop=True, reads=[row_sb.buf, ones1x128_f32.buf], writes=[ps.buf], inc=True)



def mm_big(C, name, Ws, K, X_ht, TT_, evac, x_resident=None, gt=2, gn=None, extra_w_reads=(), xdouble=True):
    P, A, nc = C.P, C.A, C.nc
    NK = K // 128
    nW = len(Ws)
    if gn is None:
        gn = max(1, 4 // (gt * nW))
    wbs = []
    for wi_, (w_ht, N, _) in enumerate(Ws):
        wb = A.sbuf(f"wb_{name}_{wi_}", [128, NK * N], BF16)
        for kt in range(NK):
            C.P.dma("gpsimd", wb.t[:, kt * N:(kt + 1) * N], w_ht.ap[kt * 128:(kt + 1) * 128, 0:N],
                    reads=[w_ht.buf, *extra_w_reads], writes=[wb.buf])
        wbs.append(wb)
    sweep_w = gt * 512
    nsweep = TT_ // sweep_w
    xbs = None
    if x_resident is None:
        nxb = 2 if xdouble else 1
        xbs = [A.sbuf(f"mmxb{i}", [128, NK * sweep_w], BF16) for i in range(nxb)]
    pss = [A.psum(f"mmps{i}") for i in range(min(7, 2 * gn * gt * nW))]
    psi = 0
    for sw in range(nsweep):
        t0s = sw * sweep_w
        if x_resident is not None:
            xb = x_resident
        else:
            xb = xbs[sw % len(xbs)]
            for kt in range(NK):
                C.P.dma("sync", xb.t[:, kt * sweep_w:(kt + 1) * sweep_w],
                        X_ht.ap[kt * 128:(kt + 1) * 128, t0s:t0s + sweep_w],
                        reads=[X_ht.buf], writes=[xb.buf])
        NN = Ws[0][1] // 128
        for ng0 in range(0, NN, gn):
            nts = list(range(ng0, min(ng0 + gn, NN)))
            grp = []
            for wi in range(nW):
                for nt in nts:
                    for tci in range(gt):
                        grp.append((wi, nt, tci, pss[psi % len(pss)]))
                        psi += 1
            for kt in range(NK):
                for (wi, nt, tci, ps) in grp:
                    N = Ws[wi][1]
                    if x_resident is not None:
                        rhs = xb.t[:, kt * T + t0s + tci * 512: kt * T + t0s + tci * 512 + 512]
                    else:
                        rhs = xb.t[:, kt * sweep_w + tci * 512: kt * sweep_w + tci * 512 + 512]
                    t_mm(P, ps[0:128, 0:512],
                         wbs[wi].t[:, kt * N + nt * 128: kt * N + nt * 128 + 128],
                         rhs,
                         start=(kt == 0), stop=(kt == NK - 1),
                         reads=[wbs[wi].buf, xb.buf], writes=[ps.buf],
                         inc=(kt == NK - 1))
            for (wi, nt, tci, ps) in grp:
                evac(wi, nt, t0s + tci * 512, 512, ps)

def _nn_total(Ws):
    return sum(w[1] // 128 for w in Ws)
def _nn_w(Ws):
    return Ws[0][1] // 128


def mm_xres(C, name, W_ht, K, N, X_sb, evac, thalf=1024, kg=4):
    P, A = C.P, C.A
    NK, NN = K // 128, N // 128
    ntc = thalf // 512
    gn = max(1, 4 // ntc)
    slabs = [A.sbuf(f"mmslab{i}", [128, kg * gn * 128], BF16) for i in range(3)]
    pss = [A.psum(f"mmps{i}") for i in range(min(7, 2 * gn * ntc))]
    psi = 0
    sli = 0
    for tswp in range(T // thalf):
        t0s = tswp * thalf
        for ng0 in range(0, NN, gn):
            gnn = min(gn, NN - ng0)
            grp = []
            for nt in range(gnn):
                for tci in range(ntc):
                    grp.append((nt, tci, pss[psi % len(pss)]))
                    psi += 1
            for ktg in range(NK // kg):
                sl = slabs[sli % 3]
                sli += 1
                src = W_ht.ap[ktg * kg * 128:(ktg + 1) * kg * 128, ng0 * 128:(ng0 + gnn) * 128]
                src = src.rearrange("(a p) n -> p a n", p=128)
                dst = sl.t[:, 0:kg * gnn * 128].rearrange("p (a n) -> p a n", a=kg)
                P.dma("gpsimd", dst, src, reads=[W_ht.buf], writes=[sl.buf])
                for ki in range(kg):
                    kt = ktg * kg + ki
                    for gi, (nt, tci, ps) in enumerate(grp):
                        last_of_slab = (ki == kg - 1) and (gi == len(grp) - 1)
                        inc = True if kt == NK - 1 and gi == len(grp) - 1 else ("mark" if last_of_slab else False)
                        t_mm(P, ps[0:128, 0:512],
                             sl.t[:, ki * gnn * 128 + nt * 128: ki * gnn * 128 + nt * 128 + 128],
                             X_sb.t[:, kt * T + t0s + tci * 512: kt * T + t0s + tci * 512 + 512],
                             start=(kt == 0), stop=(kt == NK - 1),
                             reads=[sl.buf, X_sb.buf], writes=[ps.buf],
                             inc=inc)
            for (nt, tci, ps) in grp:
                evac(0, ng0 + nt, t0s + tci * 512, 512, ps)


def load_fm(C, name, ht, R, TT_, dtype, queue="sync"):
    A, P = C.A, C.P
    sb = A.sbuf(name, [128, (R // 128) * TT_], dtype)
    for rt in range(R // 128):
        P.dma(queue, sb.t[:, rt * TT_:(rt + 1) * TT_], ht.ap[rt * 128:(rt + 1) * 128, 0:TT_],
              reads=[ht.buf], writes=[sb.buf])
    return sb

def store_fm(C, ht, sb, R, TT_, queue="scalar"):
    P = C.P
    for rt in range(R // 128):
        P.dma(queue, ht.ap[rt * 128:(rt + 1) * 128, 0:TT_], sb.t[:, rt * TT_:(rt + 1) * TT_],
              reads=[sb.buf], writes=[ht.buf])



class Consts:
    pass

def load_consts(C):
    A, P = C.A, C.P
    cs = Consts()
    ident_bf = C.hin("c_ident_bf", (128, 128), BF16)
    ident_f = C.hin("c_ident_f", (128, 128), F32)
    ones_col = C.hin("c_ones_col", (128, 1), BF16)
    ones_row_f = C.hin("c_ones_row_f", (1, 128), F32)
    ones_row_bf = C.hin("c_ones_row_bf", (1, 128), BF16)
    stair = C.hin("c_stair", (128, 1536), BF16)
    fcin = C.hin("c_fconst", (128, 128), F32)
    cs.ident_bf = A.sbuf("ident_bf", [128, 128], BF16)
    cs.ident_f = A.sbuf("ident_f", [128, 128], F32)
    cs.ones_col = A.sbuf("ones_col", [128, 1], BF16)
    cs.ones_row_f = A.sbuf("ones_row_f", [1, 128], F32)
    cs.ones_row_bf = A.sbuf("ones_row_bf", [1, 128], BF16)
    cs.stair = A.sbuf("stair", [128, 1536], BF16)
    cs.fc = A.sbuf("fconst", [128, 128], F32)
    C.fc_sb = cs.fc
    for dst, src in [(cs.ident_bf, ident_bf), (cs.ident_f, ident_f), (cs.ones_col, ones_col),
                     (cs.ones_row_f, ones_row_f), (cs.ones_row_bf, ones_row_bf), (cs.stair, stair),
                     (cs.fc, fcin)]:
        P.dma("sync", dst.ap, src.ap, reads=[src.buf], writes=[dst.buf])
    return cs


def host_consts():
    d = {}
    d["c_ident_bf"] = np.eye(128, dtype=BF)
    d["c_ident_f"] = np.eye(128, dtype=np.float32)
    d["c_ones_col"] = np.ones((128, 1), dtype=BF)
    d["c_ones_row_f"] = np.ones((1, 128), dtype=np.float32)
    d["c_ones_row_bf"] = np.ones((1, 128), dtype=BF)
    p = np.arange(128)[:, None]
    c = np.arange(1536)[None, :]
    d["c_stair"] = (p <= c - 512).astype(BF)
    sel8 = np.zeros((8, 8 * 128), dtype=np.float32)
    for e in range(8):
        sel8[e, e * 128:(e + 1) * 128] = 1.0
    d["c_sel8"] = sel8
    stairL = (p >= c - 512).astype(BF)
    d["c_stairL"] = stairL
    s16 = np.zeros((16, 16 * 128), dtype=np.float32)
    for r in range(16):
        s16[r, r * 128:(r + 1) * 128] = 1.0
    d["c_sel16"] = s16
    e127 = np.zeros((128, 1), dtype=np.float32)
    e127[127, 0] = 1.0
    d["c_e127"] = e127
    selh1 = np.zeros((32, 32 * 32), dtype=np.float32)
    for h in range(32):
        selh1[h, h * 32:(h + 1) * 32] = 1.0
    d["c_selH1"] = selh1
    selr = np.zeros((32, 64), dtype=np.float32)
    for m in range(64):
        selr[m % 32, m] = 1.0
    d["c_selR"] = selr
    selw = np.zeros((32, 16 * 128), dtype=np.float32)
    for blk in range(16):
        for m in range(128):
            selw[blk * 2 + m // 64, blk * 128 + m] = 1.0
    d["c_selW"] = selw
    d["c_selW_bf"] = selw.astype(BF)
    e32 = np.zeros((32, 32 * 64), dtype=np.float32)
    for h in range(32):
        e32[h, h * 64:(h + 1) * 64] = 1.0
    d["c_e32x64"] = e32
    return d



def phase_prep(C, cs):
    P, A = C.P, C.A
    A.push()
    x_in = C.hin("x_in", (T, D), F32)
    x_fm = C.hint("x_fm", (D, T), F32)
    xin_sb = load_fm(C, "prep_in", x_in, T, D, F32)
    out_t = [A.sbuf(f"prep_out{i}", [128, T], F32) for i in range(2)]
    ps = [A.psum(f"prep_ps{i}") for i in range(4)]
    for dt in range(8):
        ob = out_t[dt % 2]
        for tt in range(16):
            p = ps[tt % 4]
            t_tr(P, p[0:128, 0:128], xin_sb.t[:, tt * D + dt * 128: tt * D + dt * 128 + 128],
                 cs.ident_f, reads=[xin_sb.buf], writes=[p.buf])
            v_copy(P, ob, p, outsl=ob.t[:, tt * 128:(tt + 1) * 128], asl=p[0:128, 0:128])
        P.dma("scalar", x_fm.ap[dt * 128:(dt + 1) * 128, 0:T], ob.ap, reads=[ob.buf], writes=[x_fm.buf])
    A.pop()
    return x_fm

def store_tile(C, ht, sb, row0, TT_, queue="scalar", sl=None):
    ap = sb.ap if sl is None else sl
    C.P.dma(queue, ht.ap[row0:row0 + 128, 0:TT_], ap, reads=[sb.buf], writes=[ht.buf])



def phase_ln(C, cs):
    P, A = C.P, C.A
    A.push()
    x_fm = C.hint("x_fm", (D, T), F32)
    xn_h = C.hint("xn", (D, T), BF16)
    xq = [A.sbuf(f"ln_xq{i}", [128, 8 * 512], F32) for i in range(2)]
    xb = [A.sbuf(f"ln_xb{i}", [128, 8 * 512], BF16) for i in range(2)]
    sq = [A.sbuf(f"ln_sq{i}", [128, 8 * 512], BF16) for i in range(2)]
    xnt = [A.sbuf(f"ln_xnt{i}", [128, 8 * 512], BF16) for i in range(2)]
    tmp = [A.sbuf(f"ln_tmp{i}", [128, 8 * 512], F32) for i in range(2)]
    sums = A.sbuf("ln_sums", [1, T], F32)
    sqs = A.sbuf("ln_sqs", [1, T], F32)
    mean = A.sbuf("ln_mean", [1, T], F32)
    var = A.sbuf("ln_var", [1, T], F32)
    rstd = A.sbuf("ln_rstd", [1, T], F32)
    m2 = A.sbuf("ln_m2", [1, T], F32)
    ps_s = A.psum("ln_ps_s")
    ps_q = A.psum("ln_ps_q")
    ps_m = [A.psum(f"ln_ps_m{i}") for i in range(2)]
    ps_r = [A.psum(f"ln_ps_r{i}") for i in range(2)]
    for tc in range(4):
        pp = tc % 2
        x_, b_, q_, n_, t_ = xq[pp], xb[pp], sq[pp], xnt[pp], tmp[pp]
        csl = slice(tc * 512, (tc + 1) * 512)
        for ft in range(8):
            P.dma("sync", x_.t[:, ft * 512:(ft + 1) * 512],
                  x_fm.ap[ft * 128:(ft + 1) * 128, csl], reads=[x_fm.buf], writes=[x_.buf])
        for ft in range(8):
            fsl = slice(ft * 512, (ft + 1) * 512)
            v_copy(P, b_, x_, outsl=b_.t[:, fsl], asl=x_.t[:, fsl])
            s_act(P, q_, x_, AF.Square, outsl=q_.t[:, fsl], asl=x_.t[:, fsl])
        for ft in range(8):
            fsl = slice(ft * 512, (ft + 1) * 512)
            t_mm(P, ps_s[0:1, 0:512], cs.ones_col.t[0:128, 0:1], b_.t[:, fsl],
                 start=(ft == 0), stop=(ft == 7), reads=[b_.buf, cs.ones_col.buf],
                 writes=[ps_s.buf], inc=(ft == 7))
        for ft in range(8):
            fsl = slice(ft * 512, (ft + 1) * 512)
            t_mm(P, ps_q[0:1, 0:512], cs.ones_col.t[0:128, 0:1], q_.t[:, fsl],
                 start=(ft == 0), stop=(ft == 7), reads=[q_.buf, cs.ones_col.buf],
                 writes=[ps_q.buf], inc=(ft == 7))
        v_copy(P, sums, ps_s, outsl=sums.t[0:1, csl], asl=ps_s[0:1, 0:512])
        v_copy(P, sqs, ps_q, outsl=sqs.t[0:1, csl], asl=ps_q[0:1, 0:512])
        v_ts(P, mean, sums, 1.0 / D, ALU.mult, outsl=mean.t[0:1, csl], asl=sums.t[0:1, csl])
        v_tt(P, m2, mean, mean, ALU.mult, outsl=m2.t[0:1, csl], asl=mean.t[0:1, csl], bsl=mean.t[0:1, csl])
        v_ts(P, var, sqs, 1.0 / D, ALU.mult, outsl=var.t[0:1, csl], asl=sqs.t[0:1, csl])
        v_tt(P, var, var, m2, ALU.subtract, outsl=var.t[0:1, csl], asl=var.t[0:1, csl], bsl=m2.t[0:1, csl])
        s_act(P, rstd, var, AF.Ln, bias=cfc(C, 1e-5)[0:1, :], sbufs=[C.fc_sb.buf],
              outsl=rstd.t[0:1, csl], asl=var.t[0:1, csl])
        s_act(P, rstd, rstd, AF.Exp, scale=cfc(C, -0.5)[0:1, :], sbufs=[C.fc_sb.buf],
              outsl=rstd.t[0:1, csl], asl=rstd.t[0:1, csl])
        bcast_row(P, C, ps_m[pp], mean, mean.t[0:1, csl], 512, cs.ones_row_f)
        bcast_row(P, C, ps_r[pp], rstd, rstd.t[0:1, csl], 512, cs.ones_row_f)
        for ft in range(8):
            fsl = slice(ft * 512, (ft + 1) * 512)
            v_tt(P, t_, x_, ps_m[pp], ALU.subtract, outsl=t_.t[:, fsl],
                 asl=x_.t[:, fsl], bsl=ps_m[pp][0:128, 0:512])
            v_tt(P, n_, t_, ps_r[pp], ALU.mult, outsl=n_.t[:, fsl],
                 asl=t_.t[:, fsl], bsl=ps_r[pp][0:128, 0:512])
        for ft in range(8):
            P.dma("scalar", xn_h.ap[ft * 128:(ft + 1) * 128, csl],
                  n_.t[:, ft * 512:(ft + 1) * 512], reads=[n_.buf], writes=[xn_h.buf])
    A.pop()



def phase_ffn(C, cs, L):
    P, A = C.P, C.A
    pre = f"L{L}_ffn"
    x_fm = C.hint("x_fm", (D, T), F32)
    xn_h = C.hint("xn", (D, T), BF16)
    ha = C.hint("ffn_ha", (4096, T), BF16)
    hb = C.hint("ffn_hb", (4096, T), BF16)

    w1 = C.hin(f"{pre}_w1T", (D, 4096), F32)
    w2 = C.hin(f"{pre}_w2T", (D, 4096), F32)
    h1 = C.hin(f"{pre}_h1T", (4096, 4096), F32)
    h2 = C.hin(f"{pre}_h2T", (4096, 4096), F32)
    w3 = C.hin(f"{pre}_w3T", (4096, D), F32)
    sc = C.hin(f"{pre}_scales", (128, 32 * 4 + 8), F32)

    A.push()
    scs = A.sbuf("ffn_sc", [128, 32 * 4 + 8], F32)
    P.dma("sync", scs.ap, sc.ap, reads=[sc.buf], writes=[scs.buf])
    stg = [A.sbuf(f"ffn_stgA{i}", [128, 512], F32) for i in range(4)]
    outt = [A.sbuf(f"ffn_outA{i}", [128, 512], BF16) for i in range(4)]
    si = [0]
    def evacA(wi, nt, t0, tw, ps, _state={}):
        key = (nt, t0)
        if wi == 0:
            k = si[0] % 4
            si[0] += 1
            _state[key] = k
            s_act(P, stg[k], ps, AF.Silu, scale=scs.t[:, nt:nt + 1], asl=ps[0:128, 0:tw],
                  outsl=stg[k].t[:, 0:tw], sbufs=[scs.buf])
        else:
            k = _state.pop(key)
            v_ts(P, outt[k], ps, scs.t[:, 32 + nt:32 + nt + 1], ALU.mult,
                 outsl=outt[k].t[:, 0:tw], asl=ps[0:128, 0:tw], sbufs=[scs.buf])
            v_tt(P, outt[k], stg[k], outt[k], ALU.mult, outsl=outt[k].t[:, 0:tw],
                 asl=stg[k].t[:, 0:tw], bsl=outt[k].t[:, 0:tw])
            P.dma("scalar", ha.ap[nt * 128:(nt + 1) * 128, t0:t0 + tw], outt[k].t[:, 0:tw],
                  reads=[outt[k].buf], writes=[ha.buf])
    mm_big(C, "ffnA", [(w1, 4096, None), (w2, 4096, None)], D, C.hint("xn", (D, T), BF16), T, evacA, gt=2, gn=1)
    A.pop()

    def run_gelu_stage(name, W_ht, src_ht, dst_ht, scol0):
        A.push()
        scl = A.sbuf("ffn_scg", [128, 32 * 4 + 8], F32)
        P.dma("sync", scl.ap, sc.ap, reads=[sc.buf], writes=[scl.buf])
        xres = load_fm(C, f"xres_{name}", src_ht, 4096, T, BF16)
        obufs = [A.sbuf(f"ffn_og{i}", [128, 512], BF16) for i in range(4)]
        oi = [0]
        def ev(wi, nt, t0, tw, ps):
            ob = obufs[oi[0] % 4]
            oi[0] += 1
            s_act(P, ob, ps, AF.Gelu, scale=scl.t[:, scol0 + nt:scol0 + nt + 1],
                  asl=ps[0:128, 0:tw], outsl=ob.t[:, 0:tw], sbufs=[scl.buf])
            P.dma("scalar", dst_ht.ap[nt * 128:(nt + 1) * 128, t0:t0 + tw], ob.t[:, 0:tw],
                  reads=[ob.buf], writes=[dst_ht.buf])
        mm_xres(C, name, W_ht, 4096, 4096, xres, ev)
        A.pop()
    run_gelu_stage("ffnB", h1, ha, hb, 64)
    run_gelu_stage("ffnC", h2, hb, ha, 96)

    A.push()
    scs3 = A.sbuf("ffn_sc4", [128, 32 * 4 + 8], F32)
    P.dma("sync", scs3.ap, sc.ap, reads=[sc.buf], writes=[scs3.buf])
    xres = load_fm(C, "xres_ffnD", ha, 4096, T, BF16)
    xt = [A.sbuf(f"ffn_xt{i}", [128, 512], F32) for i in range(4)]
    xo = [A.sbuf(f"ffn_xo{i}", [128, 512], F32) for i in range(4)]
    di = [0]
    def evacD(wi, nt, t0, tw, ps):
        k = di[0] % 4
        di[0] += 1
        P.dma("sync", xt[k].t[:, 0:tw], x_fm.ap[nt * 128:(nt + 1) * 128, t0:t0 + tw],
              reads=[x_fm.buf], writes=[xt[k].buf])
        v_ts(P, xo[k], ps, scs3.t[:, 128 + nt:128 + nt + 1], ALU.mult,
             outsl=xo[k].t[:, 0:tw], asl=ps[0:128, 0:tw], sbufs=[scs3.buf])
        v_tt(P, xo[k], xo[k], xt[k], ALU.add, outsl=xo[k].t[:, 0:tw],
             asl=xo[k].t[:, 0:tw], bsl=xt[k].t[:, 0:tw])
        P.dma("scalar", x_fm.ap[nt * 128:(nt + 1) * 128, t0:t0 + tw], xo[k].t[:, 0:tw],
              reads=[xo[k].buf], writes=[x_fm.buf])
    mm_xres(C, "ffnD", w3, 4096, D, xres, evacD)
    A.pop()



def phase_attn0(C, cs):
    P, A = C.P, C.A
    x_fm = C.hint("x_fm", (D, T), F32)
    xn_h = C.hint("xn", (D, T), BF16)
    wq = C.hin("L0_attn_W_q", (D, 2048), F32)
    wk = C.hin("L0_attn_W_k", (D, 2048), F32)
    wv = C.hin("L0_attn_W_v", (D, 2048), F32)
    wo = C.hin("L0_attn_W_o", (2048, D), F32)
    lq1 = C.hin("L0_attn_lambda_q1", (8, 128), F32)
    lk1 = C.hin("L0_attn_lambda_k1", (8, 128), F32)
    lq2 = C.hin("L0_attn_lambda_q2", (8, 128), F32)
    lk2 = C.hin("L0_attn_lambda_k2", (8, 128), F32)
    rmss = C.hin("L0_attn_rms_fm", (128, 2), F32)
    lam_hbm = C.hint("l0_lam", (1, 8), F32)
    SCALE = 128.0 ** -0.5

    A.push()
    lt = [A.sbuf(f"a0_l{i}", [8, 128], F32) for i in range(4)]
    for sb, ht in zip(lt, [lq1, lk1, lq2, lk2]):
        P.dma("sync", sb.ap, ht.ap, reads=[ht.buf], writes=[sb.buf])
    lprod = A.sbuf("a0_lp", [8, 128], F32)
    lred = A.sbuf("a0_lr", [8, 2], F32)
    v_tt(P, lprod, lt[0], lt[1], ALU.mult)
    v_red(P, lred, lprod, ALU.add, outsl=lred.t[0:8, 0:1], asl=lprod.t[0:8, 0:128])
    v_tt(P, lprod, lt[2], lt[3], ALU.mult)
    v_red(P, lred, lprod, ALU.add, outsl=lred.t[0:8, 1:2], asl=lprod.t[0:8, 0:128])
    s_act(P, lred, lred, AF.Exp)
    lamc = A.sbuf("a0_lam", [8, 1], F32)
    v_tt(P, lamc, lred, lred, ALU.subtract, outsl=lamc.t[0:8, 0:1],
         asl=lred.t[0:8, 0:1], bsl=lred.t[0:8, 1:2])
    v_ts(P, lamc, lamc, 0.8, ALU.add)
    P.dma("scalar", lam_hbm.ap.rearrange("a b -> b a"), lamc.t[0:8, 0:1], reads=[lamc.buf], writes=[lam_hbm.buf])
    lam_row = A.sbuf("a0_lamr", [1, 8], F32)
    P.dma("sync", lam_row.ap, lam_hbm.ap, reads=[lam_hbm.buf], writes=[lam_row.buf])
    gsc = A.sbuf("a0_g", [128, 2], F32)
    P.dma("sync", gsc.ap, rmss.ap, reads=[rmss.buf], writes=[gsc.buf])
    v_ts(P, gsc, gsc, 0.2, ALU.mult)

    xn_sb = load_fm(C, "a0_xn", xn_h, D, T, BF16)
    O_fm = A.sbuf("a0_O", [128, 16 * T], BF16)

    A.push()
    for h in range(8):
        qkv = {}
        ps_p = [A.psum(f"a0_psp{i}") for i in range(2)]
        for nm, w_ht in [("q", wq), ("k", wk)]:
            wb = A.sbuf(f"a0_wb_{nm}", [128, 8 * 256], BF16)
            for kt in range(8):
                P.dma("gpsimd", wb.t[:, kt * 256:(kt + 1) * 256],
                      w_ht.ap[kt * 128:(kt + 1) * 128, h * 256:(h + 1) * 256],
                      reads=[w_ht.buf], writes=[wb.buf])
            dst = A.sbuf(f"a0_{nm}", [128, 2 * T], BF16)
            for half in range(2):
                for tc in range(4):
                    ps = ps_p[tc % 2]
                    for kt in range(8):
                        t_mm(P, ps[0:128, 0:512],
                             wb.t[:, kt * 256 + half * 128: kt * 256 + half * 128 + 128],
                             xn_sb.t[:, kt * T + tc * 512: kt * T + tc * 512 + 512],
                             start=(kt == 0), stop=(kt == 7),
                             reads=[wb.buf, xn_sb.buf], writes=[ps.buf], inc=(kt == 7))
                    v_copy(P, dst, ps, outsl=dst.t[:, half * T + tc * 512: half * T + tc * 512 + 512],
                           asl=ps[0:128, 0:512])
            qkv[nm] = dst
        wb = A.sbuf("a0_wb_v", [128, 8 * 256], BF16)
        for kt in range(8):
            P.dma("gpsimd", wb.t[:, kt * 256:(kt + 1) * 256],
                  wv.ap[kt * 128:(kt + 1) * 128, h * 256:(h + 1) * 256],
                  reads=[wv.buf], writes=[wb.buf])
        vtm = A.sbuf("a0_vtm", [128, 16 * 256], BF16)
        for tt in range(16):
            ps = ps_p[tt % 2]
            for kt in range(8):
                t_mm(P, ps[0:128, 0:256],
                     xn_sb.t[:, kt * T + tt * 128: kt * T + tt * 128 + 128],
                     wb.t[:, kt * 256:(kt + 1) * 256],
                     start=(kt == 0), stop=(kt == 7),
                     reads=[wb.buf, xn_sb.buf], writes=[ps.buf], inc=(kt == 7))
            v_copy(P, vtm, ps, outsl=vtm.t[:, tt * 256:(tt + 1) * 256], asl=ps[0:128, 0:256])
        e1 = A.sbuf("a0_e1", [128, 16 * 512], BF16)
        e2 = A.sbuf("a0_e2", [128, 16 * 512], BF16)
        ps_r = [A.psum(f"a0_psr{i}") for i in range(2)]
        ps_o = [A.psum(f"a0_pso{i}") for i in range(4)]
        tmp1 = A.sbuf("a0_t1", [128, 512], F32)
        tmp2 = A.sbuf("a0_t2", [128, 512], F32)
        u_sb = A.sbuf("a0_u", [128, 2 * 512], F32)
        usq = A.sbuf("a0_usq", [128, 2 * 512], BF16)
        rows = A.sbuf("a0_rows", [1, 6 * 512], F32)
        for qc in range(4):
            nkt = qc * 4 + 4
            for ai, (e, qoff) in enumerate([(e1, 0), (e2, T)]):
                for kt in range(nkt):
                    ps = ps_p[kt % 2]
                    t_mm(P, ps[0:128, 0:512],
                         qkv["k"].t[:, qoff + kt * 128: qoff + kt * 128 + 128],
                         qkv["q"].t[:, qoff + qc * 512: qoff + qc * 512 + 512],
                         start=True, stop=True,
                         reads=[qkv["k"].buf, qkv["q"].buf], writes=[ps.buf], inc=True)
                    s_act(P, e, ps, AF.Exp, scale=SCALE,
                          outsl=e.t[:, kt * 512:(kt + 1) * 512], asl=ps[0:128, 0:512])
                for kt in range(qc * 4, nkt):
                    off = qc * 512 - kt * 128
                    v_tt(P, e, e, cs.stair, ALU.mult,
                         outsl=e.t[:, kt * 512:(kt + 1) * 512],
                         asl=e.t[:, kt * 512:(kt + 1) * 512],
                         bsl=cs.stair.t[:, off + 512: off + 1024])
                pr = ps_r[ai]
                for kt in range(nkt):
                    t_mm(P, pr[0:1, 0:512], cs.ones_col.t[0:128, 0:1],
                         e.t[:, kt * 512:(kt + 1) * 512],
                         start=(kt == 0), stop=(kt == nkt - 1),
                         reads=[e.buf, cs.ones_col.buf], writes=[pr.buf], inc=(kt == nkt - 1))
                for half in range(2):
                    po = ps_o[ai * 2 + half]
                    for kt in range(nkt):
                        t_mm(P, po[0:128, 0:512],
                             vtm.t[:, kt * 256 + half * 128: kt * 256 + half * 128 + 128],
                             e.t[:, kt * 512:(kt + 1) * 512],
                             start=(kt == 0), stop=(kt == nkt - 1),
                             reads=[vtm.buf, e.buf], writes=[po.buf], inc=(kt == nkt - 1))
            r1s = rows.t[0:1, 0:512]; rr2 = rows.t[0:1, 512:1024]
            crow = rows.t[0:1, 1024:1536]; erow = rows.t[0:1, 1536:2048]
            ssq = rows.t[0:1, 2048:2560]; rstd = rows.t[0:1, 2560:3072]
            v_copy(P, rows, ps_r[0], outsl=r1s, asl=ps_r[0][0:1, 0:512])
            v_recip(P, rows, ps_r[1], outsl=rr2, asl=ps_r[1][0:1, 0:512])
            v_tt(P, rows, rows, rows, ALU.mult, outsl=crow, asl=r1s, bsl=rr2)
            v_ts(P, rows, rows, lam_row.t[0:1, h:h + 1], ALU.mult, outsl=crow, asl=crow,
                 sbufs=[lam_row.buf])
            v_tt(P, rows, rows, rows, ALU.mult, outsl=erow, asl=r1s, bsl=r1s)
            v_ts(P, rows, rows, 1e-5, ALU.mult, outsl=erow, asl=erow)
            bcast_row(P, C, ps_p[0], rows, crow, 512, cs.ones_row_f)
            v_copy(P, tmp2, ps_p[0], asl=ps_p[0][0:128, 0:512])
            for half in range(2):
                v_tt(P, tmp1, ps_o[2 + half], tmp2, ALU.mult, asl=ps_o[2 + half][0:128, 0:512],
                     outsl=tmp1.t[:, 0:512], bsl=tmp2.t[:, 0:512])
                v_tt(P, u_sb, ps_o[half], tmp1, ALU.subtract,
                     outsl=u_sb.t[:, half * 512:(half + 1) * 512],
                     asl=ps_o[half][0:128, 0:512], bsl=tmp1.t[:, 0:512])
                s_act(P, usq, u_sb, AF.Square, outsl=usq.t[:, half * 512:(half + 1) * 512],
                      asl=u_sb.t[:, half * 512:(half + 1) * 512])
            for half in range(2):
                t_mm(P, ps_r[0][0:1, 0:512], cs.ones_col.t[0:128, 0:1],
                     usq.t[:, half * 512:(half + 1) * 512],
                     start=(half == 0), stop=(half == 1),
                     reads=[usq.buf, cs.ones_col.buf], writes=[ps_r[0].buf], inc=(half == 1))
            v_ts(P, rows, ps_r[0], 1.0 / 256, ALU.mult, outsl=ssq, asl=ps_r[0][0:1, 0:512])
            v_tt(P, rows, rows, rows, ALU.add, outsl=ssq, asl=ssq, bsl=erow)
            s_act(P, rows, rows, AF.Ln, outsl=rstd, asl=ssq)
            s_act(P, rows, rows, AF.Exp, scale=cfc(C, -0.5)[0:1, :], outsl=rstd, asl=rstd,
                  sbufs=[C.fc_sb.buf])
            bcast_row(P, C, ps_p[1], rows, rstd, 512, cs.ones_row_f)
            v_copy(P, tmp2, ps_p[1], asl=ps_p[1][0:128, 0:512])
            for half in range(2):
                v_tt(P, tmp1, u_sb, tmp2, ALU.mult,
                     outsl=tmp1.t[:, 0:512], asl=u_sb.t[:, half * 512:(half + 1) * 512],
                     bsl=tmp2.t[:, 0:512])
                v_ts(P, O_fm, tmp1, gsc.t[:, half:half + 1], ALU.mult,
                     outsl=O_fm.t[:, (h * 2 + half) * T + qc * 512: (h * 2 + half) * T + qc * 512 + 512],
                     asl=tmp1.t[:, 0:512], sbufs=[gsc.buf])
    A.pop()

    xt = [A.sbuf(f"a0_xt{i}", [128, 512], F32) for i in range(4)]
    xo = [A.sbuf(f"a0_xo{i}", [128, 512], F32) for i in range(4)]
    di = [0]
    def evacO(wi, nt, t0, tw, ps):
        k = di[0] % 4
        di[0] += 1
        P.dma("sync", xt[k].t[:, 0:tw], x_fm.ap[nt * 128:(nt + 1) * 128, t0:t0 + tw],
              reads=[x_fm.buf], writes=[xt[k].buf])
        v_tt(P, xo[k], ps, xt[k], ALU.add, outsl=xo[k].t[:, 0:tw],
             asl=ps[0:128, 0:tw], bsl=xt[k].t[:, 0:tw])
        P.dma("scalar", x_fm.ap[nt * 128:(nt + 1) * 128, t0:t0 + tw], xo[k].t[:, 0:tw],
              reads=[xo[k].buf], writes=[x_fm.buf])
    mm_big(C, "a0O", [(wo, D, None)], 2048, None, T, evacO, x_resident=O_fm, gt=2, gn=2)
    A.pop()



def phase_attn2(C, cs):
    P, A = C.P, C.A
    x_fm = C.hint("x_fm", (D, T), F32)
    xn_h = C.hint("xn", (D, T), BF16)
    wq = C.hin("L2_attn_Wq", (D, D), F32)
    wk = C.hin("L2_attn_Wk", (D, D), F32)
    wv = C.hin("L2_attn_Wv", (D, D), F32)
    wo = C.hin("L2_attn_Wo", (D, D), F32)
    bo = C.hin("L2_attn_bo_fm", (128, 8), F32)
    SCALE = 64.0 ** -0.5

    A.push()
    xn_sb = load_fm(C, "a2_xn", xn_h, D, T, BF16)
    O_fm = A.sbuf("a2_O", [128, 8 * T], BF16)
    bo_sb = A.sbuf("a2_bo", [128, 8], F32)
    P.dma("sync", bo_sb.ap, bo.ap, reads=[bo.buf], writes=[bo_sb.buf])

    A.push()
    vtm_init = [False]
    for g in range(8):
        qkv = {}
        ps_p = [A.psum(f"a2_psp{i}") for i in range(2)]
        for wi, (nm, w_ht) in enumerate([("q", wq), ("k", wk)]):
            wb = A.sbuf(f"a2_wb_{nm}", [128, 8 * 128], BF16)
            for kt in range(8):
                P.dma("gpsimd", wb.t[:, kt * 128:(kt + 1) * 128],
                      w_ht.ap[kt * 128:(kt + 1) * 128, g * 128:(g + 1) * 128],
                      reads=[w_ht.buf], writes=[wb.buf])
            dst = A.sbuf(f"a2_{nm}", [128, T], BF16)
            for tc in range(4):
                ps = ps_p[tc % 2]
                for kt in range(8):
                    t_mm(P, ps[0:128, 0:512], wb.t[:, kt * 128:(kt + 1) * 128],
                         xn_sb.t[:, kt * T + tc * 512: kt * T + tc * 512 + 512],
                         start=(kt == 0), stop=(kt == 7),
                         reads=[wb.buf, xn_sb.buf], writes=[ps.buf], inc=(kt == 7))
                v_copy(P, dst, ps, outsl=dst.t[:, tc * 512:(tc + 1) * 512], asl=ps[0:128, 0:512])
            qkv[nm] = dst
        wb = A.sbuf("a2_wb_v", [128, 8 * 128], BF16)
        for kt in range(8):
            P.dma("gpsimd", wb.t[:, kt * 128:(kt + 1) * 128],
                  wv.ap[kt * 128:(kt + 1) * 128, g * 128:(g + 1) * 128],
                  reads=[wv.buf], writes=[wb.buf])
        vtm = A.sbuf("a2_vtm", [128, 16 * 144], BF16)
        if not vtm_init[0]:
            v_memset(P, vtm, 1.0)
            vtm_init[0] = True
        for tt in range(16):
            ps = ps_p[tt % 2]
            for kt in range(8):
                t_mm(P, ps[0:128, 0:128],
                     xn_sb.t[:, kt * T + tt * 128: kt * T + tt * 128 + 128],
                     wb.t[:, kt * 128:(kt + 1) * 128],
                     start=(kt == 0), stop=(kt == 7),
                     reads=[wb.buf, xn_sb.buf], writes=[ps.buf], inc=(kt == 7))
            v_copy(P, vtm, ps, outsl=vtm.t[:, tt * 144:tt * 144 + 64], asl=ps[0:128, 0:64])
            v_copy(P, vtm, ps, outsl=vtm.t[:, tt * 144 + 65:tt * 144 + 129], asl=ps[0:128, 64:128])
        eT = [A.sbuf(f"a2_eT{i}", [128, 16 * 512], BF16) for i in range(2)]
        ps_sc = [A.psum(f"a2_pssc{i}") for i in range(2)]
        ps_o = [A.psum(f"a2_pso{i}") for i in range(2)]
        ps_bc = A.psum("a2_psbc")
        rrow = A.sbuf("a2_rrow", [1, 512], F32)
        btmp = A.sbuf("a2_btmp", [128, 512], F32)
        for h in range(2):
            hb = h * 64
            for qc in range(4):
                e = eT[qc % 2]
                for kt in range(16):
                    ps = ps_sc[kt % 2]
                    t_mm(P, ps[0:128, 0:512],
                         qkv["k"].t[hb:hb + 64, kt * 128:(kt + 1) * 128],
                         qkv["q"].t[hb:hb + 64, qc * 512:(qc + 1) * 512],
                         start=True, stop=True,
                         reads=[qkv["k"].buf, qkv["q"].buf], writes=[ps.buf], inc=True)
                    s_act(P, e, ps, AF.Exp, scale=SCALE,
                          outsl=e.t[:, kt * 512:(kt + 1) * 512], asl=ps[0:128, 0:512])
                po = ps_o[qc % 2]
                for kt in range(16):
                    t_mm(P, po[0:65, 0:512],
                         vtm.t[:, kt * 144 + h * 65: kt * 144 + h * 65 + 65],
                         e.t[:, kt * 512:(kt + 1) * 512],
                         start=(kt == 0), stop=(kt == 15),
                         reads=[vtm.buf, e.buf], writes=[po.buf], inc=(kt == 15))
                v_recip(P, rrow, po, outsl=rrow.t[0:1, 0:512], asl=po[64:65, 0:512])
                bcast_row(P, C, ps_bc, rrow, rrow.t[0:1, 0:512], 512, cs.ones_row_f)
                v_copy(P, btmp, ps_bc, asl=ps_bc[0:128, 0:512])
                v_tt(P, O_fm, po, btmp, ALU.mult,
                     outsl=O_fm.t[hb:hb + 64, g * T + qc * 512: g * T + qc * 512 + 512],
                     asl=po[0:64, 0:512], bsl=btmp[0:64, 0:512])
    A.pop()

    xt = [A.sbuf(f"a2_xt{i}", [128, 512], F32) for i in range(4)]
    xo = [A.sbuf(f"a2_xo{i}", [128, 512], F32) for i in range(4)]
    di = [0]
    def evacO(wi, nt, t0, tw, ps):
        k = di[0] % 4
        di[0] += 1
        P.dma("sync", xt[k].t[:, 0:tw], x_fm.ap[nt * 128:(nt + 1) * 128, t0:t0 + tw],
              reads=[x_fm.buf], writes=[xt[k].buf])
        v_ts(P, xo[k], ps, bo_sb.t[:, nt:nt + 1], ALU.add,
             outsl=xo[k].t[:, 0:tw], asl=ps[0:128, 0:tw], sbufs=[bo_sb.buf])
        v_tt(P, xo[k], xo[k], xt[k], ALU.add, outsl=xo[k].t[:, 0:tw],
             asl=xo[k].t[:, 0:tw], bsl=xt[k].t[:, 0:tw])
        P.dma("scalar", x_fm.ap[nt * 128:(nt + 1) * 128, t0:t0 + tw], xo[k].t[:, 0:tw],
              reads=[xo[k].buf], writes=[x_fm.buf])
    mm_big(C, "a2O", [(wo, D, None)], D, None, T, evacO, x_resident=O_fm, gt=2, gn=2)
    A.pop()



TPI = float(np.float32(2 * np.pi))
TPI2 = float(np.float64(2 * np.pi) - np.float64(np.float32(2 * np.pi)))
LN1024 = float(np.log(1024.0))

def trig_reduce(P, C, dst, src, dstsl, srcsl, yy, ki, kf, phase_add=0.0):
    sh = 8192.0 - phase_add / TPI if False else 8192.0
    if phase_add != 0.0:
        v_ts(P, dst, src, phase_add, ALU.add, outsl=dstsl, asl=srcsl)
        srcap = dstsl
        srcbuf = dst
    else:
        srcap = srcsl
        srcbuf = src
    v_ts(P, yy, srcbuf, 1.0 / TPI, ALU.mult, 8192.0, ALU.add, asl=srcap)
    v_copy(P, ki, yy)
    v_copy(P, kf, ki)
    v_ts(P, kf, kf, -8192.0, ALU.add)
    v_stt(P, dst, kf, -TPI, srcbuf, ALU.mult, ALU.add, outsl=dstsl, i1sl=srcap)
    v_stt(P, dst, kf, -TPI2, dst, ALU.mult, ALU.add, outsl=dstsl, i1sl=dstsl)
    v_ts(P, yy, dst, float(np.pi), ALU.is_gt, asl=dstsl)
    v_stt(P, dst, yy, -TPI, dst, ALU.mult, ALU.add, outsl=dstsl, i1sl=dstsl)

def softplus32(P, C, out, outsl, src, srcsl, wk, bias_col=None, bias_bufs=(), pn=32, width=T):
    v0 = wk.t[0:pn, 0:width]; a0 = wk.t[0:pn, width:2 * width]; m0 = wk.t[0:pn, 2 * width:3 * width]
    if bias_col is not None:
        v_ts(P, wk, src, bias_col, ALU.add, outsl=v0, asl=srcsl, sbufs=list(bias_bufs))
    else:
        v_copy(P, wk, src, outsl=v0, asl=srcsl)
    s_act(P, wk, wk, AF.Abs, outsl=a0, asl=v0)
    s_act(P, wk, wk, AF.Exp, scale=-1.0, outsl=a0, asl=a0)
    s_act(P, wk, wk, AF.Ln, bias=cfc(C, 1.0)[0:pn, :], outsl=a0, asl=a0, sbufs=[C.fc_sb.buf])
    v_ts(P, wk, wk, 0.0, ALU.max, outsl=m0, asl=v0)
    v_tt(P, out, wk, wk, ALU.add, outsl=outsl, asl=m0, bsl=a0)


def _sl64(sb, g, hp):
    return sb.t[hp * 64:(hp + 1) * 64, g * 128:(g + 1) * 128]


def phase_ssm(C, cs):
    P, A = C.P, C.A
    x_fm = C.hint("x_fm", (D, T), F32)
    xn_h = C.hint("xn", (D, T), BF16)
    win = C.hin("L1_ssm_winT", (D, 4736), F32)
    wout = C.hin("L1_ssm_woutT", (2048, D), F32)
    dtb = C.hin("L1_ssm_dtb", (32, 1), F32)
    Dp = C.hin("L1_ssm_D", (32, 1), F32)
    cE127 = C.hin("c_e127", (128, 1), F32)
    cSelH1 = C.hin("c_selH1", (32, 32 * 32), F32)
    cSelW = C.hin("c_selW", (32, 16 * 128), F32)
    cSelWb = C.hin("c_selW_bf", (32, 16 * 128), BF16)
    cSelR = C.hin("c_selR", (32, 64), F32)
    z_h = C.hint("ssm_z", (2048, T), BF16)
    xs_h = C.hint("ssm_xs", (2048, T), BF16)
    w_h = C.hint("ssm_w", (2048, T), BF16)
    xd_h = C.hint("ssm_xd", (2048, T), BF16)
    k1_h = C.hint("ssm_k1", (16, 64, 32 * 128), BF16)
    q1_h = C.hint("ssm_q1", (16, 64, 32 * 128), BF16)
    cE32 = C.hin("c_e32x64", (32, 32 * 64), F32)
    y2_h = C.hint("ssm_y2", (T, 2048), BF16)

    A.push()
    brest = A.sbuf("s_Br", [96, T], F32)
    crest = A.sbuf("s_Cr", [96, T], F32)
    DQ = A.sbuf("s_DQ", [128, 16 * 32], F32)
    DK = A.sbuf("s_DK", [128, 16 * 32], F32)
    DE = A.sbuf("s_DE", [128, 16 * 32], F32)
    lamR = A.sbuf("s_lamR", [128, 16 * 32], F32)
    e127 = A.sbuf("s_e127", [128, 1], F32)
    dtbc = A.sbuf("s_dtb", [32, 1], F32)
    Dc = A.sbuf("s_D", [32, 1], F32)
    for dst, srcht in [(e127, cE127), (dtbc, dtb), (Dc, Dp)]:
        P.dma("sync", dst.ap, srcht.ap, reads=[srcht.buf], writes=[dst.buf])

    A.push()
    bs = A.sbuf("s_B", [32, T], F32)
    cs_ = A.sbuf("s_C", [32, T], F32)
    dtv = A.sbuf("s_dtv", [32, T], F32)
    tah = A.sbuf("s_tah", [32, T], F32)
    cw1 = A.sbuf("s_cw1", [32, T], BF16)
    cw2 = A.sbuf("s_cw2", [32, T], BF16)

    A.push()
    sm = A.sbuf("s_sm", [128, T], F32)
    an = A.sbuf("s_an", [32, T], F32)
    ca = A.sbuf("s_ca", [32, T], F32)
    lav = A.sbuf("s_la", [32, T], F32)
    trv = A.sbuf("s_tr", [32, T], F32)
    wk = A.sbuf("s_wk", [32, 3 * T], F32)
    ca_tm = A.sbuf("s_catm", [128, 16 * 32], F32)
    tmpv = A.sbuf("s_tv", [128, 16 * 32], F32)
    ob = [A.sbuf(f"s_ob{i}", [128, 512], BF16) for i in range(4)]
    obi = [0]
    def evacP(wi, nt, t0, tw, ps):
        if nt < 32:
            k = obi[0] % 4
            obi[0] += 1
            dst = z_h if nt < 16 else xs_h
            r0 = (nt % 16) * 128
            v_copy(P, ob[k], ps, outsl=ob[k].t[:, 0:tw], asl=ps[0:128, 0:tw])
            P.dma("scalar", dst.ap[r0:r0 + 128, t0:t0 + tw], ob[k].t[:, 0:tw],
                  reads=[ob[k].buf], writes=[dst.buf])
        elif nt == 32:
            v_copy(P, bs, ps, outsl=bs.t[0:32, t0:t0 + tw], asl=ps[0:32, 0:tw])
            v_copy(P, cs_, ps, outsl=cs_.t[0:32, t0:t0 + tw], asl=ps[32:64, 0:tw])
        elif nt == 33:
            v_copy(P, brest, ps, outsl=brest.t[0:96, t0:t0 + tw], asl=ps[0:96, 0:tw])
        elif nt == 34:
            v_copy(P, crest, ps, outsl=crest.t[0:96, t0:t0 + tw], asl=ps[0:96, 0:tw])
        elif nt == 35:
            v_copy(P, sm, ps, outsl=sm.t[:, t0:t0 + tw], asl=ps[0:128, 0:tw])
        else:
            v_copy(P, an, ps, outsl=an.t[0:32, t0:t0 + tw], asl=ps[0:32, 0:tw])
    A.push()
    xn_sb = load_fm(C, "s_xn", xn_h, D, T, BF16)
    mm_xres(C, "sproj", win, D, 4736, xn_sb, evacP, thalf=1024, kg=4)
    A.pop()
    softplus32(P, C, dtv, dtv.t[0:32, 0:T], sm, sm.t[0:32, 0:T], wk, bias_col=dtbc.t[0:32, 0:1], bias_bufs=[dtbc.buf])
    softplus32(P, C, lav, lav.t[0:32, 0:T], sm, sm.t[32:64, 0:T], wk)
    v_ts(P, lav, lav, 1e-4, ALU.max)
    v_stt(P, lav, lav, -1.0, dtv, ALU.mult, ALU.mult)
    v_scan(P, ca, lav, lav, 0.0, ALU.add, ALU.bypass)
    s_act(P, trv, sm, AF.Sigmoid, asl=sm.t[64:96, 0:T])
    ela = wk.t[0:32, 0:T]
    s_act(P, wk, lav, AF.Exp, outsl=ela)
    t1_ = wk.t[0:32, T:2 * T]
    v_ts(P, wk, trv, -1.0, ALU.mult, 1.0, ALU.add, outsl=t1_)
    v_tt(P, wk, wk, dtv, ALU.mult, outsl=t1_, asl=t1_)
    v_tt(P, cw1, wk, wk, ALU.mult, asl=t1_, bsl=ela)
    v_tt(P, cw2, trv, dtv, ALU.mult)
    s_act(P, tah, an, AF.Tanh, asl=an.t[0:32, 0:T])
    A.push()
    ps_a = A.psum("s_psa")
    ps_b = A.psum("s_psb")
    for c in range(16):
        t_tr(P, ps_a[0:128, 0:32], ca.t[0:32, c * 128:(c + 1) * 128], cs.ident_f,
             reads=[ca.buf], writes=[ps_a.buf], pn=32)
        v_copy(P, ca_tm, ps_a, outsl=ca_tm.t[:, c * 32:(c + 1) * 32], asl=ps_a[0:128, 0:32])
    brow = A.sbuf("s_brow", [1, 32 + 16 * 32], F32)
    v_memset(P, brow, 0.0, outsl=brow.t[0:1, 0:32])
    t_mm(P, ps_b[0:1, 0:512], e127.t[0:128, 0:1], ca_tm.ap,
         start=True, stop=True, reads=[ca_tm.buf, e127.buf], writes=[ps_b.buf], inc=True)
    v_copy(P, brow, ps_b, outsl=brow.t[0:1, 32:32 + 512], asl=ps_b[0:1, 0:512])
    bcast_row(P, C, ps_a, brow, brow.t[0:1, 0:512], 512, cs.ones_row_f)
    v_tt(P, tmpv, ca_tm, ps_a, ALU.subtract, bsl=ps_a[0:128, 0:512])
    s_act(P, DQ, tmpv, AF.Exp, bias=cfc(C, LN1024)[0:128, :], sbufs=[C.fc_sb.buf])
    v_ts(P, tmpv, tmpv, -1.0, ALU.mult, 84.0, ALU.min)
    s_act(P, DK, tmpv, AF.Exp, bias=cfc(C, -LN1024)[0:128, :], sbufs=[C.fc_sb.buf])
    prevb = A.sbuf("s_prevb", [128, 16 * 32], F32)
    v_copy(P, prevb, ps_a, asl=ps_a[0:128, 0:512])
    bcast_row(P, C, ps_b, brow, brow.t[0:1, 32:32 + 512], 512, cs.ones_row_f)
    v_tt(P, tmpv, ps_b, ca_tm, ALU.subtract, asl=ps_b[0:128, 0:512])
    s_act(P, DE, tmpv, AF.Exp, bias=cfc(C, -LN1024)[0:128, :], sbufs=[C.fc_sb.buf])
    v_tt(P, tmpv, ps_b, prevb, ALU.subtract, asl=ps_b[0:128, 0:512])
    s_act(P, lamR, tmpv, AF.Exp)
    A.pop()
    A.pop()

    A.push()
    selw = A.sbuf("s_selw", [32, 16 * 128], F32)
    selwb = A.sbuf("s_selwb", [32, 16 * 128], BF16)
    P.dma("sync", selw.ap, cSelW.ap, reads=[cSelW.buf], writes=[selw.buf])
    P.dma("sync", selwb.ap, cSelWb.ap, reads=[cSelWb.buf], writes=[selwb.buf])
    xsb = A.sbuf("s_xsb", [128, T], BF16)
    xsh = A.sbuf("s_xsh", [128, T], BF16)
    cwb = A.sbuf("s_cwb", [128, 2 * 512], BF16)
    ut = A.sbuf("s_ut", [128, 512], F32)
    ut2 = A.sbuf("s_ut2", [128, 512], F32)
    wo_ = A.sbuf("s_wo", [128, T], BF16)
    xdt = A.sbuf("s_xdt", [128, T], BF16)
    ps_c = [A.psum(f"s_psc{i}") for i in range(2)]
    ps_d = A.psum("s_psd")
    for blk in range(16):
        P.dma("sync", xsb.ap, xs_h.ap[blk * 128:(blk + 1) * 128, 0:T], reads=[xs_h.buf], writes=[xsb.buf])
        v_memset(P, xsh, 0.0, outsl=xsh.t[:, 0:1])
        v_copy(P, xsh, xsb, outsl=xsh.t[:, 1:T], asl=xsb.t[:, 0:T - 1])
        t_mm(P, ps_d[0:128, 0:1], selw.t[0:32, blk * 128:(blk + 1) * 128], Dc.t[0:32, 0:1],
             start=True, stop=True, reads=[selw.buf, Dc.buf], writes=[ps_d.buf], inc=True)
        for tc in range(4):
            sl = slice(tc * 512, (tc + 1) * 512)
            t_mm(P, ps_c[0][0:128, 0:512], selwb.t[0:32, blk * 128:(blk + 1) * 128],
                 cw1.t[0:32, sl], start=True, stop=True,
                 reads=[selwb.buf, cw1.buf], writes=[ps_c[0].buf], inc=True)
            t_mm(P, ps_c[1][0:128, 0:512], selwb.t[0:32, blk * 128:(blk + 1) * 128],
                 cw2.t[0:32, sl], start=True, stop=True,
                 reads=[selwb.buf, cw2.buf], writes=[ps_c[1].buf], inc=True)
            v_copy(P, cwb, ps_c[0], outsl=cwb.t[:, 0:512], asl=ps_c[0][0:128, 0:512])
            v_copy(P, cwb, ps_c[1], outsl=cwb.t[:, 512:1024], asl=ps_c[1][0:128, 0:512])
            v_tt(P, ut, xsh, cwb, ALU.mult, outsl=ut.ap, asl=xsh.t[:, sl], bsl=cwb.t[:, 0:512])
            v_tt(P, ut2, xsb, cwb, ALU.mult, outsl=ut2.ap, asl=xsb.t[:, sl], bsl=cwb.t[:, 512:1024])
            v_tt(P, wo_, ut, ut2, ALU.add, outsl=wo_.t[:, sl], asl=ut.ap, bsl=ut2.ap)
        P.dma("scalar", w_h.ap[blk * 128:(blk + 1) * 128, 0:T], wo_.ap, reads=[wo_.buf], writes=[w_h.buf])
        v_ts(P, xdt, xsb, ps_d[0:128, 0:1], ALU.mult, sbufs=[ps_d.buf])
        P.dma("scalar", xd_h.ap[blk * 128:(blk + 1) * 128, 0:T], xdt.ap, reads=[xdt.buf], writes=[xd_h.buf])
    A.pop()

    A.push()
    selh = A.sbuf("s_selh", [32, 32 * 32], F32)
    P.dma("sync", selh.ap, cSelH1.ap, reads=[cSelH1.buf], writes=[selh.buf])
    rate = A.sbuf("s_rate", [32, 2 * T], F32)
    th = A.sbuf("s_th", [32, 2 * T], F32)
    cosv = A.sbuf("s_cos", [32, 2 * T], F32)
    sinv = A.sbuf("s_sin", [32, 2 * T], F32)
    yy64 = A.sbuf("s_yy", [32, 1024], F32)
    ki64 = A.sbuf("s_ki", [32, 1024], mybir.dt.int32)
    kf64 = A.sbuf("s_kf", [32, 1024], F32)
    kq = A.sbuf("s_kq", [32, 2 * T], BF16)
    qq = A.sbuf("s_qq", [32, 2 * T], BF16)
    ps_e = [A.psum(f"s_pse{i}") for i in range(2)]
    for g in range(16):
        for hp in range(2):
            h = 2 * g + hp
            for tc in range(4):
                sl = slice(tc * 512, (tc + 1) * 512)
                osl = slice(hp * T + tc * 512, hp * T + tc * 512 + 512)
                t_mm(P, ps_e[0][0:32, 0:512], selh.t[0:32, h * 32:(h + 1) * 32], dtv.t[0:32, sl],
                     start=True, stop=True, reads=[selh.buf, dtv.buf], writes=[ps_e[0].buf], inc=True)
                v_copy(P, rate, ps_e[0], outsl=rate.t[0:32, osl], asl=ps_e[0][0:32, 0:512])
                v_stt(P, rate, tah, float(np.pi), rate, ALU.mult, ALU.mult,
                      outsl=rate.t[0:32, osl], i0sl=tah.t[0:32, sl], i1sl=rate.t[0:32, osl])
        v_scan(P, th, rate, rate, 0.0, ALU.add, ALU.bypass, outsl=th.t[0:32, 0:2 * T],
               d0sl=rate.t[0:32, 0:2 * T], d1sl=rate.t[0:32, 0:2 * T])
        for qtr in range(4):
            hsl = slice(qtr * 1024, (qtr + 1) * 1024)
            trig_reduce(P, C, cosv, th, cosv.t[0:32, hsl], th.t[0:32, hsl], yy64, ki64, kf64,
                        phase_add=float(np.pi / 2))
            trig_reduce(P, C, sinv, th, sinv.t[0:32, hsl], th.t[0:32, hsl], yy64, ki64, kf64)
        s_act(P, cosv, cosv, AF.Sin, outsl=cosv.t[0:32, 0:2 * T], asl=cosv.t[0:32, 0:2 * T])
        s_act(P, sinv, sinv, AF.Sin, outsl=sinv.t[0:32, 0:2 * T], asl=sinv.t[0:32, 0:2 * T])
        for hp in range(2):
            h = 2 * g + hp
            hT = slice(hp * T, hp * T + T)
            v_tt(P, kq, bs, cosv, ALU.mult, outsl=kq.t[0:32, 0:T],
                 asl=bs.t[0:32, 0:T], bsl=cosv.t[0:32, hT])
            v_tt(P, kq, bs, sinv, ALU.mult, outsl=kq.t[0:32, T:2 * T],
                 asl=bs.t[0:32, 0:T], bsl=sinv.t[0:32, hT])
            v_tt(P, qq, cs_, cosv, ALU.mult, outsl=qq.t[0:32, 0:T],
                 asl=cs_.t[0:32, 0:T], bsl=cosv.t[0:32, hT])
            v_tt(P, qq, cs_, sinv, ALU.mult, outsl=qq.t[0:32, T:2 * T],
                 asl=cs_.t[0:32, 0:T], bsl=sinv.t[0:32, hT])
            for piece in range(2):
                P.dma("scalar", k1_h.ap[:, piece * 32:(piece + 1) * 32, h * 128:(h + 1) * 128]
                      .rearrange("c r t -> r c t"),
                      kq.t[0:32, piece * T:(piece + 1) * T].rearrange("r (c t) -> r c t", t=128),
                      reads=[kq.buf], writes=[k1_h.buf])
                P.dma("scalar", q1_h.ap[:, piece * 32:(piece + 1) * 32, h * 128:(h + 1) * 128]
                      .rearrange("c r t -> r c t"),
                      qq.t[0:32, piece * T:(piece + 1) * T].rearrange("r (c t) -> r c t", t=128),
                      reads=[qq.buf], writes=[q1_h.buf])
    A.pop()
    A.pop()

    A.push()
    e32 = A.sbuf("s_e32", [32, 2048], BF16)
    P.dma("gpsimd", e32.ap, cE32.ap, reads=[cE32.buf], writes=[e32.buf])
    stair4 = A.sbuf("s_st4", [128, 512], F32)
    for k in range(4):
        v_copy(P, stair4, cs.stair, outsl=stair4.t[:, k * 128:(k + 1) * 128],
               asl=cs.stair.t[:, 512:640])
    bbf = A.sbuf("s_bbf", [96, T], BF16)
    cbf = A.sbuf("s_cbf", [96, T], BF16)
    v_copy(P, bbf, brest)
    v_copy(P, cbf, crest)
    SaA = A.sbuf("s_SaA", [64, 2048], F32)
    SbA = A.sbuf("s_SbA", [96, 2048], F32)
    SaB = A.sbuf("s_SaB", [64, 2048], BF16)
    SbB = A.sbuf("s_SbB", [96, 2048], BF16)
    for t_ in (SaA, SbA, SaB, SbB):
        v_memset(P, t_, 0.0)
    k1c = [A.sbuf(f"s_k1c{i}", [64, 32 * 128], BF16) for i in range(2)]
    q1c = [A.sbuf(f"s_q1c{i}", [64, 32 * 128], BF16) for i in range(2)]
    k1tm = [A.sbuf(f"s_k1tm{i}", [128, 32 * 64], BF16) for i in range(2)]
    k2tm = [A.sbuf(f"s_k2tm{i}", [128, 96], BF16) for i in range(2)]
    wtm = [A.sbuf(f"s_wtm{i}", [128, 2048], BF16) for i in range(2)]
    ztm = [A.sbuf(f"s_ztm{i}", [128, 2048], BF16) for i in range(2)]
    xdtm = [A.sbuf(f"s_xdtm{i}", [128, 2048], BF16) for i in range(2)]
    zsl = A.sbuf("s_zsl", [128, 2048], BF16)
    wdk = A.sbuf("s_wdk", [128, 2048], BF16)
    wdkv = A.sbuf("s_wdkv", [128, 2048], BF16)
    eDQ = A.sbuf("s_eDQ", [128, 2048], BF16)
    eLam = A.sbuf("s_eLam", [128, 2048], BF16)
    shm = A.sbuf("s_shm", [128, 128], BF16)
    tmsb = [A.sbuf(f"s_tmsb{i}", [32, 128], BF16) for i in range(2)]
    gmb = [A.sbuf(f"s_gmb{i}", [128, 1024], BF16) for i in range(2)]
    yt1 = [A.sbuf(f"s_yt1{i}", [128, 512], F32) for i in range(2)]
    y2c = [A.sbuf(f"s_y2c{i}", [128, 2048], BF16) for i in range(2)]
    ps_ex = [A.psum(f"e_pex{i}") for i in range(2)]
    ps_sc = [A.psum(f"e_psc{i}") for i in range(2)]
    ps_py = A.psum("e_ppy")
    ps_sa = A.psum("e_psa")
    ps_sb = A.psum("e_psb")

    def grp_front(c, g8, kc, qc):
        gb = gmb[g8 % 2]
        for half in range(2):
            pb = ps_sc[half]
            for r in range(4):
                h = g8 * 8 + half * 4 + r
                t_mm(P, pb[0:128, r * 128:(r + 1) * 128], kc.t[0:64, h * 128:(h + 1) * 128],
                     qc.t[0:64, h * 128:(h + 1) * 128], start=(r == 0), stop=(r == 3),
                     reads=[kc.buf, qc.buf], writes=[pb.buf], inc=(r == 3))
            v_tt(P, gb, pb, stair4, ALU.mult, outsl=gb.t[:, half * 512:(half + 1) * 512],
                 asl=pb[0:128, 0:512], bsl=stair4.ap)

    def grp_back(c, g8, kc, qc, ktm, k2, xt, yo):
        gsl = slice(g8 * 512, (g8 + 1) * 512)
        gb = gmb[g8 % 2]
        yt = yt1[g8 % 2]
        t_mm(P, ps_py[0:128, 0:512], cbf.t[0:96, c * 128:(c + 1) * 128], SbB.t[0:96, gsl],
             start=True, stop=False, reads=[cbf.buf, SbB.buf], writes=[ps_py.buf], inc=False)
        t_mm(P, ps_py[0:128, 0:512], shm.ap, wdkv.t[:, gsl],
             start=False, stop=False, reads=[shm.buf, wdkv.buf], writes=[ps_py.buf], inc=False)
        for hl in range(8):
            h = g8 * 8 + hl
            rsl = slice(hl * 64, (hl + 1) * 64)
            t_mm(P, ps_py[0:128, rsl], gb.t[:, hl * 128:(hl + 1) * 128], wdkv.t[:, h * 64:(h + 1) * 64],
                 start=False, stop=False, reads=[gb.buf, wdkv.buf], writes=[ps_py.buf], inc=False)
            t_mm(P, ps_py[0:128, rsl], qc.t[0:64, h * 128:(h + 1) * 128], SaB.t[0:64, h * 64:(h + 1) * 64],
                 start=False, stop=(hl == 7), reads=[qc.buf, SaB.buf], writes=[ps_py.buf], inc=(hl == 7))
        for hl in range(8):
            h = g8 * 8 + hl
            t_mm(P, ps_sa[0:64, hl * 64:(hl + 1) * 64], ktm.t[:, h * 64:(h + 1) * 64],
                 wdk.t[:, h * 64:(h + 1) * 64], start=(hl == 0), stop=(hl == 7),
                 reads=[ktm.buf, wdk.buf], writes=[ps_sa.buf], inc=(hl == 7))
        t_mm(P, ps_sb[0:96, 0:512], k2.t[:, 0:96], wdk.t[:, gsl],
             start=True, stop=True, reads=[k2.buf, wdk.buf], writes=[ps_sb.buf], inc=True)
        v_tt(P, yt, ps_py, eDQ, ALU.mult, asl=ps_py[0:128, 0:512], bsl=eDQ.t[:, gsl])
        v_tt(P, yt, yt, xt, ALU.add, bsl=xt.t[:, gsl])
        v_tt(P, yo, yt, zsl, ALU.mult, outsl=yo.t[:, gsl], asl=yt.ap, bsl=zsl.t[:, gsl])
        v_tt(P, SaA, SaA, eLam, ALU.mult, outsl=SaA.t[0:64, gsl], asl=SaA.t[0:64, gsl],
             bsl=eLam.t[0:64, gsl])
        v_tt(P, SaA, SaA, ps_sa, ALU.add, outsl=SaA.t[0:64, gsl], asl=SaA.t[0:64, gsl],
             bsl=ps_sa[0:64, 0:512])
        v_copy(P, SaB, SaA, outsl=SaB.t[0:64, gsl], asl=SaA.t[0:64, gsl])
        v_tt(P, SbA, SbA, eLam, ALU.mult, outsl=SbA.t[0:96, gsl], asl=SbA.t[0:96, gsl],
             bsl=eLam.t[0:96, gsl])
        v_tt(P, SbA, SbA, ps_sb, ALU.add, outsl=SbA.t[0:96, gsl], asl=SbA.t[0:96, gsl],
             bsl=ps_sb[0:96, 0:512])
        v_copy(P, SbB, SbA, outsl=SbB.t[0:96, gsl], asl=SbA.t[0:96, gsl])

    for c in range(16):
        pp = c % 2
        csl = slice(c * 128, (c + 1) * 128)
        kc, qc, ktm, k2 = k1c[pp], q1c[pp], k1tm[pp], k2tm[pp]
        wt, zt, xt, yo = wtm[pp], ztm[pp], xdtm[pp], y2c[pp]
        P.dma("sync", kc.t[0:64, :], k1_h.ap[c, :, :], reads=[k1_h.buf], writes=[kc.buf])
        P.dma("sync", qc.t[0:64, :], q1_h.ap[c, :, :], reads=[q1_h.buf], writes=[qc.buf])
        ps_tb = A.psum("e_ptb", BF16)
        for h in range(32):
            t_tr(P, ps_tb[0:128, (h % 8) * 64:(h % 8 + 1) * 64],
                 kc.t[0:64, h * 128:(h + 1) * 128], cs.ident_bf,
                 reads=[kc.buf], writes=[ps_tb.buf], pn=64)
            v_copy(P, ktm, ps_tb, outsl=ktm.t[:, h * 64:(h + 1) * 64],
                   asl=ps_tb[0:128, (h % 8) * 64:(h % 8 + 1) * 64])
        for g in range(16):
            gg = slice(g * 128, (g + 1) * 128)
            P.dma("scalar", zt.t[:, gg], z_h.ap[gg, csl], reads=[z_h.buf], writes=[zt.buf], transpose=True)
            P.dma("scalar", xt.t[:, gg], xd_h.ap[gg, csl], reads=[xd_h.buf], writes=[xt.buf], transpose=True)
            P.dma("scalar", wt.t[:, gg], w_h.ap[gg, csl], reads=[w_h.buf], writes=[wt.buf], transpose=True)
        for q4 in range(4):
            sl = slice(q4 * 512, (q4 + 1) * 512)
            s_act(P, zsl, zt, AF.Silu, outsl=zsl.t[:, sl], asl=zt.t[:, sl])
        t_mm(P, ps_ex[1][0:128, 0:128], bbf.t[0:96, csl], cbf.t[0:96, csl],
             start=True, stop=True, reads=[bbf.buf, cbf.buf], writes=[ps_ex[1].buf], inc=True)
        v_tt(P, shm, ps_ex[1], stair4, ALU.mult, asl=ps_ex[1][0:128, 0:128],
             bsl=stair4.t[:, 0:128])
        t_tr(P, ps_ex[0][0:128, 0:96], brest.t[0:96, csl], cs.ident_f,
             reads=[brest.buf], writes=[ps_ex[0].buf], pn=96)
        v_copy(P, k2, ps_ex[0], outsl=k2.t[:, 0:96], asl=ps_ex[0][0:128, 0:96])
        cl = slice(c * 32, (c + 1) * 32)
        for i, (src, dst, fold) in enumerate([(DK, wdkv, wt), (DE, wdk, wt),
                                              (DQ, eDQ, None), (lamR, eLam, None)]):
            pt = ps_ex[i % 2]
            tms = tmsb[i % 2]
            t_tr(P, pt[0:32, 0:128], src.t[:, cl], cs.ident_f, reads=[src.buf],
                 writes=[pt.buf], pn=128)
            v_copy(P, tms, pt, outsl=tms.t[0:32, 0:128], asl=pt[0:32, 0:128])
            for q4 in range(4):
                sl = slice(q4 * 512, (q4 + 1) * 512)
                pe = (ps_ex[1], ps_sc[0], ps_sc[1], ps_py)[q4]
                t_mm(P, pe[0:128, 0:512], tms.t[0:32, 0:128], e32.t[0:32, sl],
                     start=True, stop=True, reads=[tms.buf, e32.buf], writes=[pe.buf], inc=True)
                if fold is not None:
                    v_tt(P, dst, fold, pe, ALU.mult, outsl=dst.t[:, sl], asl=fold.t[:, sl],
                         bsl=pe[0:128, 0:512])
                else:
                    v_copy(P, dst, pe, outsl=dst.t[:, sl], asl=pe[0:128, 0:512])
        PIPE = 1
        if PIPE:
            grp_front(c, 0, kc, qc)
            for g8 in range(4):
                if g8 < 3:
                    grp_front(c, g8 + 1, kc, qc)
                grp_back(c, g8, kc, qc, ktm, k2, xt, yo)
        else:
            for g8 in range(4):
                grp_front(c, g8, kc, qc)
                grp_back(c, g8, kc, qc, ktm, k2, xt, yo)
        P.dma("gpsimd", y2_h.ap[csl, :], yo.ap, reads=[yo.buf], writes=[y2_h.buf])
    A.pop()

    A.push()
    wb = A.sbuf("s_wob", [128, 16 * D], BF16)
    for kt in range(16):
        P.dma("gpsimd", wb.t[:, kt * D:(kt + 1) * D], wout.ap[kt * 128:(kt + 1) * 128, 0:D],
              reads=[wout.buf], writes=[wb.buf])
    y2f = A.sbuf("s_y2f", [128, 16 * 1024], BF16)
    ps_o = [A.psum(f"s_pso{i}") for i in range(4)]
    xt = [A.sbuf(f"s_xt{i}", [128, 512], F32) for i in range(4)]
    xo = [A.sbuf(f"s_xo{i}", [128, 512], F32) for i in range(4)]
    for sw in range(2):
        t0s = sw * 1024
        for kt in range(16):
            P.dma("sync", y2f.t[:, kt * 1024:(kt + 1) * 1024],
                  y2_h.ap[t0s:t0s + 1024, kt * 128:(kt + 1) * 128],
                  reads=[y2_h.buf], writes=[y2f.buf], transpose=True)
        for ng in range(4):
            grp = []
            for nt in range(2):
                for tci in range(2):
                    grp.append((ng * 2 + nt, tci, ps_o[len(grp)]))
            for kt in range(16):
                for gi, (nt, tci, ps) in enumerate(grp):
                    t_mm(P, ps[0:128, 0:512], wb.t[:, kt * D + nt * 128: kt * D + nt * 128 + 128],
                         y2f.t[:, kt * 1024 + tci * 512: kt * 1024 + tci * 512 + 512],
                         start=(kt == 0), stop=(kt == 15),
                         reads=[wb.buf, y2f.buf], writes=[ps.buf],
                         inc=(kt == 15 and gi == len(grp) - 1))
            for ki_, (nt, tci, ps) in enumerate(grp):
                k = ki_ % 4
                t0 = t0s + tci * 512
                P.dma("sync", xt[k].t[:, 0:512], x_fm.ap[nt * 128:(nt + 1) * 128, t0:t0 + 512],
                      reads=[x_fm.buf], writes=[xt[k].buf])
                v_tt(P, xo[k], ps, xt[k], ALU.add, outsl=xo[k].t[:, 0:512],
                     asl=ps[0:128, 0:512], bsl=xt[k].t[:, 0:512])
                P.dma("scalar", x_fm.ap[nt * 128:(nt + 1) * 128, t0:t0 + 512], xo[k].t[:, 0:512],
                      reads=[xo[k].buf], writes=[x_fm.buf])
    A.pop()
    A.pop()


def phase_moe(C, cs):
    P, A = C.P, C.A
    x_fm = C.hint("x_fm", (D, T), F32)
    xn_h = C.hint("xn", (D, T), BF16)
    gw = C.hin("L1_moe_gateT", (D, 8), F32)
    w13 = [C.hin(f"L1_moe_w13T_{e}", (D, 4096), F32) for e in range(8)]
    w2e = [C.hin(f"L1_moe_w2T_{e}", (2048, D), F32) for e in range(8)]
    sel8 = C.hin("c_sel8", (8, 8 * 128), F32)
    act_h = C.hint("moe_act", (2048, T), BF16)
    BIG = 1.0e9

    A.push()
    xn_sb = load_fm(C, "moe_xn", xn_h, D, T, BF16)
    comb8 = A.sbuf("moe_comb8", [8, T], F32)
    sel_sb = A.sbuf("moe_sel8", [8, 8 * 128], F32)
    P.dma("sync", sel_sb.ap, sel8.ap, reads=[sel8.buf], writes=[sel_sb.buf])
    A.push()
    gwb = A.sbuf("moe_gw", [128, 8 * 8], BF16)
    for kt in range(8):
        P.dma("gpsimd", gwb.t[:, kt * 8:(kt + 1) * 8], gw.ap[kt * 128:(kt + 1) * 128, 0:8],
              reads=[gw.buf], writes=[gwb.buf])
    ps_g = A.psum("moe_psg")
    ps_t = A.psum("moe_pst")
    rt = A.sbuf("moe_rt", [128, 8], F32)
    col = A.sbuf("moe_col", [128, 8], F32)
    g4 = A.sbuf("moe_g4", [128, 4], F32)
    e4 = A.sbuf("moe_e4", [128, 4], F32)
    sm8 = A.sbuf("moe_sm8", [128, 8], F32)
    for tt in range(16):
        for kt in range(8):
            t_mm(P, ps_g[0:128, 0:8], xn_sb.t[:, kt * T + tt * 128: kt * T + tt * 128 + 128],
                 gwb.t[:, kt * 8:(kt + 1) * 8], start=(kt == 0), stop=(kt == 7),
                 reads=[xn_sb.buf, gwb.buf], writes=[ps_g.buf], inc=(kt == 7))
        s_act(P, rt, ps_g, AF.Exp, outsl=rt.t[:, 0:8], asl=ps_g[0:128, 0:8])
        v_red(P, col, rt, ALU.add, outsl=col.t[:, 0:1], asl=rt.t[:, 0:8])
        v_recip(P, col, col, outsl=col.t[:, 1:2], asl=col.t[:, 0:1])
        v_ts(P, rt, rt, col.t[:, 1:2], ALU.mult, sbufs=[col.buf])
        v_tt(P, g4, rt, rt, ALU.max, outsl=g4.ap, asl=rt.t[:, 0:8:2], bsl=rt.t[:, 1:8:2])
        v_red(P, col, g4, ALU.max, outsl=col.t[:, 2:3], asl=g4.ap)
        v_ts(P, e4, g4, col.t[:, 2:3], ALU.is_equal, sbufs=[col.buf])
        v_stt(P, e4, e4, -BIG, g4, ALU.mult, ALU.add)
        v_red(P, col, e4, ALU.max, outsl=col.t[:, 3:4], asl=e4.ap)
        v_ts(P, g4, g4, col.t[:, 3:4], ALU.is_ge, sbufs=[col.buf])
        v_copy(P, sm8, g4, outsl=sm8.t[:, 0:8:2], asl=g4.ap)
        v_copy(P, sm8, g4, outsl=sm8.t[:, 1:8:2], asl=g4.ap)
        v_tt(P, rt, rt, sm8, ALU.mult)
        v_red(P, col, rt, ALU.max, outsl=col.t[:, 4:5], asl=rt.t[:, 0:8])
        v_ts(P, sm8, rt, col.t[:, 4:5], ALU.is_equal, sbufs=[col.buf])
        v_stt(P, sm8, sm8, -BIG, rt, ALU.mult, ALU.add)
        v_red(P, col, sm8, ALU.max, outsl=col.t[:, 5:6], asl=sm8.t[:, 0:8])
        v_ts(P, sm8, rt, col.t[:, 5:6], ALU.is_ge, sbufs=[col.buf])
        v_tt(P, col, col, col, ALU.add, outsl=col.t[:, 6:7], asl=col.t[:, 4:5], bsl=col.t[:, 5:6])
        v_recip(P, col, col, outsl=col.t[:, 7:8], asl=col.t[:, 6:7])
        v_tt(P, rt, rt, sm8, ALU.mult)
        v_ts(P, rt, rt, col.t[:, 7:8], ALU.mult, sbufs=[col.buf])
        t_tr(P, ps_t[0:8, 0:128], rt.t[:, 0:8], cs.ident_f, reads=[rt.buf], writes=[ps_t.buf])
        v_copy(P, comb8, ps_t, outsl=comb8.t[0:8, tt * 128:(tt + 1) * 128], asl=ps_t[0:8, 0:128])
    A.pop()

    acc = A.sbuf("moe_acc", [128, 8 * T], F32)
    for ft in range(8):
        v_memset(P, acc, 0.0, outsl=acc.t[:, ft * T:(ft + 1) * T])

    stg = [A.sbuf(f"moe_stg{i}", [128, 512], F32) for i in range(4)]
    ob = [A.sbuf(f"moe_ob{i}", [128, 512], BF16) for i in range(4)]
    tmp = [A.sbuf(f"moe_t{i}", [128, 512], F32) for i in range(4)]
    cbt = A.sbuf("moe_cb", [128, T], F32)
    for e in range(8):
        si = [0]
        def evac1(wi, nt, t0, tw, ps, _state={}):
            blk = nt // 2
            if nt % 2 == 0:
                k = si[0] % 4
                si[0] += 1
                _state[(blk, t0)] = k
                s_act(P, stg[k], ps, AF.Silu, outsl=stg[k].t[:, 0:tw], asl=ps[0:128, 0:tw])
            else:
                k = _state.pop((blk, t0))
                v_tt(P, ob[k], ps, stg[k], ALU.mult, outsl=ob[k].t[:, 0:tw],
                     asl=ps[0:128, 0:tw], bsl=stg[k].t[:, 0:tw])
                P.dma("scalar", act_h.ap[blk * 128:(blk + 1) * 128, t0:t0 + tw], ob[k].t[:, 0:tw],
                      reads=[ob[k].buf], writes=[act_h.buf])
        mm_xres(C, "moe1", w13[e], D, 4096, xn_sb, evac1, thalf=1024, kg=4)
        ps_b = A.psum("moe_psb")
        for tc in range(4):
            t_mm(P, ps_b[0:128, 0:512], sel_sb.t[0:8, e * 128:(e + 1) * 128],
                 comb8.t[0:8, tc * 512:(tc + 1) * 512], start=True, stop=True,
                 reads=[sel_sb.buf, comb8.buf], writes=[ps_b.buf], inc=True)
            v_copy(P, cbt, ps_b, outsl=cbt.t[:, tc * 512:(tc + 1) * 512], asl=ps_b[0:128, 0:512])
        ti = [0]
        def evac2(wi, nt, t0, tw, ps):
            k = ti[0] % 4
            ti[0] += 1
            v_tt(P, tmp[k], ps, cbt, ALU.mult, outsl=tmp[k].t[:, 0:tw],
                 asl=ps[0:128, 0:tw], bsl=cbt.t[:, t0:t0 + tw])
            v_tt(P, acc, acc, tmp[k], ALU.add,
                 outsl=acc.t[:, nt * T + t0: nt * T + t0 + tw],
                 asl=acc.t[:, nt * T + t0: nt * T + t0 + tw], bsl=tmp[k].t[:, 0:tw])
        mm_big(C, "moe2", [(w2e[e], D, None)], 2048, act_h, T, evac2, gt=1, gn=2, xdouble=False)

    xt2 = A.sbuf("moe_xt", [128, T], F32)
    for ft in range(8):
        P.dma("sync", xt2.ap, x_fm.ap[ft * 128:(ft + 1) * 128, 0:T], reads=[x_fm.buf], writes=[xt2.buf])
        v_tt(P, xt2, xt2, acc, ALU.add, asl=xt2.ap, bsl=acc.t[:, ft * T:(ft + 1) * T])
        P.dma("scalar", x_fm.ap[ft * 128:(ft + 1) * 128, 0:T], xt2.ap, reads=[xt2.buf], writes=[x_fm.buf])
    A.pop()



def phase_ffn3(C, cs):
    P, A = C.P, C.A
    x_fm = C.hint("x_fm", (D, T), F32)
    xn_h = C.hint("xn", (D, T), BF16)
    wr = C.hin("L3_ffn_W_rT", (D, D), F32)
    wk = C.hin("L3_ffn_W_kT", (D, D), F32)
    wv = C.hin("L3_ffn_W_vT", (D, D), F32)
    mus = C.hin("L3_ffn_mu_fm", (128, 16), F32)

    A.push()
    xr = A.sbuf("f3_xr", [128, 8 * T], BF16)
    xk = A.sbuf("f3_xk", [128, 8 * T], BF16)
    mu = A.sbuf("f3_mu", [128, 16], F32)
    P.dma("sync", mu.ap, mus.ap, reads=[mus.buf], writes=[mu.buf])
    A.push()
    xn_sb = load_fm(C, "f3_xn", xn_h, D, T, BF16)
    xsh = A.sbuf("f3_xsh", [128, 8 * T], BF16)
    dtmp = A.sbuf("f3_d", [128, T], F32)
    for ft in range(8):
        v_memset(P, xsh, 0.0, outsl=xsh.t[:, ft * T: ft * T + 1])
        v_copy(P, xsh, xn_sb, outsl=xsh.t[:, ft * T + 1:(ft + 1) * T],
               asl=xn_sb.t[:, ft * T:(ft + 1) * T - 1])
    for ft in range(8):
        v_tt(P, dtmp, xn_sb, xsh, ALU.subtract, outsl=dtmp.ap,
             asl=xn_sb.t[:, ft * T:(ft + 1) * T], bsl=xsh.t[:, ft * T:(ft + 1) * T])
        v_stt(P, xr, dtmp, mu.t[:, ft:ft + 1], xsh, ALU.mult, ALU.add,
              outsl=xr.t[:, ft * T:(ft + 1) * T], i0sl=dtmp.ap,
              i1sl=xsh.t[:, ft * T:(ft + 1) * T], sbufs=[mu.buf])
        v_stt(P, xk, dtmp, mu.t[:, 8 + ft:9 + ft], xsh, ALU.mult, ALU.add,
              outsl=xk.t[:, ft * T:(ft + 1) * T], i0sl=dtmp.ap,
              i1sl=xsh.t[:, ft * T:(ft + 1) * T], sbufs=[mu.buf])
    A.pop()
    r_sb = A.sbuf("f3_r", [128, 8 * T], BF16)
    ka = A.sbuf("f3_ka", [128, 8 * T], BF16)
    def evacR(wi, nt, t0, tw, ps):
        s_act(P, r_sb, ps, AF.Sigmoid, outsl=r_sb.t[:, nt * T + t0: nt * T + t0 + tw],
              asl=ps[0:128, 0:tw])
    mm_big(C, "f3r", [(wr, D, None)], D, None, T, evacR, x_resident=xr, gt=2, gn=2)
    relu_t = [A.sbuf(f"f3_rel{i}", [128, 512], F32) for i in range(4)]
    ri = [0]
    def evacK(wi, nt, t0, tw, ps):
        k = ri[0] % 4
        ri[0] += 1
        v_ts(P, relu_t[k], ps, 0.0, ALU.max, outsl=relu_t[k].t[:, 0:tw], asl=ps[0:128, 0:tw])
        s_act(P, ka, relu_t[k], AF.Square, outsl=ka.t[:, nt * T + t0: nt * T + t0 + tw],
              asl=relu_t[k].t[:, 0:tw])
    mm_big(C, "f3k", [(wk, D, None)], D, None, T, evacK, x_resident=xk, gt=2, gn=2)
    xt = [A.sbuf(f"f3_xt{i}", [128, 512], F32) for i in range(4)]
    xo = [A.sbuf(f"f3_xo{i}", [128, 512], F32) for i in range(4)]
    di = [0]
    def evacV(wi, nt, t0, tw, ps):
        k = di[0] % 4
        di[0] += 1
        P.dma("sync", xt[k].t[:, 0:tw], x_fm.ap[nt * 128:(nt + 1) * 128, t0:t0 + tw],
              reads=[x_fm.buf], writes=[xt[k].buf])
        v_tt(P, xo[k], ps, r_sb, ALU.mult, outsl=xo[k].t[:, 0:tw],
             asl=ps[0:128, 0:tw], bsl=r_sb.t[:, nt * T + t0: nt * T + t0 + tw])
        v_tt(P, xo[k], xo[k], xt[k], ALU.add, outsl=xo[k].t[:, 0:tw],
             asl=xo[k].t[:, 0:tw], bsl=xt[k].t[:, 0:tw])
        P.dma("scalar", x_fm.ap[nt * 128:(nt + 1) * 128, t0:t0 + tw], xo[k].t[:, 0:tw],
              reads=[xo[k].buf], writes=[x_fm.buf])
    mm_big(C, "f3v", [(wv, D, None)], D, None, T, evacV, x_resident=ka, gt=2, gn=2)
    A.pop()


TPAD = 64
TP_ = TPAD + T

def phase_attn3(C, cs):
    P, A = C.P, C.A
    x_fm = C.hint("x_fm", (D, T), F32)
    xn_h = C.hint("xn", (D, T), BF16)
    wq = C.hin("L3_wqT", (D, D), F32)
    wk = C.hin("L3_wkT", (D, D), F32)
    wv = C.hin("L3_wvT", (D, D), F32)
    wb_ = C.hin("L3_wbT", (D, 4), F32)
    wo = C.hin("L3_woT", (D, D), F32)
    convw = C.hin("L3_conv_fm", (128, 8 * 12), F32)
    firs = C.hin("L3_firs_fm", (128, 8 * 5), F32)
    firl = C.hin("L3_firl_fm", (128, 8 * 63), F32)
    gw1 = C.hin("L3_gw1T", (1088, D), F32)
    gb1 = C.hin("L3_gb1_fm", (128, 8), F32)
    gw2 = C.hin("L3_gw2T", (D, 16), F32)
    gb2 = C.hin("L3_gb2_col", (16, 1), F32)
    ltemp = C.hin("L3_attn_log_temp", (4, 4), F32)
    fpar = C.hin("L3_attn_floor_param", (4, 4), F32)
    onw = C.hin("L3_onw_fm", (128, 2), F32)
    stairL = C.hin("c_stairL", (128, 1536), BF16)
    sel16 = C.hin("c_sel16", (16, 16 * 128), F32)
    ps_hbm = C.hint("l3_ps", (D, T), BF16)
    pl_hbm = C.hint("l3_pl", (D, T), BF16)
    pd_hbm = C.hint("l3_pd", (D, T), BF16)
    gi_h = C.hint("l3_gate_in", (1088, T), BF16)
    sc_h = C.hint("l3_small", (1, 32), F32)

    A.push()
    stl = A.sbuf("a3_stl", [128, 1536], BF16)
    P.dma("sync", stl.ap, stairL.ap, reads=[stairL.buf], writes=[stl.buf])
    vpad = A.sbuf("a3_v", [128, 8 * TP_], BF16)
    qn = A.sbuf("a3_q", [128, 8 * T], BF16)
    kn = A.sbuf("a3_k", [128, 8 * T], BF16)
    beta_tm = A.sbuf("a3_btm", [128, 16 * 4], F32)
    wts = A.sbuf("a3_wts", [16, T], F32)

    A.push()
    xn_sb = load_fm(C, "a3_xn", xn_h, D, T, BF16)
    cwf = A.sbuf("a3_cw", [128, 8 * 12], F32)
    P.dma("sync", cwf.ap, convw.ap, reads=[convw.buf], writes=[cwf.buf])
    raw = A.sbuf("a3_raw", [128, 8 * TP_], BF16)
    diag = A.sbuf("a3_diag", [128, 4 * 128], BF16)
    wbq = A.sbuf("a3_wb", [128, 8 * 128], BF16)
    ps_pj = [A.psum(f"a3_pspj{i}") for i in range(2)]
    ps_cv = [A.psum(f"a3_pscv{i}") for i in range(2)]
    for si, (w_ht, dstname) in enumerate([(wq, "q"), (wk, "k"), (wv, "v")]):
        for ft in range(8):
            v_memset(P, raw, 0.0, outsl=raw.t[:, ft * TP_: ft * TP_ + TPAD])
        for nt in range(8):
            for kt in range(8):
                P.dma("gpsimd", wbq.t[:, kt * 128:(kt + 1) * 128],
                      w_ht.ap[kt * 128:(kt + 1) * 128, nt * 128:(nt + 1) * 128],
                      reads=[w_ht.buf], writes=[wbq.buf])
            for tc in range(4):
                ps = ps_pj[tc % 2]
                for kt in range(8):
                    t_mm(P, ps[0:128, 0:512], wbq.t[:, kt * 128:(kt + 1) * 128],
                         xn_sb.t[:, kt * T + tc * 512: kt * T + tc * 512 + 512],
                         start=(kt == 0), stop=(kt == 7),
                         reads=[wbq.buf, xn_sb.buf], writes=[ps.buf], inc=(kt == 7))
                v_copy(P, raw, ps,
                       outsl=raw.t[:, nt * TP_ + TPAD + tc * 512: nt * TP_ + TPAD + tc * 512 + 512],
                       asl=ps[0:128, 0:512])
        dst = {"q": qn, "k": kn, "v": vpad}[dstname]
        for pt in range(8):
            for tap in range(4):
                v_ts(P, diag, cs.ident_bf, cwf.t[:, pt * 12 + si * 4 + tap: pt * 12 + si * 4 + tap + 1],
                     ALU.mult, outsl=diag.t[:, tap * 128:(tap + 1) * 128], sbufs=[cwf.buf])
            for tc in range(4):
                ps = ps_cv[tc % 2]
                for tap in range(4):
                    off = pt * TP_ + TPAD + tc * 512 - 3 + tap
                    t_mm(P, ps[0:128, 0:512], diag.t[:, tap * 128:(tap + 1) * 128],
                         raw.t[:, off: off + 512],
                         start=(tap == 0), stop=(tap == 3),
                         reads=[diag.buf, raw.buf], writes=[ps.buf], inc=(tap == 3))
                if dstname == "v":
                    osl = dst.t[:, pt * TP_ + TPAD + tc * 512: pt * TP_ + TPAD + tc * 512 + 512]
                else:
                    osl = dst.t[:, pt * T + tc * 512: pt * T + tc * 512 + 512]
                s_act(P, dst, ps, AF.Silu, outsl=osl, asl=ps[0:128, 0:512])
    for ft in range(8):
        v_memset(P, vpad, 0.0, outsl=vpad.t[:, ft * TP_: ft * TP_ + TPAD])
    wbb = A.sbuf("a3_wbb", [128, 8 * 4], BF16)
    for kt in range(8):
        P.dma("gpsimd", wbb.t[:, kt * 4:(kt + 1) * 4], wb_.ap[kt * 128:(kt + 1) * 128, 0:4],
              reads=[wb_.buf], writes=[wbb.buf])
    brow = A.sbuf("a3_brow", [4, T], F32)
    for tc in range(4):
        ps = ps_pj[tc % 2]
        for kt in range(8):
            t_mm(P, ps[0:4, 0:512], wbb.t[:, kt * 4:(kt + 1) * 4],
                 xn_sb.t[:, kt * T + tc * 512: kt * T + tc * 512 + 512],
                 start=(kt == 0), stop=(kt == 7),
                 reads=[wbb.buf, xn_sb.buf], writes=[ps.buf], inc=(kt == 7))
        s_act(P, brow, ps, AF.Sigmoid, outsl=brow.t[0:4, tc * 512:(tc + 1) * 512], asl=ps[0:4, 0:512])
    for tt in range(16):
        ps = ps_cv[tt % 2]
        t_tr(P, ps[0:128, 0:4], brow.t[0:4, tt * 128:(tt + 1) * 128], cs.ident_f,
             reads=[brow.buf], writes=[ps.buf], pn=4)
        v_copy(P, beta_tm, ps, outsl=beta_tm.t[:, tt * 4:(tt + 1) * 4], asl=ps[0:128, 0:4])
    A.pop()

    A.push()
    sqt = A.sbuf("a3_sq", [128, 2 * 512], BF16)
    rows = A.sbuf("a3_rows", [1, 512], F32)
    btmp = A.sbuf("a3_bt", [128, 512], F32)
    ps_r = A.psum("a3_psr")
    ps_bc = A.psum("a3_psbc")
    for tens in (qn, kn):
        for h in range(4):
            for tc in range(4):
                for pt in range(2):
                    blk = h * 2 + pt
                    s_act(P, sqt, tens, AF.Square, outsl=sqt.t[:, pt * 512:(pt + 1) * 512],
                          asl=tens.t[:, blk * T + tc * 512: blk * T + tc * 512 + 512])
                for pt in range(2):
                    t_mm(P, ps_r[0:1, 0:512], cs.ones_col.t[0:128, 0:1],
                         sqt.t[:, pt * 512:(pt + 1) * 512],
                         start=(pt == 0), stop=(pt == 1),
                         reads=[sqt.buf, cs.ones_col.buf], writes=[ps_r.buf], inc=(pt == 1))
                r0 = rows.t[0:1, 0:512]
                s_act(P, rows, ps_r, AF.Ln, bias=cfc(C, 1e-6)[0:1, :], outsl=r0,
                      asl=ps_r[0:1, 0:512], sbufs=[C.fc_sb.buf])
                s_act(P, rows, rows, AF.Exp, scale=-0.5, outsl=r0, asl=r0)
                bcast_row(P, C, ps_bc, rows, r0, 512, cs.ones_row_f)
                v_copy(P, btmp, ps_bc, asl=ps_bc[0:128, 0:512])
                for pt in range(2):
                    blk = h * 2 + pt
                    sl2 = slice(blk * T + tc * 512, blk * T + tc * 512 + 512)
                    v_tt(P, tens, tens, btmp, ALU.mult, outsl=tens.t[:, sl2],
                         asl=tens.t[:, sl2], bsl=btmp.t[:, 0:512])
    A.pop()

    A.push()
    fsw = A.sbuf("a3_fsw", [128, 8 * 5], F32)
    flw = A.sbuf("a3_flw", [128, 8 * 63], F32)
    P.dma("sync", fsw.ap, firs.ap, reads=[firs.buf], writes=[fsw.buf])
    P.dma("sync", flw.ap, firl.ap, reads=[firl.buf], writes=[flw.buf])
    dg = A.sbuf("a3_dg", [128, 63 * 128], BF16)
    fob = [A.sbuf(f"a3_fob{i}", [128, 512], BF16) for i in range(4)]
    ps_f = [A.psum(f"a3_psf{i}") for i in range(2)]
    fi = [0]
    for (wsb, ntap, dsth) in [(fsw, 5, ps_hbm), (flw, 63, pl_hbm)]:
        for pt in range(8):
            for tap in range(ntap):
                v_ts(P, dg, cs.ident_bf, wsb.t[:, pt * ntap + tap: pt * ntap + tap + 1],
                     ALU.mult, outsl=dg.t[:, tap * 128:(tap + 1) * 128], sbufs=[wsb.buf])
            for tc in range(4):
                ps = ps_f[tc % 2]
                for tap in range(ntap):
                    off = pt * TP_ + TPAD + tc * 512 - (ntap - 1) + tap
                    t_mm(P, ps[0:128, 0:512], dg.t[:, tap * 128:(tap + 1) * 128],
                         vpad.t[:, off: off + 512],
                         start=(tap == 0), stop=(tap == ntap - 1),
                         reads=[dg.buf, vpad.buf], writes=[ps.buf], inc=(tap == ntap - 1))
                k = fi[0] % 4
                fi[0] += 1
                v_copy(P, fob[k], ps, outsl=fob[k].t[:, 0:512], asl=ps[0:128, 0:512])
                P.dma("scalar", dsth.ap[pt * 128:(pt + 1) * 128, tc * 512:(tc + 1) * 512],
                      fob[k].t[:, 0:512], reads=[fob[k].buf], writes=[dsth.buf])
    A.pop()

    A.push()
    SstH = [[A.sbuf(f"a3_S{st}_{hh}", [128, 256], F32) for hh in range(4)] for st in range(2)]
    for st in range(2):
        for hh in range(4):
            v_memset(P, SstH[st][hh], 0.0)
    ktm_r = [A.sbuf(f"a3_ktm{i}", [128, 256], BF16) for i in range(2)]
    vbtm_r = [A.sbuf(f"a3_vbtm{i}", [128, 256], BF16) for i in range(2)]
    kbtm_r = [A.sbuf(f"a3_kbtm{i}", [128, 256], BF16) for i in range(2)]
    Amat_r = [A.sbuf(f"a3_A{i}", [128, 128], BF16) for i in range(2)]
    AmatT_r = [A.sbuf(f"a3_AT{i}", [128, 128], BF16) for i in range(2)]
    Bk_r = [A.sbuf(f"a3_Bk{i}", [128, 6 * 128], BF16) for i in range(2)]
    BkT_r = [A.sbuf(f"a3_BkT{i}", [128, 6 * 128], BF16) for i in range(2)]
    B1T_r = [A.sbuf(f"a3_B1T{i}", [128, 128], BF16) for i in range(2)]
    PpT_r = [A.sbuf(f"a3_PT{i}", [128, 128], BF16) for i in range(2)]
    utm_r = [A.sbuf(f"a3_u{i}", [128, 256], BF16) for i in range(2)]
    uadj_r = [A.sbuf(f"a3_ua{i}", [128, 256], BF16) for i in range(2)]
    otm_r = [A.sbuf(f"a3_otm{i}", [128, 256], BF16) for i in range(2)]
    attnT_r = [A.sbuf(f"a3_at{i}", [128, 128], BF16) for i in range(2)]
    pdev_r = [A.sbuf(f"a3_pdev{i}", [128, 128], BF16) for i in range(2)]
    wF_r = [A.sbuf(f"a3_wF{i}", [128, 256], F32) for i in range(2)]
    qf32_r = [A.sbuf(f"a3_qf32{i}", [128, 256], F32) for i in range(2)]
    ps_1 = A.psum("a3_ps1")
    ps_2 = A.psum("a3_ps2")
    ps_3 = A.psum("a3_ps3", BF16)
    ps_o = A.psum("a3_pso")
    ps_su = A.psum("a3_pssu")
    for c in range(16):
        for h in range(4):
            pp = (c * 4 + h) % 2
            ktm = ktm_r[pp]; vbtm = vbtm_r[pp]; kbtm = kbtm_r[pp]
            Amat = Amat_r[pp]; AmatT = AmatT_r[pp]; Bk = Bk_r[pp]; BkT = BkT_r[pp]
            B1T = B1T_r[pp]; PpT = PpT_r[pp]; utm = utm_r[pp]; uadj = uadj_r[pp]
            otm = otm_r[pp]; attnT = attnT_r[pp]; pdev = pdev_r[pp]
            wF = wF_r[pp]; qf32 = qf32_r[pp]
            bcol = beta_tm.t[:, c * 4 + h: c * 4 + h + 1]
            for pt in range(2):
                blk = h * 2 + pt
                t_tr(P, ps_3[0:128, 0:128], kn.t[:, blk * T + c * 128: blk * T + c * 128 + 128],
                     cs.ident_bf, reads=[kn.buf], writes=[ps_3.buf])
                v_copy(P, ktm, ps_3, outsl=ktm.t[:, pt * 128:(pt + 1) * 128], asl=ps_3[0:128, 0:128])
                t_tr(P, ps_3[0:128, 0:128],
                     vpad.t[:, blk * TP_ + TPAD + c * 128: blk * TP_ + TPAD + c * 128 + 128],
                     cs.ident_bf, reads=[vpad.buf], writes=[ps_3.buf])
                v_copy(P, vbtm, ps_3, outsl=vbtm.t[:, pt * 128:(pt + 1) * 128], asl=ps_3[0:128, 0:128])
            v_ts(P, kbtm, ktm, bcol, ALU.mult, sbufs=[beta_tm.buf])
            v_ts(P, vbtm, vbtm, bcol, ALU.mult, sbufs=[beta_tm.buf])
            for pt in range(2):
                blk = h * 2 + pt
                ksl = kn.t[:, blk * T + c * 128: blk * T + c * 128 + 128]
                t_mm(P, ps_1[0:128, 0:128], ksl, ksl, start=(pt == 0), stop=(pt == 1),
                     reads=[kn.buf], writes=[ps_1.buf], inc=(pt == 1))
            v_ts(P, Amat, ps_1, bcol, ALU.mult, -1.0, ALU.mult, asl=ps_1[0:128, 0:128],
                 sbufs=[beta_tm.buf])
            v_tt(P, Amat, Amat, stl, ALU.mult, bsl=stl.t[:, 513:641])
            t_tr(P, ps_3[0:128, 0:128], Amat.ap, cs.ident_bf, reads=[Amat.buf], writes=[ps_3.buf])
            v_copy(P, AmatT, ps_3, asl=ps_3[0:128, 0:128])
            for sq in range(6):
                lT = AmatT.ap if sq == 0 else BkT.t[:, (sq - 1) * 128: sq * 128]
                lv = Amat.ap if sq == 0 else Bk.t[:, (sq - 1) * 128: sq * 128]
                t_mm(P, ps_1[0:128, 0:128], lT, lv, start=True, stop=True,
                     reads=[Amat.buf, AmatT.buf, Bk.buf, BkT.buf], writes=[ps_1.buf], inc=True)
                v_copy(P, Bk, ps_1, outsl=Bk.t[:, sq * 128:(sq + 1) * 128], asl=ps_1[0:128, 0:128])
                t_mm(P, ps_2[0:128, 0:128], lv, lT, start=True, stop=True,
                     reads=[Amat.buf, AmatT.buf, Bk.buf, BkT.buf], writes=[ps_2.buf], inc=True)
                v_copy(P, BkT, ps_2, outsl=BkT.t[:, sq * 128:(sq + 1) * 128], asl=ps_2[0:128, 0:128])
            v_tt(P, B1T, AmatT, cs.ident_bf, ALU.add)
            for sq in range(6):
                v_tt(P, Bk, Bk, cs.ident_bf, ALU.add, outsl=Bk.t[:, sq * 128:(sq + 1) * 128],
                     asl=Bk.t[:, sq * 128:(sq + 1) * 128])
            v_copy(P, PpT, B1T)
            for sq in range(6):
                t_mm(P, ps_2[0:128, 0:128], Bk.t[:, sq * 128:(sq + 1) * 128], PpT.ap,
                     start=True, stop=True, reads=[PpT.buf, Bk.buf], writes=[ps_2.buf], inc=True)
                v_copy(P, PpT, ps_2, asl=ps_2[0:128, 0:128])
            t_mm(P, ps_1[0:128, 0:256], PpT.ap, vbtm.ap, start=True, stop=True,
                 reads=[PpT.buf, vbtm.buf], writes=[ps_1.buf], inc=True)
            v_copy(P, utm, ps_1, asl=ps_1[0:128, 0:256])
            for mt in range(2):
                t_mm(P, ps_2[0:128, mt * 128:(mt + 1) * 128],
                     kbtm.t[:, mt * 128:(mt + 1) * 128], PpT.ap,
                     start=(mt == 0), stop=(mt == 1),
                     reads=[kbtm.buf, PpT.buf], writes=[ps_2.buf], inc=(mt == 1))
            v_copy(P, wF, ps_2, asl=ps_2[0:128, 0:256])
            for st in range(2):
                t_mm(P, ps_1[0:128, 0:256], wF.t[:, st * 128:(st + 1) * 128],
                     SstH[st][h].t[:, 0:256],
                     start=(st == 0), stop=(st == 1),
                     reads=[wF.buf, SstH[st][h].buf], writes=[ps_1.buf], inc=(st == 1))
            v_tt(P, uadj, utm, ps_1, ALU.subtract, asl=utm.ap, bsl=ps_1[0:128, 0:256])
            for pt in range(2):
                blk = h * 2 + pt
                t_mm(P, ps_2[0:128, 0:128], kn.t[:, blk * T + c * 128: blk * T + c * 128 + 128],
                     qn.t[:, blk * T + c * 128: blk * T + c * 128 + 128],
                     start=(pt == 0), stop=(pt == 1),
                     reads=[kn.buf, qn.buf], writes=[ps_2.buf], inc=(pt == 1))
            v_copy(P, attnT, ps_2, asl=ps_2[0:128, 0:128])
            v_tt(P, attnT, attnT, cs.stair, ALU.mult, bsl=cs.stair.t[:, 512:640])
            for st in range(2):
                blk = h * 2 + st
                v_copy(P, qf32, qn, outsl=qf32.t[:, st * 128:(st + 1) * 128],
                       asl=qn.t[:, blk * T + c * 128: blk * T + c * 128 + 128])
            t_mm(P, ps_o[0:128, 0:256], attnT.ap, uadj.ap, start=True, stop=False,
                 reads=[attnT.buf, uadj.buf], writes=[ps_o.buf], inc=False)
            for st in range(2):
                t_mm(P, ps_o[0:128, 0:256], qf32.t[:, st * 128:(st + 1) * 128],
                     SstH[st][h].t[:, 0:256],
                     start=False, stop=(st == 1),
                     reads=[qf32.buf, SstH[st][h].buf], writes=[ps_o.buf], inc=(st == 1))
            v_copy(P, otm, ps_o, asl=ps_o[0:128, 0:256])
            for st in range(2):
                t_mm(P, ps_su[0:128, 0:256], ktm.t[:, st * 128:(st + 1) * 128], uadj.ap,
                     start=True, stop=True, reads=[ktm.buf, uadj.buf], writes=[ps_su.buf], inc=True)
                v_tt(P, SstH[st][h], SstH[st][h], ps_su, ALU.add,
                     outsl=SstH[st][h].t[:, 0:256],
                     asl=SstH[st][h].t[:, 0:256], bsl=ps_su[0:128, 0:256])
            for pt in range(2):
                blk = h * 2 + pt
                t_tr(P, ps_3[0:128, 0:128], otm.t[:, pt * 128:(pt + 1) * 128], cs.ident_bf,
                     reads=[otm.buf], writes=[ps_3.buf])
                v_copy(P, pdev, ps_3, outsl=pdev.t[:, 0:128], asl=ps_3[0:128, 0:128])
                P.dma("scalar", pd_hbm.ap[blk * 128:(blk + 1) * 128, c * 128:(c + 1) * 128],
                      pdev.t[:, 0:128], reads=[pdev.buf], writes=[pd_hbm.buf])
    A.pop()

    P.dma("sync", gi_h.ap[0:1024, 0:T].rearrange("(a p) t -> a p t", p=128),
          xn_h.ap.rearrange("(a p) t -> a p t", p=128), reads=[xn_h.buf], writes=[gi_h.buf])
    A.push()
    pblk = A.sbuf("a3_pblk", [128, 2 * T], BF16)
    sq2 = A.sbuf("a3_sq2", [128, 2 * T], BF16)
    ab2 = A.sbuf("a3_ab2", [128, 2 * T], BF16)
    statr = A.sbuf("a3_statr", [1, T], F32)
    statb = A.sbuf("a3_statb", [1, T], BF16)
    mrow = A.sbuf("a3_mrow", [1, T], F32)
    m2row = A.sbuf("a3_m2row", [1, T], F32)
    ps_s = [A.psum(f"a3_pss{i}") for i in range(3)]
    for pi in range(4):
        for h in range(4):
            if pi in (0, 1, 2):
                hsrc = ps_hbm if pi == 0 else (pl_hbm if pi == 1 else pd_hbm)
                for pt in range(2):
                    blk = h * 2 + pt
                    P.dma("sync", pblk.t[:, pt * T:(pt + 1) * T],
                          hsrc.ap[blk * 128:(blk + 1) * 128, 0:T],
                          reads=[hsrc.buf], writes=[pblk.buf])
            else:
                for pt in range(2):
                    blk = h * 2 + pt
                    v_copy(P, pblk, vpad, outsl=pblk.t[:, pt * T:(pt + 1) * T],
                           asl=vpad.t[:, blk * TP_ + TPAD: blk * TP_ + TPAD + T])
            for pt in range(2):
                s_act(P, sq2, pblk, AF.Square, outsl=sq2.t[:, pt * T:(pt + 1) * T],
                      asl=pblk.t[:, pt * T:(pt + 1) * T])
                s_act(P, ab2, pblk, AF.Abs, outsl=ab2.t[:, pt * T:(pt + 1) * T],
                      asl=pblk.t[:, pt * T:(pt + 1) * T])
            for tc in range(4):
                for pt in range(2):
                    co = pt * T + tc * 512
                    t_mm(P, ps_s[0][0:1, 0:512],
                         cs.ones_col.t[0:128, 0:1], pblk.t[:, co: co + 512],
                         start=(pt == 0), stop=(pt == 1),
                         reads=[pblk.buf, cs.ones_col.buf], writes=[ps_s[0].buf], inc=(pt == 1))
                    t_mm(P, ps_s[1][0:1, 0:512], cs.ones_col.t[0:128, 0:1], sq2.t[:, co: co + 512],
                         start=(pt == 0), stop=(pt == 1),
                         reads=[sq2.buf, cs.ones_col.buf], writes=[ps_s[1].buf], inc=(pt == 1))
                    t_mm(P, ps_s[2][0:1, 0:512], cs.ones_col.t[0:128, 0:1], ab2.t[:, co: co + 512],
                         start=(pt == 0), stop=(pt == 1),
                         reads=[ab2.buf, cs.ones_col.buf], writes=[ps_s[2].buf], inc=(pt == 1))
                tsl = slice(tc * 512, (tc + 1) * 512)
                v_ts(P, mrow, ps_s[0], 1.0 / 256, ALU.mult, outsl=mrow.t[0:1, tsl], asl=ps_s[0][0:1, 0:512])
                v_ts(P, statr, ps_s[1], 1.0 / 256, ALU.mult, outsl=statr.t[0:1, tsl], asl=ps_s[1][0:1, 0:512])
                v_tt(P, m2row, mrow, mrow, ALU.mult, outsl=m2row.t[0:1, tsl],
                     asl=mrow.t[0:1, tsl], bsl=mrow.t[0:1, tsl])
                v_tt(P, statr, statr, m2row, ALU.subtract, outsl=statr.t[0:1, tsl],
                     asl=statr.t[0:1, tsl], bsl=m2row.t[0:1, tsl])
            rbase = 1024 + pi * 16 + h * 4
            v_copy(P, statb, mrow)
            P.dma("scalar", gi_h.ap[rbase + 0: rbase + 1, 0:T], statb.t[0:1, 0:T],
                  reads=[statb.buf], writes=[gi_h.buf])
            v_copy(P, statb, statr)
            P.dma("scalar", gi_h.ap[rbase + 1: rbase + 2, 0:T], statb.t[0:1, 0:T],
                  reads=[statb.buf], writes=[gi_h.buf])
            for tc in range(4):
                for pt in range(2):
                    co = pt * T + tc * 512
                    t_mm(P, ps_s[2][0:1, 0:512], cs.ones_col.t[0:128, 0:1], ab2.t[:, co: co + 512],
                         start=(pt == 0), stop=(pt == 1),
                         reads=[ab2.buf, cs.ones_col.buf], writes=[ps_s[2].buf], inc=(pt == 1))
                    t_mm(P, ps_s[1][0:1, 0:512], cs.ones_col.t[0:128, 0:1], sq2.t[:, co: co + 512],
                         start=(pt == 0), stop=(pt == 1),
                         reads=[sq2.buf, cs.ones_col.buf], writes=[ps_s[1].buf], inc=(pt == 1))
                tsl = slice(tc * 512, (tc + 1) * 512)
                v_ts(P, mrow, ps_s[2], 1.0 / 256, ALU.mult, outsl=mrow.t[0:1, tsl], asl=ps_s[2][0:1, 0:512])
                s_act(P, statr, ps_s[1], AF.Ln, outsl=statr.t[0:1, tsl], asl=ps_s[1][0:1, 0:512])
                s_act(P, statr, statr, AF.Exp, scale=0.5, outsl=statr.t[0:1, tsl], asl=statr.t[0:1, tsl])
            v_copy(P, statb, mrow)
            P.dma("scalar", gi_h.ap[rbase + 2: rbase + 3, 0:T], statb.t[0:1, 0:T],
                  reads=[statb.buf], writes=[gi_h.buf])
            v_copy(P, statb, statr)
            P.dma("scalar", gi_h.ap[rbase + 3: rbase + 4, 0:T], statb.t[0:1, 0:T],
                  reads=[statb.buf], writes=[gi_h.buf])
    A.pop()

    A.push()
    sm4 = A.sbuf("a3_sm4", [4, 4], F32)
    sm4b = A.sbuf("a3_sm4b", [4, 4], F32)
    wk4 = A.sbuf("a3_wk4", [4, 3 * 4], F32)
    P.dma("sync", sm4.ap, ltemp.ap, reads=[ltemp.buf], writes=[sm4.buf])
    softplus32(P, C, sm4b, sm4b.t[0:4, 0:4], sm4, sm4.t[0:4, 0:4], wk4, pn=4, width=4)
    v_ts(P, sm4b, sm4b, 0.25, ALU.add)
    v_recip(P, sm4b, sm4b)
    P.dma("scalar", sc_h.ap[0:1, 0:16].rearrange("a (h p) -> (a h) p", h=4),
          sm4b.t[0:4, 0:4], reads=[sm4b.buf], writes=[sc_h.buf])
    P.dma("sync", sm4.ap, fpar.ap, reads=[fpar.buf], writes=[sm4.buf])
    s_act(P, sm4b, sm4, AF.Sigmoid)
    v_ts(P, sm4b, sm4b, 0.07, ALU.mult, 0.005, ALU.add)
    v_ts(P, sm4b, sm4b, 0.25, ALU.min)
    v_red(P, wk4, sm4b, ALU.add, outsl=wk4.t[0:4, 0:1], asl=sm4b.t[0:4, 0:4])
    v_ts(P, wk4, wk4, 0.99, ALU.min, outsl=wk4.t[0:4, 1:2], asl=wk4.t[0:4, 0:1])
    v_recip(P, wk4, wk4, outsl=wk4.t[0:4, 2:3], asl=wk4.t[0:4, 1:2])
    v_ts(P, sm4b, sm4b, wk4.t[0:4, 2:3], ALU.mult, sbufs=[wk4.buf])
    v_ts(P, sm4b, sm4b, 0.99, ALU.mult, 1e-9, ALU.add)
    P.dma("scalar", sc_h.ap[0:1, 16:32].rearrange("a (h p) -> (a h) p", h=4),
          sm4b.t[0:4, 0:4], reads=[sm4b.buf], writes=[sc_h.buf])
    A.pop()
    A.push()
    gwb = A.sbuf("a3_gwb", [128, 9 * D], BF16)
    for kt in range(8):
        P.dma("gpsimd", gwb.t[:, kt * D:(kt + 1) * D], gw1.ap[kt * 128:(kt + 1) * 128, 0:D],
              reads=[gw1.buf], writes=[gwb.buf])
    P.dma("gpsimd", gwb.t[0:64, 8 * D:9 * D], gw1.ap[1024:1088, 0:D], reads=[gw1.buf], writes=[gwb.buf])
    gb1s = A.sbuf("a3_gb1", [128, 8], F32)
    P.dma("sync", gb1s.ap, gb1.ap, reads=[gb1.buf], writes=[gb1s.buf])
    gxb = A.sbuf("a3_gxb", [128, 9 * 1024], BF16)
    h1g = A.sbuf("a3_h1g", [128, 8 * T], BF16)
    ps_gm = [A.psum(f"a3_psgm{i}") for i in range(4)]
    for sw in range(2):
        t0s = sw * 1024
        for kt in range(8):
            P.dma("sync", gxb.t[:, kt * 1024:(kt + 1) * 1024],
                  gi_h.ap[kt * 128:(kt + 1) * 128, t0s:t0s + 1024],
                  reads=[gi_h.buf], writes=[gxb.buf])
        P.dma("sync", gxb.t[0:64, 8 * 1024:9 * 1024], gi_h.ap[1024:1088, t0s:t0s + 1024],
              reads=[gi_h.buf], writes=[gxb.buf])
        for ng in range(4):
            grp = [(ng * 2 + nt, tci, ps_gm[nt * 2 + tci]) for nt in range(2) for tci in range(2)]
            for kt in range(9):
                for gi_, (nt, tci, ps) in enumerate(grp):
                    lh = gwb.t[:, kt * D + nt * 128: kt * D + nt * 128 + 128] if kt < 8 else \
                         gwb.t[0:64, kt * D + nt * 128: kt * D + nt * 128 + 128]
                    rh = gxb.t[:, kt * 1024 + tci * 512: kt * 1024 + tci * 512 + 512] if kt < 8 else \
                         gxb.t[0:64, kt * 1024 + tci * 512: kt * 1024 + tci * 512 + 512]
                    t_mm(P, ps[0:128, 0:512], lh, rh,
                         start=(kt == 0), stop=(kt == 8),
                         reads=[gwb.buf, gxb.buf], writes=[ps.buf],
                         inc=(kt == 8 and gi_ == len(grp) - 1))
            for (nt, tci, ps) in grp:
                t0 = t0s + tci * 512
                s_act(P, h1g, ps, AF.Gelu, bias=gb1s.t[:, nt: nt + 1],
                      outsl=h1g.t[:, nt * T + t0: nt * T + t0 + 512],
                      asl=ps[0:128, 0:512], sbufs=[gb1s.buf])
    gw2b = A.sbuf("a3_gw2b", [128, 8 * 16], BF16)
    for kt in range(8):
        P.dma("gpsimd", gw2b.t[:, kt * 16:(kt + 1) * 16], gw2.ap[kt * 128:(kt + 1) * 128, 0:16],
              reads=[gw2.buf], writes=[gw2b.buf])
    gb2s = A.sbuf("a3_gb2", [16, 1], F32)
    P.dma("sync", gb2s.ap, gb2.ap, reads=[gb2.buf], writes=[gb2s.buf])
    logits = A.sbuf("a3_lg", [16, T], F32)
    for tc in range(4):
        ps = ps_gm[tc % 2]
        for kt in range(8):
            t_mm(P, ps[0:16, 0:512], gw2b.t[:, kt * 16:(kt + 1) * 16],
                 h1g.t[:, kt * T + tc * 512: kt * T + tc * 512 + 512],
                 start=(kt == 0), stop=(kt == 7),
                 reads=[gw2b.buf, h1g.buf], writes=[ps.buf], inc=(kt == 7))
        v_ts(P, logits, ps, gb2s.t[0:16, 0:1], ALU.add, outsl=logits.t[0:16, tc * 512:(tc + 1) * 512],
             asl=ps[0:16, 0:512], sbufs=[gb2s.buf])
    taucol = A.sbuf("a3_tau", [16, 1], F32)
    P.dma("sync", taucol.ap, sc_h.ap[0:1, 0:16].rearrange("a b -> b a"), reads=[sc_h.buf], writes=[taucol.buf])
    v_ts(P, logits, logits, taucol.t[0:16, 0:1], ALU.mult, sbufs=[taucol.buf])
    nfrow = A.sbuf("a3_nf", [1, 16], F32)
    P.dma("sync", nfrow.ap, sc_h.ap[0:1, 16:32], reads=[sc_h.buf], writes=[nfrow.buf])
    nfb = A.sbuf("a3_nfb", [128, 16], F32)
    ps_t1 = A.psum("a3_pst1")
    bcast_row(P, C, ps_t1, nfrow, nfrow.t[0:1, 0:16], 16, cs.ones_row_f)
    v_copy(P, nfb, ps_t1, asl=ps_t1[0:128, 0:16])
    lgtm = A.sbuf("a3_lgtm", [128, 16], F32)
    cols = A.sbuf("a3_cols", [128, 8], F32)
    for tt in range(16):
        t_tr(P, ps_t1[0:128, 0:16], logits.t[0:16, tt * 128:(tt + 1) * 128], cs.ident_f,
             reads=[logits.buf], writes=[ps_t1.buf], pn=16)
        v_copy(P, lgtm, ps_t1, asl=ps_t1[0:128, 0:16])
        for h in range(4):
            hsl = slice(h * 4, (h + 1) * 4)
            v_red(P, cols, lgtm, ALU.max, outsl=cols.t[:, h: h + 1], asl=lgtm.t[:, hsl], negate=True)
            s_act(P, lgtm, lgtm, AF.Exp, bias=cols.t[:, h: h + 1], outsl=lgtm.t[:, hsl],
                  asl=lgtm.t[:, hsl], sbufs=[cols.buf])
            v_red(P, cols, lgtm, ALU.add, outsl=cols.t[:, 4 + h: 5 + h], asl=lgtm.t[:, hsl])
            v_recip(P, cols, cols, outsl=cols.t[:, 4 + h: 5 + h], asl=cols.t[:, 4 + h: 5 + h])
            v_ts(P, lgtm, lgtm, cols.t[:, 4 + h: 5 + h], ALU.mult, outsl=lgtm.t[:, hsl],
                 asl=lgtm.t[:, hsl], sbufs=[cols.buf])
        v_tt(P, lgtm, lgtm, nfb, ALU.max)
        for h in range(4):
            hsl = slice(h * 4, (h + 1) * 4)
            v_red(P, cols, lgtm, ALU.add, outsl=cols.t[:, h: h + 1], asl=lgtm.t[:, hsl])
            v_ts(P, cols, cols, 1e-8, ALU.add, outsl=cols.t[:, h: h + 1], asl=cols.t[:, h: h + 1])
            v_recip(P, cols, cols, outsl=cols.t[:, h: h + 1], asl=cols.t[:, h: h + 1])
            v_ts(P, lgtm, lgtm, cols.t[:, h: h + 1], ALU.mult, outsl=lgtm.t[:, hsl],
                 asl=lgtm.t[:, hsl], sbufs=[cols.buf])
        t_tr(P, ps_t1[0:16, 0:128], lgtm.ap, cs.ident_f, reads=[lgtm.buf], writes=[ps_t1.buf])
        v_copy(P, wts, ps_t1, outsl=wts.t[0:16, tt * 128:(tt + 1) * 128], asl=ps_t1[0:16, 0:128])
    A.pop()

    A.push()
    sel_sb = A.sbuf("a3_sel", [16, 16 * 128], F32)
    P.dma("sync", sel_sb.ap, sel16.ap, reads=[sel16.buf], writes=[sel_sb.buf])
    onws = A.sbuf("a3_onw", [128, 2], F32)
    P.dma("sync", onws.ap, onw.ap, reads=[onw.buf], writes=[onws.buf])
    obuf = A.sbuf("a3_obuf", [128, T], F32)
    wb_t = A.sbuf("a3_wbt", [128, T], F32)
    ptmp = A.sbuf("a3_ptmp", [128, T], BF16)
    rtmp = A.sbuf("a3_rtmp", [128, 512], F32)
    osq = A.sbuf("a3_osq", [128, 2 * T], BF16)
    obf = A.sbuf("a3_obf", [128, 8 * T], BF16)
    rrow = A.sbuf("a3_rr", [1, T], F32)
    ps_w = A.psum("a3_psw")
    ps_q2 = A.psum("a3_psq2")
    ps_b2 = A.psum("a3_psb2")
    for h in range(4):
        for pt in range(2):
            blk = h * 2 + pt
            first = True
            for pi in range(4):
                for tc in range(4):
                    t_mm(P, ps_w[0:128, 0:512], sel_sb.t[0:16, (h * 4 + pi) * 128:(h * 4 + pi) * 128 + 128],
                         wts.t[0:16, tc * 512:(tc + 1) * 512], start=True, stop=True,
                         reads=[sel_sb.buf, wts.buf], writes=[ps_w.buf], inc=True)
                    v_copy(P, wb_t, ps_w, outsl=wb_t.t[:, tc * 512:(tc + 1) * 512], asl=ps_w[0:128, 0:512])
                if pi in (0, 1, 2):
                    hsrc = ps_hbm if pi == 0 else (pl_hbm if pi == 1 else pd_hbm)
                    P.dma("sync", ptmp.ap, hsrc.ap[blk * 128:(blk + 1) * 128, 0:T],
                          reads=[hsrc.buf], writes=[ptmp.buf])
                    psl = ptmp.t[:, 0:T]
                    pbuf = ptmp
                else:
                    psl = vpad.t[:, blk * TP_ + TPAD: blk * TP_ + TPAD + T]
                    pbuf = vpad
                if first:
                    v_tt(P, obuf, wb_t, pbuf, ALU.mult, asl=wb_t.t[:, 0:T], bsl=psl)
                    first = False
                else:
                    v_tt(P, wb_t, wb_t, pbuf, ALU.mult, asl=wb_t.t[:, 0:T], bsl=psl)
                    v_tt(P, obuf, obuf, wb_t, ALU.add)
            s_act(P, osq, obuf, AF.Square, outsl=osq.t[:, pt * T:(pt + 1) * T])
            v_copy(P, obf, obuf, outsl=obf.t[:, blk * T:(blk + 1) * T])
        for tc in range(4):
            for pt in range(2):
                t_mm(P, ps_q2[0:1, 0:512], cs.ones_col.t[0:128, 0:1],
                     osq.t[:, pt * T + tc * 512: pt * T + tc * 512 + 512],
                     start=(pt == 0), stop=(pt == 1),
                     reads=[osq.buf, cs.ones_col.buf], writes=[ps_q2.buf], inc=(pt == 1))
            v_ts(P, rrow, ps_q2, 1.0 / 256, ALU.mult, outsl=rrow.t[0:1, tc * 512:(tc + 1) * 512],
                 asl=ps_q2[0:1, 0:512])
            s_act(P, rrow, rrow, AF.Ln, bias=cfc(C, 1e-5)[0:1, :],
                  outsl=rrow.t[0:1, tc * 512:(tc + 1) * 512],
                  asl=rrow.t[0:1, tc * 512:(tc + 1) * 512], sbufs=[C.fc_sb.buf])
            s_act(P, rrow, rrow, AF.Exp, scale=-0.5,
                  outsl=rrow.t[0:1, tc * 512:(tc + 1) * 512],
                  asl=rrow.t[0:1, tc * 512:(tc + 1) * 512])
            bcast_row(P, C, ps_b2, rrow, rrow.t[0:1, tc * 512:(tc + 1) * 512], 512, cs.ones_row_f)
            v_copy(P, rtmp, ps_b2, outsl=rtmp.t[:, 0:512], asl=ps_b2[0:128, 0:512])
            for pt in range(2):
                blk = h * 2 + pt
                sl3 = slice(blk * T + tc * 512, blk * T + tc * 512 + 512)
                v_tt(P, obf, obf, rtmp, ALU.mult, outsl=obf.t[:, sl3], asl=obf.t[:, sl3],
                     bsl=rtmp.t[:, 0:512])
                v_ts(P, obf, obf, onws.t[:, pt: pt + 1], ALU.mult, outsl=obf.t[:, sl3],
                     asl=obf.t[:, sl3], sbufs=[onws.buf])
    A.pop()

    A.push()
    wob = A.sbuf("a3_wob", [128, 8 * D], BF16)
    for kt in range(8):
        P.dma("gpsimd", wob.t[:, kt * D:(kt + 1) * D], wo.ap[kt * 128:(kt + 1) * 128, 0:D],
              reads=[wo.buf], writes=[wob.buf])
    xt = [A.sbuf(f"a3_xt{i}", [128, 512], F32) for i in range(4)]
    xo = [A.sbuf(f"a3_xo{i}", [128, 512], F32) for i in range(4)]
    ps_wo = [A.psum(f"a3_pswo{i}") for i in range(4)]
    di = [0]
    for ng in range(4):
        for tcq in range(2):
            grp = [(ng * 2 + nt, tcq * 2 + tci, ps_wo[nt * 2 + tci]) for nt in range(2) for tci in range(2)]
            for kt in range(8):
                for gi_, (nt, tci, ps) in enumerate(grp):
                    t_mm(P, ps[0:128, 0:512], wob.t[:, kt * D + nt * 128: kt * D + nt * 128 + 128],
                         obf.t[:, kt * T + tci * 512: kt * T + tci * 512 + 512],
                         start=(kt == 0), stop=(kt == 7),
                         reads=[wob.buf, obf.buf], writes=[ps.buf],
                         inc=(kt == 7 and gi_ == len(grp) - 1))
            for ki_, (nt, tci, ps) in enumerate(grp):
                k = di[0] % 4
                di[0] += 1
                t0 = tci * 512
                P.dma("sync", xt[k].t[:, 0:512], x_fm.ap[nt * 128:(nt + 1) * 128, t0:t0 + 512],
                      reads=[x_fm.buf], writes=[xt[k].buf])
                v_tt(P, xo[k], ps, xt[k], ALU.add, outsl=xo[k].t[:, 0:512],
                     asl=ps[0:128, 0:512], bsl=xt[k].t[:, 0:512])
                P.dma("scalar", x_fm.ap[nt * 128:(nt + 1) * 128, t0:t0 + 512], xo[k].t[:, 0:512],
                      reads=[xo[k].buf], writes=[x_fm.buf])
    A.pop()
    A.pop()




def phase_final(C, cs):
    P, A = C.P, C.A
    A.push()
    x_fm = C.hint("x_fm", (D, T), F32)
    y = C.hout("y", (T, D), F32)
    x_sb = load_fm(C, "fin_x", x_fm, D, T, F32)
    out_t = [A.sbuf(f"fin_out{i}", [128, D], F32) for i in range(2)]
    ps = [A.psum(f"fin_ps{i}") for i in range(4)]
    for tt in range(16):
        ob = out_t[tt % 2]
        for dt in range(8):
            p = ps[dt % 4]
            t_tr(P, p[0:128, 0:128], x_sb.t[:, dt * T + tt * 128: dt * T + tt * 128 + 128],
                 cs.ident_f, reads=[x_sb.buf], writes=[p.buf])
            v_copy(P, ob, p, outsl=ob.t[:, dt * 128:(dt + 1) * 128], asl=p[0:128, 0:128])
        P.dma("scalar", y.ap[tt * 128:(tt + 1) * 128, 0:D], ob.ap, reads=[ob.buf], writes=[y.buf])
    A.pop()



PHASE_FNS = {}

def build_launch(parts, hostw):
    nc = bass.Bass("TRN2", target_bir_lowering=False, debug=False)
    C = Ctx(nc)
    sems = {}
    import contextlib
    stack = contextlib.ExitStack()
    semnames = ["semt", "semv", "sems", "semg"] + [f"{q}{i}" for q in DSEM.values() for i in range(NQSEM)]
    for s in semnames:
        sems[s] = stack.enter_context(nc.semaphore(s))
    C.A.push()
    cs = load_consts(C)
    for part in parts:
        emit_part(C, cs, part)
    C.P.barrier()
    C.A.pop()
    C.P.build_block(sems)
    needed = list(C.inputs.keys())
    stack.close()
    return nc, needed


def emit_part(C, cs, part):
    if part == "prep":
        phase_prep(C, cs)
    elif part == "final":
        phase_final(C, cs)
    elif part == "xfm_in":
        xin = C.hin("xfm_ext", (D, T), F32)
        x_fm = C.hint("x_fm", (D, T), F32)
        C.P.dma("sync", x_fm.ap, xin.ap, reads=[xin.buf], writes=[x_fm.buf])
    elif part == "xfm_out":
        xo = C.hout("xfm_ext_out", (D, T), F32)
        x_fm = C.hint("x_fm", (D, T), F32)
        C.P.dma("sync", xo.ap, x_fm.ap, reads=[x_fm.buf], writes=[xo.buf])
    elif part == "ln":
        phase_ln(C, cs)
    elif part.startswith("ffn:"):
        phase_ffn(C, cs, int(part.split(":")[1]))
    elif part == "attn2":
        phase_attn2(C, cs)
    elif part == "attn0":
        phase_attn0(C, cs)
    elif part == "ffn3":
        phase_ffn3(C, cs)
    elif part == "moe":
        phase_moe(C, cs)
    elif part == "ssm":
        phase_ssm(C, cs)
    elif part == "attn3":
        phase_attn3(C, cs)
    elif part in PHASE_FNS:
        PHASE_FNS[part](C, cs)
    else:
        raise ValueError(f"unknown part {part}")



def prep_host_weights(inp):
    w = {}
    cc = np.ascontiguousarray
    for L in (0, 2):
        pre = f"L{L}_ffn"
        if f"{pre}_w1_w" not in inp:
            continue
        w[f"{pre}_w1T"] = cc(inp[f"{pre}_w1_w"].T)
        w[f"{pre}_w2T"] = cc(inp[f"{pre}_w2_w"].T)
        w[f"{pre}_h1T"] = cc(inp[f"{pre}_h1_w"].T)
        w[f"{pre}_h2T"] = cc(inp[f"{pre}_h2_w"].T)
        w[f"{pre}_w3T"] = cc(inp[f"{pre}_w3_w"].T)
        sc = np.zeros((128, 32 * 4 + 8), dtype=np.float32)
        sc[:, 0:32] = inp[f"{pre}_w1_s"].reshape(32, 128).T
        sc[:, 32:64] = inp[f"{pre}_w2_s"].reshape(32, 128).T
        sc[:, 64:96] = inp[f"{pre}_h1_s"].reshape(32, 128).T
        sc[:, 96:128] = inp[f"{pre}_h2_s"].reshape(32, 128).T
        sc[:, 128:136] = inp[f"{pre}_w3_s"].reshape(8, 128).T
        w[f"{pre}_scales"] = sc
    if "L2_attn_bo" in inp:
        w["L2_attn_bo_fm"] = cc(inp["L2_attn_bo"].reshape(8, 128).T)
    if "L0_attn_rms_scale" in inp:
        w["L0_attn_rms_fm"] = cc(inp["L0_attn_rms_scale"].reshape(2, 128).T)
    if "L3_ffn_W_r" in inp:
        w["L3_ffn_W_rT"] = cc(inp["L3_ffn_W_r"].T)
        w["L3_ffn_W_kT"] = cc(inp["L3_ffn_W_k"].T)
        w["L3_ffn_W_vT"] = cc(inp["L3_ffn_W_v"].T)
        muf = np.zeros((128, 16), dtype=np.float32)
        muf[:, 0:8] = inp["L3_ffn_mu_r"].reshape(8, 128).T
        muf[:, 8:16] = inp["L3_ffn_mu_k"].reshape(8, 128).T
        w["L3_ffn_mu_fm"] = muf
    if "L1_attn_w_in" in inp:
        wt = inp["L1_attn_w_in"].T
        pad = np.zeros((D, 4736), dtype=np.float32)
        pad[:, 0:4096] = wt[:, 0:4096]
        pad[:, 4096:4128] = wt[:, 4096:4128]
        pad[:, 4128:4160] = wt[:, 4224:4256]
        pad[:, 4224:4320] = wt[:, 4128:4224]
        pad[:, 4352:4448] = wt[:, 4256:4352]
        pad[:, 4480:4576] = wt[:, 4352:4448]
        pad[:, 4608:4640] = wt[:, 4448:4480]
        w["L1_ssm_winT"] = pad
        w["L1_ssm_woutT"] = cc(inp["L1_attn_w_out"].T)
        w["L1_ssm_dtb"] = cc(inp["L1_attn_dt_bias"].reshape(32, 1))
        w["L1_ssm_D"] = cc(inp["L1_attn_D"].reshape(32, 1))
    if "L3_attn_Wq" in inp:
        w["L3_wqT"] = cc(inp["L3_attn_Wq"].T)
        w["L3_wkT"] = cc(inp["L3_attn_Wk"].T)
        w["L3_wvT"] = cc(inp["L3_attn_Wv"].T)
        w["L3_wbT"] = cc(inp["L3_attn_Wb"].T)
        w["L3_woT"] = cc(inp["L3_attn_Wo"].T)
        cw = np.zeros((128, 8 * 12), dtype=np.float32)
        for si, nm in enumerate(["conv_q_w", "conv_k_w", "conv_v_w"]):
            a = inp[f"L3_attn_{nm}"].reshape(8, 128, 4)
            for pt in range(8):
                for tap in range(4):
                    cw[:, pt * 12 + si * 4 + tap] = a[pt, :, tap]
        w["L3_conv_fm"] = cw
        fs = inp["L3_attn_fir_short_w"].reshape(8, 128, 5)
        fl = inp["L3_attn_fir_long_w"].reshape(8, 128, 63)
        fsw = np.zeros((128, 8 * 5), dtype=np.float32)
        flw = np.zeros((128, 8 * 63), dtype=np.float32)
        for pt in range(8):
            fsw[:, pt * 5:(pt + 1) * 5] = fs[pt]
            flw[:, pt * 63:(pt + 1) * 63] = fl[pt]
        w["L3_firs_fm"] = fsw
        w["L3_firl_fm"] = flw
        w["L3_gw1T"] = cc(inp["L3_attn_gate_W1"].T)
        w["L3_gb1_fm"] = cc(inp["L3_attn_gate_b1"].reshape(8, 128).T)
        w["L3_gw2T"] = cc(inp["L3_attn_gate_W2"].T)
        w["L3_gb2_col"] = cc(inp["L3_attn_gate_b2"].reshape(16, 1))
        w["L3_onw_fm"] = cc(inp["L3_attn_o_norm_w"].reshape(2, 128).T)
    if "L1_ffn_gate_w" in inp:
        w["L1_moe_gateT"] = cc(inp["L1_ffn_gate_w"].T)
        for e in range(8):
            w13t = inp["L1_ffn_w13_weight"][e].T
            w[f"L1_moe_w13T_{e}"] = cc(w13t.reshape(D, 2, 16, 128).transpose(0, 2, 1, 3).reshape(D, 4096))
            w[f"L1_moe_w2T_{e}"] = cc(inp["L1_ffn_w2_weight"][e].T)
    w.update(host_consts())
    return w


TEST_PLAN = None

def launch_plan():
    if TEST_PLAN is not None:
        plans = [list(p) for p in TEST_PLAN]
    else:
        l0 = ["prep", "ln", "attn0", "ffn:0"]
        l1 = ["ln", "ssm", "moe"]
        l2 = ["ln", "attn2", "ffn:2"]
        l3 = ["ln", "attn3", "ffn3", "final"]
        if SINGLE_LAUNCH:
            plans = [l0 + l1 + l2 + l3]
        else:
            plans = [l0, l1, l2, l3]
    for i, p in enumerate(plans):
        if i > 0:
            p.insert(0, "xfm_in")
        if i < len(plans) - 1:
            p.append("xfm_out")
    return plans


_launch_cache = {}

def kernel(**inputs):
    hostw = prep_host_weights(inputs)
    x = np.ascontiguousarray(inputs["L0_attn_X"]).astype(np.float32, copy=False)
    plans = launch_plan()
    per_core_x = [np.ascontiguousarray(x[c]) for c in range(NCORE)]
    carry = None
    out = None
    for li, parts in enumerate(plans):
        key = (li, tuple(parts))
        if key not in _launch_cache:
            _launch_cache[key] = build_launch(tuple(parts), hostw)
        nc, needed = _launch_cache[key]
        hostw["c_fconst"] = make_fconst()
        in_maps = []
        for c in range(NCORE):
            m = {}
            for name in needed:
                if name == "x_in":
                    m[name] = per_core_x[c]
                elif name == "xfm_ext":
                    m[name] = carry[c]
                elif name in hostw:
                    m[name] = hostw[name]
                else:
                    m[name] = np.ascontiguousarray(inputs[name])
            in_maps.append(m)
        res = bass_utils.run_bass_kernel_spmd(nc, in_maps, core_ids=list(range(NCORE)))
        if "xfm_ext_out" in res.results[0]:
            carry = [res.results[c]["xfm_ext_out"] for c in range(NCORE)]
        if "y" in res.results[0]:
            out = np.stack([res.results[c]["y"] for c in range(NCORE)], axis=0)
    return out
